# version 13
# baseline (speedup 1.0000x reference)
import sys
sys.path.insert(0, "/opt/trn_rl_repo")
from contextlib import ExitStack
import numpy as np
import ml_dtypes
import concourse.bass as bass
import concourse.mybir as mybir
from concourse.bass_utils import run_bass_kernel_spmd

F32 = mybir.dt.float32
BF16 = mybir.dt.bfloat16
AF = mybir.ActivationFunctionType
ALU = mybir.AluOpType
AXX = mybir.AxisListType.X

D = 1024
NH = 16
DH = 64
NPAIR = 8
HID = 4096
NE = 8
NKC = 8
LN_EPS = 1e-5

ENGS = ("tensor", "scalar", "vector", "sync", "gpsimd")

_UID = [0]


def _u(n):
    _UID[0] += 1
    return f"{n}_{_UID[0]}"


def host_consts():
    ident = np.eye(128).astype(ml_dtypes.bfloat16)
    tri = np.triu(np.ones((128, 128))).astype(ml_dtypes.bfloat16)
    esel = np.zeros((16, 8 * 128), dtype=ml_dtypes.bfloat16)
    for ci in range(8):
        for vl in range(2):
            for c in range(64):
                esel[2 * ci + vl, ci * 128 + vl * 64 + c] = 1.0
    dpat = np.zeros((128, 128), dtype=ml_dtypes.bfloat16)
    for vl in range(2):
        for il in range(2):
            for c in range(64):
                dpat[vl * 64 + c, il * 64 + c] = 1.0
    identf = np.eye(128).astype(np.float32)
    return {"cst_ident": ident, "cst_tri": tri, "cst_esel": esel,
            "cst_dpat": dpat, "cst_identf": identf}


class Prog:

    def __init__(self, nc, st: ExitStack):
        self.nc = nc
        self.st = st
        self.q = {e: [] for e in ENGS}
        self.sem = {}
        self.nblk = 0

    def get_sem(self, name):
        if name not in self.sem:
            h = self.st.enter_context(self.nc.semaphore(name))
            self.sem[name] = [h, 0]
        return self.sem[name]

    def n(self, name):
        return self.get_sem(name)[1]

    def emit(self, eng, fn, waits=(), inc=None, inc_by=1):
        winfo = [(self.get_sem(s)[0], v) for (s, v) in waits if v > 0]
        ih = None
        if inc is not None:
            rec = self.get_sem(inc)
            rec[1] += inc_by
            ih = rec[0]
        self.q[eng].append((fn, winfo, ih, inc_by))

    def dma(self, eng, dst, src, sem, waits=(), **kw):
        self.emit(eng, lambda e, d=dst, s=src: e.dma_start(d, s, **kw),
                  waits=waits, inc=sem, inc_by=16)

    def dma_tr(self, dst, src, sem, waits=()):
        self.emit("sync", lambda e, d=dst, s=src: e.dma_start_transpose(d, s),
                  waits=waits, inc=sem, inc_by=16)

    def wait(self, eng, sem_name, value=None):
        v = self.n(sem_name) if value is None else value
        if v > 0:
            self.emit(eng, None, waits=((sem_name, v),))

    def flush(self, name=None):
        self.nblk += 1
        name = name or f"b{self.nblk}"
        q = self.q
        self.q = {e: [] for e in ENGS}
        with self.nc.Block(name) as blk:
            for eng_name in ENGS:
                ops = q[eng_name]
                if not ops:
                    continue

                def runner(engine, ops=ops):
                    for fn, winfo, ih, iv in ops:
                        for (h, v) in winfo:
                            engine.wait_ge(h, v)
                        if fn is not None:
                            ins = fn(engine)
                            if ih is not None:
                                ins.then_inc(ih, iv)
                getattr(blk, eng_name)(runner)


def sb_ap(h, offset, dims):
    return bass.AP(h, offset, [[int(h.shape[1]), int(h.shape[0])]] + dims)



def emit_ln_chunk(p, x_ap, nfree, b, waits=(), dest=("rstd", "nmrs")):
    rstd_k, nmrs_k = dest
    k = (nfree + 511) // 512
    for i in range(k):
        w = tuple(waits) + ((("ln", p.n("ln")),) if i == 0 else ())
        lo, hi = i * 512, min((i + 1) * 512, nfree)
        p.emit("vector",
               lambda v, i=i, lo=lo, hi=hi: v.bn_stats(
                   b["s6"][:, i * 6:(i + 1) * 6], x_ap[:, lo:hi]),
               waits=w, inc="ln")
    p.emit("vector", lambda v, k=k: v.bn_aggr(b["s2"][:, :], b["s6"][:, 0:6 * k]),
           waits=(("ln", p.n("ln")),), inc="ln")
    p.emit("scalar", lambda s: s.activation(b[rstd_k][:, :], b["s2"][:, 1:2],
                                            AF.Ln, bias=b["eps_col"][:, :],
                                            scale=1.0),
           waits=(("ln", p.n("ln")),), inc="ln")
    p.emit("scalar", lambda s: s.activation(b[rstd_k][:, :], b[rstd_k][:, :],
                                            AF.Exp, scale=-0.5),
           waits=(("ln", p.n("ln")),), inc="ln")
    p.emit("vector", lambda v: v.scalar_tensor_tensor(
        b[nmrs_k][:, :], b["s2"][:, 0:1], -1.0, b[rstd_k][:, :],
        ALU.mult, ALU.mult),
        waits=(("ln", p.n("ln")),), inc="ln")


def emit_row_bcast(p, out, src_dram, n, b):
    row, ones, ps = b["brow"], b["ones1"], b["ps_bc"]
    p.dma("sync", row[:, 0:n], src_dram[None, :], "dbr",
          waits=(("bmm", p.n("bmm")),))
    dlv = p.n("dbr")
    p.emit("vector", lambda v: v.memset(ones[:, :], 1.0),
           waits=(("bc", p.n("bc")),), inc="bc")
    ev0 = p.n("bc")
    for j in range(0, n, 512):
        w = min(512, n - j)
        guard = p.n("bc")
        p.emit("tensor",
               lambda t, j=j, w=w: t.matmul(ps[:, 0:w], ones[:, :],
                                            row[:, j:j + w],
                                            start=True, stop=True),
               waits=(("dbr", dlv), ("bc", guard)), inc="bmm")
        p.emit("scalar",
               lambda s, j=j, w=w: s.copy(out[:, j:j + w], ps[:, 0:w]),
               waits=(("bmm", p.n("bmm")),), inc="bc")


def emit_tiles_to_fmajor(p, dst, T, cd_range, ct_range, src_col, b, waits=(),
                         gb=None):
    tiles = [(ct, cd) for ct in ct_range for cd in cd_range]
    first = True
    tev0 = p.n("tev")
    ngrp = (len(tiles) + 3) // 4
    for g in range(ngrp):
        grp = tiles[4 * g:4 * g + 4]
        half = (g % 2) * 1024
        w0 = [("tev", tev0 + 4 * (g - 1))] if g >= 2 else []
        for j, (ct, cd) in enumerate(grp):
            w = list(w0) if j == 0 else []
            if first:
                w += list(waits)
                first = False
            p.emit("tensor",
                   lambda t, ct=ct, cd=cd, j=j, half=half: t.transpose(
                       b["psT"][:, half + j * 128: half + (j + 1) * 128],
                       src_col(ct, cd), b["ident"][:, :]),
                   waits=w, inc="tmm")
            w0 = []
        mmv = p.n("tmm")
        for j, (ct, cd) in enumerate(grp):
            if gb is None:
                p.emit("vector",
                       lambda v, ct=ct, cd=cd, j=j, half=half: v.tensor_copy(
                           dst[:, cd * T + ct * 128: cd * T + (ct + 1) * 128],
                           b["psT"][:, half + j * 128: half + (j + 1) * 128]),
                       waits=((("tmm", mmv),) if j == 0 else ()), inc="tev")
            else:
                gc, bc = gb
                p.emit("vector",
                       lambda v, ct=ct, cd=cd, j=j, half=half: v.tensor_scalar(
                           dst[:, cd * T + ct * 128: cd * T + (ct + 1) * 128],
                           b["psT"][:, half + j * 128: half + (j + 1) * 128],
                           gc[:, cd:cd + 1], bc[:, cd:cd + 1],
                           ALU.mult, ALU.add),
                       waits=((("tmm", mmv),) if j == 0 else ()), inc="tev")


def emit_load_xF(p, x_dram, xF, T, b):
    nt = T // 128
    xc0, tmm0 = p.n("xc"), p.n("tmm")
    tev0 = p.n("tev")
    for ct in range(nt):
        sf, sb = b["stage_f"][ct % 2], b["stage_b"][ct % 2]
        dxs = f"dx{ct % 2}"
        p.dma("sync", sf[:, :], x_dram[ct * 128:(ct + 1) * 128, :], dxs,
              waits=(("xc", max(xc0, p.n("xc") - 1)),))
        p.emit("vector", lambda v, sf=sf, sb=sb: v.tensor_copy(sb[:, :], sf[:, :]),
               waits=((dxs, p.n(dxs)), ("tmm", tmm0 + 8 * max(0, ct - 1))),
               inc="xc")
        emit_tiles_to_fmajor(
            p, xF, T, range(NKC), [ct],
            lambda ct2, cd, sb=sb: sb[:, cd * 128:(cd + 1) * 128],
            b, waits=(("xc", p.n("xc")),))



def emit_load_xF_fast(p, xbf_dram, xF, T, sem="dxt", group_waits=None):
    for j in range(T // 512):
        gw = list(group_waits(j)) if group_waits is not None else []
        for kc in range(NKC):
            p.dma_tr(xF[:, kc * T + j * 512: kc * T + (j + 1) * 512],
                     xbf_dram[j * 512:(j + 1) * 512,
                              kc * 128:(kc + 1) * 128], sem,
                     waits=gw if kc == 0 else ())


def emit_attention(p, cfg, b):
    emit_attention_v(p, cfg, b)
    emit_attention_pairs(p, cfg, b)


def emit_attention_v(p, cfg, b):
    T, causal = cfg["T"], cfg["causal"]
    nt, nq = T // 128, T // 512
    xF = cfg["xF"]
    wv = cfg["wv_sb"]
    bvr = cfg.get("bv_row")
    wready = tuple(cfg.get("qkv_ready", ()))
    xf_ready = tuple(cfg.get("xf_ready", ()))
    v65 = b["v65"]
    pm = b["ps_main"]

    p.emit("vector", lambda v: v.memset(
        sb_ap(v65, 64, [[1040, nt], [65, 16]]), 1.0), inc="vev")
    xft = cfg.get("xf_tev")
    vev0 = p.n("vev")
    for tt in range(nt):
        ps_v = pm[:, (tt % 2) * 1024:(tt % 2) * 1024 + 1024]
        guard = max(vev0, p.n("vev") - 1)
        for kc in range(NKC):
            w = [("vev", guard)] if kc == 0 else []
            if kc == 0 and xft is not None:
                w.append(("tev", xft + 8 * (tt + 1)))
            if kc == 0 and cfg.get("xf_dxt0") is not None:
                w.append(("dxt", cfg["xf_dxt0"] + 128 * (tt // 4 + 1)))
            if tt == 0 and kc == 0:
                w += list(wready) + list(xf_ready)
            p.emit("tensor",
                   lambda t, tt=tt, kc=kc, ps_v=ps_v: t.matmul(
                       ps_v[:, 0:512],
                       xF[:, kc * T + tt * 128: kc * T + (tt + 1) * 128],
                       wv[:, kc * 1024: kc * 1024 + 512],
                       start=(kc == 0), stop=(kc == NKC - 1)),
                   waits=w)
            p.emit("tensor",
                   lambda t, tt=tt, kc=kc, ps_v=ps_v: t.matmul(
                       ps_v[:, 512:1024],
                       xF[:, kc * T + tt * 128: kc * T + (tt + 1) * 128],
                       wv[:, kc * 1024 + 512: (kc + 1) * 1024],
                       start=(kc == 0), stop=(kc == NKC - 1)),
                   inc="amm" if kc == NKC - 1 else None)
        mmv = p.n("amm")
        dst = sb_ap(v65, tt * 1040, [[65, 16], [1, 64]])
        if bvr is not None:
            bvw = tuple(cfg.get("bv_ready", ())) if tt == 0 else ()
            p.emit("vector",
                   lambda v, dst=dst, ps_v=ps_v: v.scalar_tensor_tensor(
                       dst, ps_v[:, 0:1024], 1.0, bvr[:, 0:1024],
                       ALU.mult, ALU.add),
                   waits=(("amm", mmv),) + bvw, inc="vev")
        else:
            p.emit("vector",
                   lambda v, dst=dst, ps_v=ps_v: v.tensor_copy(
                       dst, ps_v[:, 0:1024]),
                   waits=(("amm", mmv),), inc="vev")


def emit_attention_pairs(p, cfg, b):
    T, causal = cfg["T"], cfg["causal"]
    nt, nq = T // 128, T // 512
    xF, z_tm = cfg["xF"], cfg["z_tm"]
    wk, wq = cfg["wk_sb"], cfg["wq_sb"]
    bqc, bkc = cfg.get("bq_cols"), cfg.get("bk_cols")
    tri = cfg["tri"]
    wready = tuple(cfg.get("qkv_ready", ()))
    xf_ready = tuple(cfg.get("xf_ready", ()))
    v65, kq = b["v65"], b["kq"]
    est = b["est"]
    pm = b["ps_main"]
    ps_qk = pm[:, 0:min(T, 2048)]
    ps_zfA, ps_zfB = b["ps_zfA"], b["ps_zfB"]
    psT = b["psT"]
    zfsb = b["zfsb"]
    ident = b["ident"]
    kbar, rcol = b["kbar"], b["rcol"]
    v_done = p.n("vev")

    stmm_prev = 0
    zev_prev = 0
    for hp in range(NPAIR):
        cur = kq[hp % len(kq)]
        for which, wmat, bcol, off in (("k", wk, bkc, 0), ("q", wq, bqc, T)):
            guard = p.n("qkev")
            xft = cfg.get("xf_tev")
            for kc in range(NKC):
                for J in range(nq):
                    w = []
                    if kc == 0 and J == 0:
                        w = [("qkev", guard), ("amm", stmm_prev),
                             ("eev", p.n("eev")), ("vev", v_done)]
                        if hp == 0 and which == "k":
                            w += list(wready) + list(xf_ready)
                    if kc == 0 and hp == 0 and xft is not None:
                        w.append(("tev", xft + 8 * min(nt, 4 * (J + 1))))
                    if kc == 0 and hp == 0 and cfg.get("xf_dxt0") is not None:
                        w.append(("dxt", cfg["xf_dxt0"] + 128 * (J + 1)))
                    p.emit("tensor",
                           lambda t, kc=kc, J=J, hp=hp, wmat=wmat: t.matmul(
                               ps_qk[:, J * 512:(J + 1) * 512],
                               wmat[:, kc * 1024 + hp * 128: kc * 1024 + hp * 128 + 128],
                               xF[:, kc * T + J * 512: kc * T + (J + 1) * 512],
                               start=(kc == 0), stop=(kc == NKC - 1)),
                           waits=w,
                           inc="amm" if (kc == NKC - 1 and J == nq - 1) else None)
            mmv = p.n("amm")
            for J in range(nq):
                if bcol is not None:
                    p.emit("scalar",
                           lambda s, J=J, off=off, hp=hp, bcol=bcol, cur=cur:
                           s.activation(
                               cur[:, off + J * 512: off + (J + 1) * 512],
                               ps_qk[:, J * 512:(J + 1) * 512], AF.Identity,
                               bias=bcol[:, hp:hp + 1], scale=1.0),
                           waits=(("amm", mmv),) if J == 0 else (), inc="qkev")
                else:
                    p.emit("scalar",
                           lambda s, J=J, off=off, cur=cur: s.copy(
                               cur[:, off + J * 512: off + (J + 1) * 512],
                               ps_qk[:, J * 512:(J + 1) * 512]),
                           waits=(("amm", mmv),) if J == 0 else (), inc="qkev")
        qkv = p.n("qkev")
        p.emit("vector", lambda v, cur=cur: v.reduce_sum(
            kbar[:, 0:1], cur[:, 0:T], axis=AXX),
            waits=(("qkev", qkv),), inc="ctr")
        p.emit("vector", lambda v: v.tensor_scalar_mul(
            kbar[:, 0:1], kbar[:, 0:1], -1.0 / T),
            waits=(("ctr", p.n("ctr")),), inc="ctr")
        p.emit("vector", lambda v, cur=cur: v.tensor_scalar(
            cur[:, 0:T], cur[:, 0:T], kbar[:, 0:1], None, ALU.add),
            waits=(("ctr", p.n("ctr")),), inc="ctr")
        kready = p.n("ctr")

        for J in range(nq):
            plist = list(range(min(nt, (J + 1) * 4))) if causal                 else list(range(nt))
            npl = len(plist)
            exp_after = {}
            ready_after = {}
            zmm_after = {}

            def emit_st(idx):
                P = plist[idx]
                slot = idx % 2
                w = [("ctr", kready)]
                if idx >= 2:
                    w.append(("eev", exp_after[idx - 2]))
                for (base, tp, hb) in ((0, (0, 0), 0), (64, (64, 0), 1)):
                    p.emit("tensor",
                           lambda t, P=P, J=J, slot=slot, base=base, tp=tp,
                           hb=hb, cur=cur: t.matmul(
                               pm[:, slot * 1024 + hb * 512:
                                  slot * 1024 + (hb + 1) * 512],
                               cur[base:base + 64, P * 128:(P + 1) * 128],
                               cur[base:base + 64, T + J * 512: T + (J + 1) * 512],
                               start=True, stop=True, tile_position=tp),
                           waits=w, inc="amm")
                    w = []

            def emit_exp(idx):
                P = plist[idx]
                slot, ring = idx % 2, idx % 4
                we = [("amm", p.n("amm"))]
                if idx >= 4:
                    we.append(("zmm", zmm_after[idx - 4]))
                p.emit("scalar",
                       lambda s, ring=ring, slot=slot: s.activation(
                           est[:, ring * 1024:(ring + 1) * 1024],
                           pm[:, slot * 1024:(slot + 1) * 1024],
                           AF.Exp, scale=0.125),
                       waits=we, inc="eev")
                exp_after[idx] = p.n("eev")
                rd = [("eev", exp_after[idx])]
                if causal and 4 * J <= P:
                    off = P * 128 - J * 512
                    for hb in (0, 1):
                        base = ring * 1024 + hb * 512
                        if off > 0:
                            p.emit("vector",
                                   lambda v, base=base, off=off: v.memset(
                                       est[:, base: base + off], 0.0),
                                   waits=(("eev", exp_after[idx]),),
                                   inc="bev")
                        p.emit("vector",
                               lambda v, base=base, off=off: v.tensor_tensor(
                                   est[:, base + off: base + off + 128],
                                   est[:, base + off: base + off + 128],
                                   tri[:, :], ALU.mult),
                               waits=(("eev", exp_after[idx]),), inc="bev")
                    rd.append(("bev", p.n("bev")))
                ready_after[idx] = rd

            def emit_z(idx):
                P = plist[idx]
                ring = idx % 4
                wz = list(ready_after[idx])
                if idx == 0:
                    wz += [("zfev", zev_prev), ("vev", v_done)]
                last = idx == npl - 1
                for hb, ps_zf in ((0, ps_zfA), (1, ps_zfB)):
                    p.emit("tensor",
                           lambda t, ps_zf=ps_zf, ring=ring, P=P, hp=hp,
                           hb=hb, idx=idx, last=last: t.matmul(
                               ps_zf[0:65, 0:512],
                               sb_ap(v65, P * 1040 + (2 * hp + hb) * 65,
                                     [[1, 65]]),
                               est[:, ring * 1024 + hb * 512:
                                   ring * 1024 + (hb + 1) * 512],
                               start=(idx == 0), stop=last),
                           waits=wz, inc="zmm")
                    wz = []
                zmm_after[idx] = p.n("zmm")

            for idx in range(npl):
                emit_st(idx)
                emit_exp(idx)
                if idx > 1:
                    emit_z(idx - 2)
            if npl > 1:
                emit_z(npl - 2)
            emit_z(npl - 1)

            zmv = p.n("zmm")
            for hb, ps_zf in ((0, ps_zfA), (1, ps_zfB)):
                p.emit("vector",
                       lambda v, hb=hb, ps_zf=ps_zf: v.tensor_copy(
                           zfsb[0:65, hb * 512:(hb + 1) * 512],
                           ps_zf[0:65, 0:512]),
                       waits=(("zmm", zmv), ("ztr", p.n("ztr"))), inc="zfev")
            zfv = p.n("zfev")
            ztr0 = p.n("zev")
            for m in range(4):
                for hb in (0, 1):
                    slot = 2 * m + hb
                    p.emit("tensor",
                           lambda t, m=m, hb=hb, slot=slot: t.transpose(
                               psT[:, slot * 68: slot * 68 + 65],
                               zfsb[0:65, hb * 512 + m * 128: hb * 512 + (m + 1) * 128],
                               ident[0:65, 0:65]),
                           waits=(("zfev", zfv), ("zev", ztr0)), inc="ztr")
            ztv = p.n("ztr")
            for m in range(4):
                ct = J * 4 + m
                for hb in (0, 1):
                    head = 2 * hp + hb
                    slot = 2 * m + hb
                    p.emit("vector",
                           lambda v, slot=slot: v.reciprocal(
                               rcol[:, slot:slot + 1],
                               psT[:, slot * 68 + 64: slot * 68 + 65]),
                           waits=(("ztr", ztv),), inc="zev")
                    p.emit("vector",
                           lambda v, slot=slot, ct=ct, head=head:
                           v.tensor_scalar(
                               z_tm[:, ct * 1024 + head * 64: ct * 1024 + (head + 1) * 64],
                               psT[:, slot * 68: slot * 68 + 64],
                               rcol[:, slot:slot + 1], None, ALU.mult),
                           waits=(("zev", p.n("zev")),), inc="zev")
            zev_prev = p.n("zfev")
        stmm_prev = p.n("amm")


def alloc_attn_bufs(nc, st, T):
    nt = T // 128
    sb = lambda n, shp, dt: st.enter_context(nc.sbuf_tensor(_u(n), shp, dt))
    ps = lambda n, shp, dt=F32: st.enter_context(nc.psum_tensor(_u(n), shp, dt))
    return {
        "v65": sb("at_v65", [128, nt * 1040], BF16),
        "kq": [sb("at_kq0", [128, 2 * T], BF16)],
        "est": sb("at_est", [128, 4 * 1024], BF16),
        "kbar": sb("at_kbar", [128, 2], F32),
        "rcol": sb("at_rcol", [128, 8], F32),
        "ps_main": ps("at_psmain", [128, 2048]),
        "ps_zfA": ps("at_pszfA", [128, 512]),
        "ps_zfB": ps("at_pszfB", [128, 512]),
        "zfsb": sb("at_zfsb", [128, 1024], BF16),
    }


def emit_resid_ln_out(p, x_dram, z_tm, out_dram, T, b, zscale=1.0,
                      trailing=True):
    nt = T // 128
    uadd0 = p.n("uadd")
    xcv = p.n("xc")
    for ct in range(nt):
        sf = b["stage_f"][ct % 2]
        drs, dws = f"dr{ct % 2}", f"dw{ct % 2}"
        dwp = f"dw{(ct - 1) % 2}"
        w = [("uadd", max(uadd0, p.n("uadd") - 1))]
        if ct < 2:
            w.append(("xc", xcv))
        p.dma("sync", sf[:, :], x_dram[ct * 128:(ct + 1) * 128, :], drs,
              waits=w)
        p.emit("vector",
               lambda v, sf=sf, ct=ct: v.tensor_tensor(
                   b["ubuf"][:, :], sf[:, :],
                   z_tm[:, ct * 1024:(ct + 1) * 1024], ALU.add),
               waits=((drs, p.n(drs)), (dwp, p.n(dwp)),
                      ("zev", p.n("zev"))),
               inc="uadd")
        emit_ln_chunk(p, b["ubuf"], 1024, b, waits=(("uadd", p.n("uadd")),))
        p.emit("scalar",
               lambda s: s.activation(b["ubuf"][:, :], b["ubuf"][:, :],
                                      AF.Identity, bias=b["nmrs"][:, :],
                                      scale=b["rstd"][:, :]),
               waits=(("ln", p.n("ln")),), inc="uout")
        p.dma("sync", out_dram[ct * 128:(ct + 1) * 128, :], b["ubuf"][:, :],
              dws, waits=(("uout", p.n("uout")),))
    if trailing:
        p.wait("sync", "dw0")
        p.wait("sync", "dw1")


def dw_chunk_waits(p, ndw0, ndw1):
    def f(ct):
        n = (ndw0, ndw1)[ct % 2]
        return [(f"dw{ct % 2}", n - 16 * (7 - ct // 2))]
    return f


def alloc_common_bufs(nc, st):
    sb = lambda n, shp, dt: st.enter_context(nc.sbuf_tensor(n, shp, dt))
    ps = lambda n, shp, dt=F32: st.enter_context(nc.psum_tensor(n, shp, dt))
    return {
        "stage_f": [sb("c_stf0", [128, 1024], F32),
                    sb("c_stf1", [128, 1024], F32)],
        "stage_b": [sb("c_stb0", [128, 1024], BF16),
                    sb("c_stb1", [128, 1024], BF16)],
        "ubuf": sb("c_ubuf", [128, 1024], F32),

        "s6": sb("c_s6", [128, 48], F32),
        "s2": sb("c_s2", [128, 2], F32),
        "rstd": sb("c_rstd", [128, 1], F32),
        "nmrs": sb("c_nmrs", [128, 1], F32),
        "rstd2": sb("c_rstd2", [128, 1], F32),
        "nmrs2": sb("c_nmrs2", [128, 1], F32),
        "eps_col": sb("c_eps", [128, 1], F32),
        "brow": sb("c_brow", [1, 1024], F32),
        "ones1": sb("c_ones1", [1, 128], F32),
        "_ident_sb": sb("c_ident", [128, 128], BF16),
        "_identf_sb": sb("c_identf", [128, 128], F32),
        "_colrow": sb("c_colrow", [32, 128], F32),
        "_tri_sb": sb("c_tri", [128, 128], BF16),
        "psT": ps("c_psT", [128, 2048], BF16),
        "xfb": sb("c_xfb", [128, 8 * 2048], BF16),
    }


def emit_load_consts(p, cb, aps):
    cb["ident"] = cb["_ident_sb"]
    p.dma("sync", cb["_ident_sb"][:, :], aps["cst_ident"][:, :], "dl")
    p.dma("sync", cb["_tri_sb"][:, :], aps["cst_tri"][:, :], "dl")
    p.emit("vector", lambda v: v.memset(cb["eps_col"][:, :], LN_EPS), inc="ev")
    p.wait("vector", "dl")
    p.wait("tensor", "dl")
    p.wait("scalar", "ev")


def load_weight_slab(p, dst, w_dram, cast, sem=None, waits=()):
    n = int(w_dram.shape[1])
    eng = "gpsimd" if cast else "sync"
    if sem is None:
        sem = "dwg" if cast else "dwt"
    for kc in range(NKC):
        p.dma(eng, dst[:, kc * n:(kc + 1) * n],
              w_dram[kc * 128:(kc + 1) * 128, :], sem,
              waits=waits if kc == 0 else ())
    return sem



def alloc_p0_bufs(nc, st):
    sb = lambda n, shp, dt: st.enter_context(nc.sbuf_tensor(_u(n), shp, dt))
    ps = lambda n, shp, dt=F32: st.enter_context(nc.psum_tensor(_u(n), shp, dt))
    return {
        "w16": sb("p0_w16", [16, 48], F32),
        "m16": sb("p0_m16", [16, 4], F32),
        "pstage": sb("p0_pstage", [16, 16], BF16),
        "probsT": sb("p0_probsT", [16, 48], BF16),
        "pbc": sb("p0_pbc", [16, 128], BF16),
        "esel": sb("p0_esel", [16, 8 * 128], BF16),
        "pco": sb("p0_pco", [128, 8 * 128], BF16),
        "dpat": sb("p0_dpat", [128, 128], BF16),
        "slab": sb("p0_slab", [128, 3072], BF16),
        "ps16": ps("p0_ps16", [16, 16], BF16),
        "psA": ps("p0_psA", [128, 128]),
    }


def emit_phase0_l0(p, aps, b, ident):
    p.dma("sync", b["esel"][:, :], aps["cst_esel"][:, :], "dp0")
    p.dma("sync", b["dpat"][:, :], aps["cst_dpat"][:, :], "dp0")
    for wi, wname in enumerate(("L0_attn_W_K", "L0_attn_W_Q", "L0_attn_W_V")):
        p.dma("sync", b["w16"][:, wi * 16:(wi + 1) * 16], aps[wname][:, :],
              "dp0")
    wp = aps["L0_attn_W_pred"].tensor
    for co in range(8):
        for vl in range(2):
            src = bass.AP(wp, (2 * co) * 4096, [[64, 64], [4096, 2], [1, 64]])
            p.emit("gpsimd",
                   lambda e, co=co, vl=vl, src=src: e.dma_start(
                       b["pco"][vl * 64:(vl + 1) * 64, co * 128:(co + 1) * 128],
                       src),
                   inc="dp0g", inc_by=16)
    dl0 = p.n("dp0")
    dg0 = p.n("dp0g")

    for wi in range(3):
        sl = b["w16"][:, wi * 16:(wi + 1) * 16]
        p.emit("vector", lambda v, sl=sl: v.reduce_max(
            b["m16"][:, 0:1], sl, axis=AXX, negate=True),
            waits=(("dp0", dl0), ("p0", p.n("p0"))), inc="p0")
        p.emit("scalar", lambda s, sl=sl: s.activation(
            sl, sl, AF.Exp, bias=b["m16"][:, 0:1], scale=1.0,
            accum_out=b["m16"][:, 1:2]),
            waits=(("p0", p.n("p0")),), inc="p0")
        p.emit("vector", lambda v: v.reciprocal(b["m16"][:, 2:3],
                                                b["m16"][:, 1:2]),
               waits=(("p0", p.n("p0")),), inc="p0")
        p.emit("vector", lambda v, sl=sl: v.tensor_scalar(
            sl, sl, b["m16"][:, 2:3], None, ALU.mult),
            waits=(("p0", p.n("p0")),), inc="p0")
        p.emit("vector", lambda v, sl=sl: v.tensor_copy(b["pstage"][:, :], sl),
               waits=(("p0", p.n("p0")), ("p0mm", p.n("p0mm"))), inc="p0")
        p.emit("tensor", lambda t: t.transpose(
            b["ps16"][:, :], b["pstage"][:, :], ident[0:16, 0:16]),
            waits=(("p0", p.n("p0")),), inc="p0mm")
        p.emit("vector", lambda v, wi=wi: v.tensor_copy(
            b["probsT"][:, wi * 16:(wi + 1) * 16], b["ps16"][:, :]),
            waits=(("p0mm", p.n("p0mm")),), inc="p0")

    colbase = {0: 1024, 1: 0, 2: 2048}
    for ci in range(8):
        slab_guard = p.n("dp0s")
        for wi in range(3):
            for co in range(8):
                pbv = p.n("p0")
                src = bass.AP(b["probsT"], wi * 16 + 2 * co,
                              [[48, 16], [1, 2], [0, 64]])
                p.emit("vector", lambda v, src=src: v.tensor_copy(
                    b["pbc"][:, :], src),
                    waits=(("p0", pbv), ("p0mm", p.n("p0mm"))), inc="p0")
                p.emit("tensor", lambda t, ci=ci: t.matmul(
                    b["psA"][:, :], b["esel"][:, ci * 128:(ci + 1) * 128],
                    b["pbc"][:, :], start=True, stop=True),
                    waits=(("p0", p.n("p0")),), inc="p0mm")
                pat = b["pco"][:, co * 128:(co + 1) * 128] if wi == 1                     else b["dpat"][:, :]
                patw = (("dp0g", dg0),) if wi == 1 else (("dp0", dl0),)
                p.emit("vector",
                       lambda v, ci=ci, co=co, wi=wi, pat=pat: v.tensor_tensor(
                           b["slab"][:, colbase[wi] + co * 128:
                                     colbase[wi] + (co + 1) * 128],
                           b["psA"][:, :], pat, ALU.mult),
                       waits=(("p0mm", p.n("p0mm")), ("dp0s", slab_guard))
                       + patw, inc="p0")
        p.dma("sync", aps["l0qkv"][ci * 128:(ci + 1) * 128, :], b["slab"][:, :],
              "dp0s", waits=(("p0", p.n("p0")),))
    p.wait("sync", "dp0s")


def emit_load_ln_hF(p, x_dram, hF, T, b, g_cols, b_cols, x_waits=None,
                    first_guard=()):
    nt = T // 128
    xc0, tmm0 = p.n("xc"), p.n("tmm")
    tev0 = p.n("tev")
    for ct in range(nt):
        sf, sb = b["stage_f"][ct % 2], b["stage_b"][ct % 2]
        dxs = f"dx{ct % 2}"
        w = [("xc", max(xc0, p.n("xc") - 1))]
        if x_waits is not None:
            w += list(x_waits(ct))
        p.dma("sync", sf[:, :], x_dram[ct * 128:(ct + 1) * 128, :], dxs,
              waits=w)
        emit_ln_chunk(p, sf, 1024, b, waits=((dxs, p.n(dxs)),))
        p.emit("scalar",
               lambda s, sf=sf, sb=sb: s.activation(
                   sb[:, :], sf[:, :], AF.Identity, bias=b["nmrs"][:, :],
                   scale=b["rstd"][:, :]),
               waits=(("ln", p.n("ln")), ("tmm", tmm0 + 8 * max(0, ct - 1))),
               inc="xc")
        fg = tuple(first_guard) if ct == 0 else ()
        emit_tiles_to_fmajor(
            p, hF, T, range(NKC), [ct],
            lambda ct2, cd, sb=sb: sb[:, cd * 128:(cd + 1) * 128],
            b, waits=(("xc", p.n("xc")),) + fg, gb=(g_cols, b_cols))
    return tev0


def emit_zf_transpose(p, z_tm, zF, T, b, guard=()):
    nt = T // 128
    emit_tiles_to_fmajor(
        p, zF, T, range(NKC), range(nt),
        lambda ct, cd: z_tm[:, ct * 1024 + cd * 128: ct * 1024 + (cd + 1) * 128],
        b, waits=tuple(guard))


def emit_proj_resid(p, zF, w_sb, bp_row, x_dram, out_dram, T, ps, b,
                    final_ln, w_ready=(), xfb=None, x_waits=None,
                    trailing=True):
    nt = T // 128
    uadd0 = p.n("uadd")
    tevv = p.n("tev")
    xcv = p.n("xc")
    tmm0 = p.n("tmm")
    tev0 = p.n("tev")
    zev_tail = p.n("zev")
    ubf_of = {}

    def hand(hct):
        sbb = b["stage_b"][hct % 2]
        tw = (("ubf", ubf_of[hct]),)
        if hct == 0:
            tw = tw + (("zev", zev_tail),)
        else:
            tw = tw + (("tev", tev0 + 8 * hct),)
        emit_tiles_to_fmajor(
            p, xfb, T, range(NKC), [hct],
            lambda ct2, cd, sbb=sbb: sbb[:, cd * 128:(cd + 1) * 128],
            b, waits=tw)
    for ct in range(nt):
        sf = b["stage_f"][ct % 2]
        drs, dws = f"dr{ct % 2}", f"dw{ct % 2}"
        dwp = f"dw{(ct - 1) % 2}"
        w = [("uadd", max(uadd0, p.n("uadd") - 1))]
        if ct < 2:
            w.append(("xc", xcv))
        if x_waits is not None:
            w += list(x_waits(ct))
        p.dma("sync", sf[:, :], x_dram[ct * 128:(ct + 1) * 128, :], drs,
              waits=w)
        guard = p.n("uadd")
        for kc in range(NKC):
            ww = []
            if kc == 0:
                ww = [("uadd", max(uadd0, guard)), ("tev", tevv)] + list(w_ready)
            p.emit("tensor",
                   lambda t, ct=ct, kc=kc: t.matmul(
                       ps[:, 0:512],
                       zF[:, kc * T + ct * 128: kc * T + (ct + 1) * 128],
                       w_sb[:, kc * 1024: kc * 1024 + 512],
                       start=(kc == 0), stop=(kc == NKC - 1)),
                   waits=ww)
            p.emit("tensor",
                   lambda t, ct=ct, kc=kc: t.matmul(
                       ps[:, 512:1024],
                       zF[:, kc * T + ct * 128: kc * T + (ct + 1) * 128],
                       w_sb[:, kc * 1024 + 512: (kc + 1) * 1024],
                       start=(kc == 0), stop=(kc == NKC - 1)),
                   inc="pmm" if kc == NKC - 1 else None)
        mmv = p.n("pmm")
        w2 = (("bc", p.n("bc")),) if ct == 0 else ()
        if xfb is not None:
            w2 = w2 + (("ubf", p.n("ubf")),)
        p.emit("vector",
               lambda v: v.scalar_tensor_tensor(
                   b["ubuf"][:, :], ps[:, 0:1024], 1.0, bp_row[:, 0:1024],
                   ALU.mult, ALU.add),
               waits=(("pmm", mmv), (dwp, p.n(dwp))) + w2, inc="uadd")
        p.emit("vector",
               lambda v, sf=sf: v.tensor_tensor(
                   b["ubuf"][:, :], b["ubuf"][:, :], sf[:, :], ALU.add),
               waits=((drs, p.n(drs)), ("uadd", p.n("uadd"))), inc="uadd")
        if final_ln:
            emit_ln_chunk(p, b["ubuf"], 1024, b, waits=(("uadd", p.n("uadd")),))
            p.emit("scalar",
                   lambda s: s.activation(b["ubuf"][:, :], b["ubuf"][:, :],
                                          AF.Identity, bias=b["nmrs"][:, :],
                                          scale=b["rstd"][:, :]),
                   waits=(("ln", p.n("ln")),), inc="uout")
            p.dma("sync", out_dram[ct * 128:(ct + 1) * 128, :],
                  b["ubuf"][:, :], dws, waits=(("uout", p.n("uout")),))
            if xfb is not None:
                sbb = b["stage_b"][ct % 2]
                p.emit("vector",
                       lambda v, sbb=sbb: v.tensor_copy(sbb[:, :],
                                                        b["ubuf"][:, :]),
                       waits=(("uout", p.n("uout")),
                              ("tmm", tmm0 + 8 * max(0, ct - 1))),
                       inc="ubf")
                ubf_of[ct] = p.n("ubf")
                if ct >= 1:
                    hand(ct - 1)
                if ct == nt - 1:
                    hand(ct)
        else:
            p.dma("sync", out_dram[ct * 128:(ct + 1) * 128, :],
                  b["ubuf"][:, :], dws, waits=(("uadd", p.n("uadd")),))
    if trailing:
        p.wait("sync", "dw0")
        p.wait("sync", "dw1")
    return tev0


def emit_col_load(p, dst_cols, src_dram, cb=None, sem="dbr"):
    n = int(src_dram.shape[0]) // 128
    if cb is None:
        src2 = src_dram.rearrange("(c p) -> p c", p=128)
        p.emit("sync", lambda e: e.dma_start(dst_cols[:, 0:n], src2,
                                             allow_slow_non_contiguous=True),
               inc=sem, inc_by=16)
        return
    cr, idf, pscl = cb["_colrow"], cb["_identf_sb"], cb["ps_cl"]
    src2 = src_dram.rearrange("(c p) -> c p", p=128)
    p.dma("sync", cr[0:n, :], src2, "dcl",
          waits=(("clm", p.n("clm")),))
    p.emit("tensor", lambda t, n=n: t.transpose(
        pscl[:, 0:n], cr[0:n, :], idf[0:n, 0:n]),
        waits=(("dcl", p.n("dcl")), ("clv", p.n("clv"))), inc="clm")
    p.emit("vector", lambda v, n=n: v.tensor_copy(dst_cols[:, 0:n],
                                                  pscl[:, 0:n]),
           waits=(("clm", p.n("clm")),), inc="clv")



def alloc_moe_bufs(nc, st):
    sb = lambda n, shp, dt: st.enter_context(nc.sbuf_tensor(_u(n), shp, dt))
    ps = lambda n, shp, dt=F32: st.enter_context(nc.psum_tensor(_u(n), shp, dt))
    return {
        "w1": sb("mo_w1", [128, 8 * 4096], BF16),
        "w2": sb("mo_w2", [128, 32 * 1024], BF16),
        "mid": sb("mo_mid", [128, 32 * 256], BF16),
        "gw": sb("mo_gw", [128, 8 * 8], BF16),
        "glog": sb("mo_glog", [128, 128], F32),
        "wcols": sb("mo_wcols", [128, 128], F32),
        "gsc": sb("mo_gsc", [128, 12], F32),
        "wb8": sb("mo_wb8", [128, 8], BF16),
        "wT": sb("mo_wT", [8, 128], BF16),
        "b2m": sb("mo_b2m", [8, 1024], BF16),
        "b1a": sb("mo_b1a", [128, 8 * 32], F32),
        "gbrow": sb("mo_gbrow", [128, 8], F32),
        "ubuf2": sb("mo_ubuf2", [128, 1024], F32),
        "ps_mid": ps("mo_psmid", [128, 1024]),
        "ps_y": ps("mo_psy", [128, 2048]),
    }


def emit_moe(p, cfg, b, cb):
    T = cfg["T"]
    nt = T // 128
    n256 = T // 256
    h2F = cfg["h2F"]
    aps = cfg["aps"]
    hready = tuple(cfg.get("h2f_ready", ()))
    ps_g = b["ps_mid"]
    cache = cfg.get("w_cache")
    cmode = cfg.get("cache_mode")

    for kc in range(NKC):
        p.dma("gpsimd", b["gw"][:, kc * 8:(kc + 1) * 8],
              aps["gate_W"][kc * 128:(kc + 1) * 128, :], "dgw")
    p.dma("gpsimd", b["b2m"][:, :], aps["exp_b2"][:, :], "dgw")
    cb["ps_bc"] = b["ps_y"][:, 0:512]
    emit_row_bcast(p, b["gbrow"], aps["gate_b"], 8, cb)
    cb.pop("ps_bc")

    def load_w1(e, waits):
        if cmode == "read":
            for kc in range(NKC):
                p.dma("sync", b["w1"][:, kc * 4096:(kc + 1) * 4096],
                      cache[0][(e * 8 + kc) * 128:(e * 8 + kc + 1) * 128, :],
                      "dl1r", waits=waits if kc == 0 else ())
            return ("dl1r", p.n("dl1r"))
        if cmode == "write":
            waits = tuple(waits) + (("dc1", p.n("dc1")),)
        for kc in range(NKC):
            p.dma("gpsimd", b["w1"][:, kc * 4096:(kc + 1) * 4096],
                  bass.AP(aps["exp_W1"].tensor, e * 1024 * 4096 + kc * 128 * 4096,
                          [[4096, 128], [1, 4096]]),
                  "dlw1", waits=waits if kc == 0 else ())
        if cmode == "write":
            lv = p.n("dlw1")
            for kc in range(NKC):
                p.dma("sync", cache[0][(e * 8 + kc) * 128:(e * 8 + kc + 1) * 128, :],
                      b["w1"][:, kc * 4096:(kc + 1) * 4096], "dc1",
                      waits=(("dlw1", lv),) if kc == 0 else ())
        return ("dlw1", p.n("dlw1"))

    def load_w2(e, waits):
        if cmode == "read":
            for g in range(8):
                p.dma("sync", b["w2"][:, g * 4096:(g + 1) * 4096],
                      cache[1][(e * 8 + g) * 128:(e * 8 + g + 1) * 128, :],
                      "dl2r", waits=waits if g == 0 else ())
            return ("dl2r", p.n("dl2r"))
        if cmode == "write":
            waits = tuple(waits) + (("dc2", p.n("dc2")),)
        for g in range(8):
            src = bass.AP(aps["exp_W2"].tensor,
                          e * 4096 * 1024 + g * 512 * 1024,
                          [[1024, 128], [128 * 1024, 4], [1, 1024]])
            p.dma("gpsimd", b["w2"][:, g * 4096:(g + 1) * 4096], src,
                  "dlw2", waits=waits if g == 0 else ())
        if cmode == "write":
            lv = p.n("dlw2")
            for g in range(8):
                p.dma("sync", cache[1][(e * 8 + g) * 128:(e * 8 + g + 1) * 128, :],
                      b["w2"][:, g * 4096:(g + 1) * 4096], "dc2",
                      waits=(("dlw2", lv),) if g == 0 else ())
        return ("dlw2", p.n("dlw2"))

    def load_b1_all():
        cr, idf = cb["_colrow"], cb["_identf_sb"]
        for e in range(NE):
            src = aps["exp_b1"].tensor
            ap = bass.AP(src, e * 4096, [[128, 32], [1, 128]])
            p.dma("sync", cr[0:32, :], ap, "db1d",
                  waits=(("b1t", p.n("b1t")),))
            p.emit("tensor", lambda t: t.transpose(
                ps_g[:, 0:32], cr[0:32, :], idf[0:32, 0:32]),
                waits=(("db1d", p.n("db1d")), ("db1", p.n("db1"))),
                inc="b1t")
            p.emit("vector", lambda v, e=e: v.tensor_copy(
                b["b1a"][:, e * 32:(e + 1) * 32], ps_g[:, 0:32]),
                waits=(("b1t", p.n("b1t")),), inc="db1")

    load_b1_all()
    b1_done = p.n("db1")
    w1sem = load_w1(0, ())
    w2sem = load_w2(0, ())

    xft = cfg.get("h2f_tev")
    for tt in range(nt):
        for kc in range(NKC):
            w = []
            if tt == 0 and kc == 0:
                w = [("dgw", p.n("dgw")), ("db1", b1_done)] + list(hready)
            if kc == 0 and xft is not None:
                w.append(("tev", xft + 8 * (tt + 1)))
            p.emit("tensor",
                   lambda t, tt=tt, kc=kc: t.matmul(
                       ps_g[:, tt * 8:(tt + 1) * 8],
                       h2F[:, kc * T + tt * 128: kc * T + (tt + 1) * 128],
                       b["gw"][:, kc * 8:(kc + 1) * 8],
                       start=(kc == 0), stop=(kc == NKC - 1)),
                   waits=w, inc="gmm" if kc == NKC - 1 else None)
    gmmv = p.n("gmm")
    p.emit("vector", lambda v: v.scalar_tensor_tensor(
        b["glog"][:, 0:nt * 8], ps_g[:, 0:nt * 8], 1.0,
        sb_ap(b["gbrow"], 0, [[0, nt], [1, 8]]), ALU.mult, ALU.add),
        waits=(("gmm", gmmv), ("bc", p.n("bc"))), inc="gv")
    glog_done = p.n("gv")
    gv_tile = {}
    gs = b["gsc"]
    for tt in range(nt):
        lg = b["glog"][:, tt * 8:(tt + 1) * 8]
        wc = b["wcols"][:, tt * 8:(tt + 1) * 8]
        p.emit("vector", lambda v, lg=lg: v.reduce_max(
            gs[:, 0:1], lg, axis=AXX, negate=True),
            waits=(("gv", p.n("gv")),), inc="gv")
        p.emit("vector", lambda v, lg=lg: v.tensor_scalar(
            gs[:, 2:3], gs[:, 0:1], -1.0, None, ALU.mult),
            waits=(("gv", p.n("gv")),), inc="gv")
        p.emit("vector", lambda v, lg=lg: v.tensor_scalar(
            b["glog"][:, 120:128] if False else gs[:, 4:12], lg, gs[:, 2:3],
            None, ALU.is_equal),
            waits=(("gv", p.n("gv")),), inc="gv")
        p.emit("vector", lambda v, lg=lg: v.scalar_tensor_tensor(
            gs[:, 4:12], gs[:, 4:12], -1e30, lg, ALU.mult, ALU.add),
            waits=(("gv", p.n("gv")),), inc="gv")
        p.emit("vector", lambda v: v.reduce_max(gs[:, 1:2], gs[:, 4:12],
                                                axis=AXX),
               waits=(("gv", p.n("gv")),), inc="gv")
        p.emit("scalar", lambda s, lg=lg, wc=wc: s.activation(
            wc, lg, AF.Exp, bias=gs[:, 0:1], scale=1.0),
            waits=(("gv", p.n("gv")),), inc="gv")
        p.emit("scalar", lambda s: s.activation(
            gs[:, 3:4], gs[:, 1:2], AF.Exp, bias=gs[:, 0:1], scale=1.0),
            waits=(("gv", p.n("gv")),), inc="gv")
        p.emit("vector", lambda v: v.tensor_scalar(
            gs[:, 3:4], gs[:, 3:4], 1.0, None, ALU.add),
            waits=(("gv", p.n("gv")),), inc="gv")
        p.emit("vector", lambda v: v.reciprocal(gs[:, 3:4], gs[:, 3:4]),
               waits=(("gv", p.n("gv")),), inc="gv")
        p.emit("vector", lambda v, lg=lg: v.tensor_scalar(
            gs[:, 4:12], lg, gs[:, 1:2], None, ALU.is_ge),
            waits=(("gv", p.n("gv")),), inc="gv")
        p.emit("vector", lambda v, wc=wc: v.tensor_tensor(
            wc, wc, gs[:, 4:12], ALU.mult),
            waits=(("gv", p.n("gv")),), inc="gv")
        p.emit("vector", lambda v, wc=wc: v.tensor_scalar(
            wc, wc, gs[:, 3:4], None, ALU.mult),
            waits=(("gv", p.n("gv")),), inc="gv")
        gv_tile[tt] = p.n("gv")
    gv_done = p.n("gv")

    def emit_acc_init():
        uadd0 = p.n("uadd")
        x1w, x2w = cfg.get("x1_waits"), cfg.get("x2_waits")
        for ct in range(nt):
            sf, sf2 = cb["stage_f"][0], cb["stage_f"][1]
            w = [("uadd", max(uadd0, p.n("uadd") - 2))]
            if ct == 0:
                w.append(("xc", p.n("xc")))
            w1x = w + (list(x1w(ct)) if x1w else [])
            w2x = w + (list(x2w(ct)) if x2w else [])
            p.dma("sync", sf[:, :], cfg["x1_dram"][ct * 128:(ct + 1) * 128, :],
                  f"dr0", waits=w1x)
            p.dma("sync", sf2[:, :], cfg["x2_dram"][ct * 128:(ct + 1) * 128, :],
                  f"dr1", waits=w2x)
            p.emit("vector", lambda v, ct=ct: v.tensor_copy(
                b["wb8"][:, :], b["wcols"][:, ct * 8:(ct + 1) * 8]),
                waits=(("gv", gv_tile[ct]), ("wtm", p.n("wtm"))), inc="wte")
            p.emit("tensor", lambda t: t.transpose(
                cb["psT"][0:8, 0:128], b["wb8"][:, :], cb["ident"][:, :]),
                waits=(("wte", p.n("wte")), ("bmm2", p.n("bmm2"))), inc="wtm")
            p.emit("vector", lambda v: v.tensor_copy(
                b["wT"][:, :], cb["psT"][0:8, 0:128]),
                waits=(("wtm", p.n("wtm")),), inc="wte")
            p.emit("tensor", lambda t, ct=ct: t.matmul(
                b["ps_y"][:, 0:512], b["wT"][:, :],
                b["b2m"][:, 0:512], start=True, stop=True),
                waits=(("wte", p.n("wte")), ("dgw", p.n("dgw")),
                       ("bc", p.n("bc")), ("uadd", p.n("uadd"))))
            p.emit("tensor", lambda t, ct=ct: t.matmul(
                b["ps_y"][:, 512:1024], b["wT"][:, :],
                b["b2m"][:, 512:1024], start=True, stop=True), inc="bmm2")
            p.emit("vector",
                   lambda v, sf=sf, sf2=sf2: v.tensor_tensor(
                       cb["ubuf"][:, :], sf[:, :], sf2[:, :], ALU.add),
                   waits=(("dr0", p.n("dr0")), ("dr1", p.n("dr1")),
                          ("dwa", p.n("dwa")), ("bmm2", p.n("bmm2"))),
                   inc="uadd")
            p.emit("vector",
                   lambda v: v.tensor_tensor(cb["ubuf"][:, :], cb["ubuf"][:, :],
                                             b["ps_y"][:, 0:1024], ALU.add),
                   waits=(("uadd", p.n("uadd")),), inc="uadd")
            p.dma("sync", cfg["acc_dram"][ct * 128:(ct + 1) * 128, :],
                  cb["ubuf"][:, :], "dwa", waits=(("uadd", p.n("uadd")),))
        return p.n("dwa")


    dacc_prev = p.n("dacc")
    acc_init = None
    for e in range(NE):
        w1v, w2v = w1sem, w2sem
        mm1_first = True
        for tt in range(n256):
            for m in range(32):
                slot = (m % 2) * 512
                guard = ("gel", max(0, p.n("gel") - 1))
                for kc in range(NKC):
                    w = [guard] if kc == 0 else []
                    if mm1_first and kc == 0:
                        w += [w1v, ("db1", b1_done), ("gv", glog_done)]
                        mm1_first = False
                    if e == 0 and m == 0 and kc == 0 and xft is not None:
                        w.append(("tev", xft + 8 * min(nt, 2 * (tt + 1))))
                    p.emit("tensor",
                           lambda t, tt=tt, m=m, kc=kc, slot=slot: t.matmul(
                               b["ps_mid"][:, slot:slot + 256],
                               b["w1"][:, kc * 4096 + m * 128:
                                       kc * 4096 + (m + 1) * 128],
                               h2F[:, kc * T + tt * 256: kc * T + tt * 256 + 256],
                               start=(kc == 0), stop=(kc == NKC - 1)),
                           waits=w, inc="mm1" if kc == NKC - 1 else None)
                p.emit("scalar",
                       lambda s, m=m, slot=slot, e=e: s.activation(
                           b["mid"][:, m * 256:(m + 1) * 256],
                           b["ps_mid"][:, slot:slot + 256], AF.Gelu,
                           bias=b["b1a"][:, e * 32 + m:e * 32 + m + 1],
                           scale=1.0),
                       waits=(("mm1", p.n("mm1")), ("mm2", max(0, p.n("mm2") - 0)),), inc="gel")
            gelv = p.n("gel")
            if e == 0 and tt == 0:
                acc_init = emit_acc_init()
                cfg["_uadd_ps"] = p.n("uadd")
                cfg["_wte"] = p.n("wte")
            for ts in range(2):
                ysl = b["ps_y"][:, ts * 1024:(ts + 1) * 1024]
                for m in range(32):
                    w = []
                    if m == 0:
                        w = [("gel", gelv), ("yev", max(0, p.n("yev") - 1)),
                             w2v, ("dwa", p.n("dwa"))]
                    p.emit("tensor",
                           lambda t, ts=ts, m=m, ysl=ysl: t.matmul(
                               ysl[:, 0:512],
                               b["mid"][:, m * 256 + ts * 128:
                                        m * 256 + ts * 128 + 128],
                               b["w2"][:, m * 1024: m * 1024 + 512],
                               start=(m == 0), stop=(m == 31)),
                           waits=w)
                    p.emit("tensor",
                           lambda t, ts=ts, m=m, ysl=ysl: t.matmul(
                               ysl[:, 512:1024],
                               b["mid"][:, m * 256 + ts * 128:
                                        m * 256 + ts * 128 + 128],
                               b["w2"][:, m * 1024 + 512: (m + 1) * 1024],
                               start=(m == 0), stop=(m == 31)),
                           inc="mm2" if m == 31 else None)
                mm2v = p.n("mm2")
                tile = tt * 2 + ts
                p.emit("vector",
                       lambda v, ysl=ysl, tile=tile, e=e: v.tensor_scalar(
                           b["ubuf2"][:, :], ysl[:, 0:1024],
                           b["wcols"][:, tile * 8 + e: tile * 8 + e + 1],
                           None, ALU.mult),
                       waits=(("mm2", mm2v), ("gv", gv_tile[tile]),
                              ("dacc", p.n("dacc"))), inc="yev")
                p.emit("gpsimd",
                       lambda g, tile=tile: g.dma_start(
                           cfg["acc_dram"][tile * 128:(tile + 1) * 128, :],
                           b["ubuf2"][:, :], accum_op=ALU.add),
                       waits=(("yev", p.n("yev")), ("dwa", acc_init),
                              ("dacc", dacc_prev)),
                       inc="dacc", inc_by=16)
        dacc_prev = p.n("dacc")
        if e + 1 < NE:
            w1sem = load_w1(e + 1, (("mm1", p.n("mm1")),))
            w2sem = load_w2(e + 1, (("mm2", p.n("mm2")),))

    uadd0 = p.n("uadd")
    dacc_fin = p.n("dacc")
    xfb = cfg.get("xfb")
    tmm0 = p.n("tmm")
    tev0 = p.n("tev")
    for ct in range(nt):
        sf = cb["stage_f"][ct % 2]
        drs, dws = f"dr{ct % 2}", f"dw{ct % 2}"
        dwp = f"dw{(ct - 1) % 2}"
        w = [("uadd", max(uadd0, p.n("uadd") - 1)),
             ("dacc", dacc_fin - 16 * (nt - 1 - ct))]
        p.dma("sync", sf[:, :], cfg["acc_dram"][ct * 128:(ct + 1) * 128, :],
              drs, waits=w)
        emit_ln_chunk(p, sf, 1024, cb, waits=((drs, p.n(drs)),))
        wbf = (("ubf", p.n("ubf")),) if xfb is not None else ()
        p.emit("scalar",
               lambda s, sf=sf: s.activation(cb["ubuf"][:, :], sf[:, :],
                                             AF.Identity,
                                             bias=cb["nmrs"][:, :],
                                             scale=cb["rstd"][:, :]),
               waits=(("ln", p.n("ln")), (dwp, p.n(dwp))) + wbf, inc="uadd")
        p.dma("sync", cfg["out_dram"][ct * 128:(ct + 1) * 128, :],
              cb["ubuf"][:, :], dws, waits=(("uadd", p.n("uadd")),))
        if xfb is not None:
            sbb = cb["stage_b"][ct % 2]
            p.emit("vector",
                   lambda v, sbb=sbb: v.tensor_copy(sbb[:, :],
                                                    cb["ubuf"][:, :]),
                   waits=(("uadd", p.n("uadd")),
                          ("tmm", tmm0 + 8 * max(0, ct - 1))),
                   inc="ubf")
            emit_tiles_to_fmajor(
                p, xfb, T, range(NKC), [ct],
                lambda ct2, cd, sbb=sbb: sbb[:, cd * 128:(cd + 1) * 128],
                cb, waits=(("ubf", p.n("ubf")),))
    if cfg.get("trailing", True):
        p.wait("sync", "dw0")
        p.wait("sync", "dw1")
    if cmode == "write":
        p.wait("sync", "dc1")
        p.wait("sync", "dc2")
    return tev0


def emit_cast_to_scratch(p, src_dram, dst_dram, rows, cols, stg, sem="dsc"):
    assert rows % 128 == 0
    for i in range(rows // 128):
        g = stg[i % 2]
        p.dma("gpsimd", g[:, 0:cols], src_dram[i * 128:(i + 1) * 128, :],
              f"dcc{i % 2}", waits=((sem + str(i % 2), p.n(sem + str(i % 2))),))
        p.dma("sync", dst_dram[i * 128:(i + 1) * 128, :], g[:, 0:cols],
              sem + str(i % 2), waits=((f"dcc{i % 2}", p.n(f"dcc{i % 2}")),))
    p.wait("sync", sem + "0")
    p.wait("sync", sem + "1")


def emit_trload_slab(p, dst_sb, src_dram_bf, n_out, kc_range, sem="dtw"):
    for kc in kc_range:
        p.dma_tr(dst_sb[:, kc * n_out:(kc + 1) * n_out],
                 src_dram_bf[0:n_out, kc * 128:(kc + 1) * 128], sem)



def emit_absmean_quant(p, src_dram, dst_dram, rows, cols, b, cb, tag):
    n128 = rows // 128
    sA, sB = b["qs"]
    acc = b["qacc"]
    dsem = "dq" + tag
    p.emit("vector", lambda v: v.memset(acc[:, :], 0.0),
           waits=(("qv", p.n("qv")),), inc="qv")
    p.emit("vector", lambda v: v.memset(b["qone"][:, :], 1.0), inc="qv")
    for i in range(n128):
        g = sA if i % 2 == 0 else sB
        ds = dsem + str(i % 2)
        p.dma("sync", g[:, 0:cols], src_dram[i * 128:(i + 1) * 128, :], ds,
              waits=(("qv", max(0, p.n("qv") - 2)),))
        p.emit("vector",
               lambda v, g=g: v.reduce_sum(b["thr"][:, :], g[:, 0:cols],
                                           axis=AXX,
                                           apply_absolute_value=True),
               waits=((ds, p.n(ds)), ("qv", p.n("qv"))), inc="qv")
        p.emit("vector",
               lambda v: v.tensor_tensor(acc[:, :], acc[:, :], b["thr"][:, :],
                                         ALU.add),
               waits=(("qv", p.n("qv")),), inc="qv")
    p.emit("tensor", lambda t: t.matmul(
        cb["ps_bc"][0:1, 0:1], acc[:, :], b["qone"][:, :],
        start=True, stop=True),
        waits=(("qv", p.n("qv")),), inc="qmm")
    p.emit("vector", lambda v: v.memset(b["q11"][:, 0:1], 0.5e-6),
           waits=(("qmm", p.n("qmm")),), inc="qv")
    p.emit("vector", lambda v, N=rows * cols: v.scalar_tensor_tensor(
        b["q11"][:, 1:2], cb["ps_bc"][0:1, 0:1], 0.5 / N, b["q11"][:, 0:1],
        ALU.mult, ALU.add),
        waits=(("qv", p.n("qv")),), inc="qv")
    p.emit("vector", lambda v: v.memset(cb["ones1"][:, :], 1.0),
           waits=(("bc", p.n("bc")),), inc="bc")
    p.emit("tensor", lambda t: t.matmul(
        cb["ps_bc"][:, 0:1], cb["ones1"][:, :], b["q11"][:, 1:2],
        start=True, stop=True),
        waits=(("qv", p.n("qv")), ("bc", p.n("bc"))), inc="qmm")
    p.emit("scalar", lambda s: s.copy(b["thr"][:, :], cb["ps_bc"][:, 0:1]),
           waits=(("qmm", p.n("qmm")),), inc="qv")
    thr_done = p.n("qv")
    for i in range(n128):
        g = sA if i % 2 == 0 else sB
        gm = b["qm"]
        ds = dsem + str(i % 2)
        p.dma("sync", g[:, 0:cols], src_dram[i * 128:(i + 1) * 128, :], ds,
              waits=(("qv", max(thr_done, p.n("qv") - 3)),))
        p.emit("scalar", lambda s, g=g: s.activation(
            b["qm"][:, 0:cols], g[:, 0:cols], AF.Abs),
            waits=((ds, p.n(ds)), ("qv", p.n("qv"))), inc="qv")
        p.emit("vector", lambda v: v.tensor_scalar(
            b["qm"][:, 0:cols], b["qm"][:, 0:cols], b["thr"][:, :], None,
            ALU.is_gt),
            waits=(("qv", p.n("qv")), ("qmm", p.n("qmm"))), inc="qv")
        p.emit("scalar", lambda s, g=g: s.sign(g[:, 0:cols], g[:, 0:cols]),
               waits=(("qv", p.n("qv") - 1),), inc="qv")
        p.emit("vector", lambda v, g=g: v.tensor_tensor(
            b["qb"][:, 0:cols], g[:, 0:cols], b["qm"][:, 0:cols], ALU.mult),
            waits=(("qv", p.n("qv")),), inc="qv")
        p.dma("sync", dst_dram[i * 128:(i + 1) * 128, :], b["qb"][:, 0:cols],
              "dqo" + tag, waits=(("qv", p.n("qv")),))
        p.wait("vector", "dqo" + tag)
    p.wait("sync", "dqo" + tag)


def alloc_quant_bufs(nc, st):
    sb = lambda n, shp, dt: st.enter_context(nc.sbuf_tensor(_u(n), shp, dt))
    return {
        "qs": [sb("q_s0", [128, 4096], F32), sb("q_s1", [128, 4096], F32)],
        "qm": sb("q_m", [128, 4096], F32),
        "qb": sb("q_b", [128, 4096], BF16),
        "qacc": sb("q_acc", [128, 1], F32),
        "qone": sb("q_one", [128, 1], F32),
        "q11": sb("q_11", [1, 2], F32),
        "thr": sb("q_thr", [128, 1], F32),
    }


def emit_bitffn(p, cfg, cb):
    T = cfg["T"]
    nt = T // 128
    xF = cfg["xF"]
    b = cfg["bufs"]
    wready = tuple(cfg.get("w_ready", ()))
    uadd0 = p.n("uadd")
    xcv = p.n("xc")
    fmm10, fgg0, fgn0 = p.n("fmm1"), p.n("fgg"), p.n("fgn")
    fmm20, tmm0, tev0 = p.n("fmm2"), p.n("tmm"), p.n("tev")
    xft = cfg.get("xf_tev")

    def emit_mm1(ct):
        sf = cb["stage_f"][ct % 2]
        w = [("uadd", max(uadd0, p.n("uadd") - 1))]
        if ct < 2:
            w.append(("xc", xcv))
        if cfg.get("x_waits") is not None:
            w += list(cfg["x_waits"](ct))
        p.dma("sync", sf[:, :], cfg["x_dram"][ct * 128:(ct + 1) * 128, :],
              f"dr{ct % 2}", waits=w)
        for qd in range(4):
            slot = (qd % 2) * 1024
            gslot = fgg0 + 4 * ct + qd - 1
            for kc in range(NKC):
                ww = []
                if kc == 0 and gslot > fgg0:
                    ww.append(("fgg", gslot))
                if ct == 0 and qd == 0 and kc == 0:
                    ww += list(wready)
                if qd == 0 and kc == 0 and xft is not None:
                    ww.append(("tev", xft + 8 * (ct + 1)))
                for h2 in range(2):
                    p.emit("tensor",
                           lambda t, ct=ct, qd=qd, kc=kc, h2=h2, slot=slot:
                           t.matmul(
                               b["ps_h"][:, slot + h2 * 512: slot + (h2 + 1) * 512],
                               xF[:, kc * T + ct * 128: kc * T + (ct + 1) * 128],
                               cfg["wfc_sb"][:, kc * 4096 + qd * 1024 + h2 * 512:
                                             kc * 4096 + qd * 1024 + (h2 + 1) * 512],
                               start=(kc == 0), stop=(kc == NKC - 1)),
                           waits=ww,
                           inc="fmm1" if (kc == NKC - 1 and h2 == 1) else None)
                    ww = []
            htm = b["h_tm"][ct % 2]
            p.emit("scalar",
                   lambda s, ct=ct, qd=qd, slot=slot, htm=htm: s.activation(
                       htm[:, qd * 1024:(qd + 1) * 1024],
                       b["ps_h"][:, slot:slot + 1024], AF.Gelu),
                   waits=(("fmm1", fmm10 + 4 * ct + qd + 1),
                          ("tmm", tmm0 + 32 * max(0, ct - 1))), inc="fgg")
        htm = b["h_tm"][ct % 2]
        emit_ln_chunk(p, htm, 4096, cb,
                      waits=(("fgg", fgg0 + 4 * (ct + 1)),),
                      dest=("rstd2", "nmrs2"))
        p.emit("scalar",
               lambda s, htm=htm: s.activation(htm[:, :], htm[:, :],
                                               AF.Identity,
                                               bias=cb["nmrs2"][:, :],
                                               scale=cb["rstd2"][:, :]),
               waits=(("ln", p.n("ln")),), inc="fgn")

    def emit_mm2(ct):
        sf = cb["stage_f"][ct % 2]
        drs, dws = f"dr{ct % 2}", f"dw{ct % 2}"
        dwp = f"dw{(ct - 1) % 2}"
        htm = b["h_tm"][ct % 2]
        fg = tuple(cfg.get("psT_ready", ())) if ct == 0 else ()
        emit_tiles_to_fmajor(
            p, b["hF_t"], 128, range(32), [0],
            lambda ct2, cd, htm=htm: htm[:, cd * 128:(cd + 1) * 128],
            cb, waits=(("fgn", fgn0 + ct + 1),
                       ("fmm2", max(fmm20, fmm20 + ct))) + fg,
            gb=(cfg["lnw_cols"], cfg["lnb_cols"]))
        for kc2 in range(32):
            ww = []
            if kc2 == 0:
                ww = [("tev", p.n("tev")), ("uadd", max(uadd0, p.n("uadd")))]
            for h2 in range(2):
                p.emit("tensor",
                       lambda t, kc2=kc2, h2=h2: t.matmul(
                           b["ps_o"][:, h2 * 512:(h2 + 1) * 512],
                           b["hF_t"][:, kc2 * 128:(kc2 + 1) * 128],
                           cfg["wproj_sb"][:, kc2 * 1024 + h2 * 512:
                                           kc2 * 1024 + (h2 + 1) * 512],
                           start=(kc2 == 0), stop=(kc2 == 31)),
                       waits=ww,
                       inc="fmm2" if (kc2 == 31 and h2 == 1) else None)
                ww = []
        mm2v = p.n("fmm2")
        p.emit("vector",
               lambda v, sf=sf: v.tensor_tensor(
                   cb["ubuf"][:, :], sf[:, :], b["ps_o"][:, 0:1024], ALU.add),
               waits=(("fmm2", mm2v), (drs, p.n(drs)), (dwp, p.n(dwp))),
               inc="uadd")
        emit_ln_chunk(p, cb["ubuf"], 1024, cb, waits=(("uadd", p.n("uadd")),))
        p.emit("scalar",
               lambda s: s.activation(cb["ubuf"][:, :], cb["ubuf"][:, :],
                                      AF.Identity, bias=cb["nmrs"][:, :],
                                      scale=cb["rstd"][:, :]),
               waits=(("ln", p.n("ln")),), inc="uout")
        p.dma("sync", cfg["out_dram"][ct * 128:(ct + 1) * 128, :],
              cb["ubuf"][:, :], dws, waits=(("uout", p.n("uout")),))

    for ct in range(nt):
        emit_mm1(ct)
        if ct >= 1:
            emit_mm2(ct - 1)
    emit_mm2(nt - 1)
    if cfg.get("trailing", True):
        p.wait("sync", "dw0")
        p.wait("sync", "dw1")

WNAMES = [
    ("L0_attn_W_K", [16, 16]), ("L0_attn_W_Q", [16, 16]),
    ("L0_attn_W_V", [16, 16]), ("L0_attn_W_pred", [16, 64, 64]),
    ("L0_ffn_ln1_g", [D]), ("L0_ffn_ln1_b", [D]),
    ("L0_ffn_ln2_g", [D]), ("L0_ffn_ln2_b", [D]),
    ("L0_ffn_Wq", [D, D]), ("L0_ffn_bq", [D]),
    ("L0_ffn_Wk", [D, D]), ("L0_ffn_bk", [D]),
    ("L0_ffn_Wv", [D, D]), ("L0_ffn_bv", [D]),
    ("L0_ffn_Wp", [D, D]), ("L0_ffn_bp", [D]),
    ("L0_ffn_gate_W", [D, NE]), ("L0_ffn_gate_b", [NE]),
    ("L0_ffn_exp_W1", [NE, D, HID]), ("L0_ffn_exp_b1", [NE, HID]),
    ("L0_ffn_exp_W2", [NE, HID, D]), ("L0_ffn_exp_b2", [NE, D]),
    ("L1_attn_qkv_w", [3 * D, D]), ("L1_attn_qkv_b", [3 * D]),
    ("L1_attn_proj_w", [D, D]), ("L1_attn_proj_b", [D]),
    ("L1_ffn_W_fc", [HID, D]), ("L1_ffn_ln_w", [HID]),
    ("L1_ffn_ln_b", [HID]), ("L1_ffn_W_proj", [D, HID]),
]


def build_full(T=2048, n_seq=4):
    nc = bass.Bass("TRN2", target_bir_lowering=False)
    a = {}
    a["x_in"] = nc.dram_tensor("x_in", [n_seq * T, D], F32,
                               kind="ExternalInput").ap()
    for n, shp in WNAMES:
        a[n] = nc.dram_tensor(n, shp, F32, kind="ExternalInput").ap()
    for n, shp in (("cst_ident", [128, 128]), ("cst_tri", [128, 128]),
                   ("cst_esel", [16, 8 * 128]), ("cst_dpat", [128, 128])):
        a[n] = nc.dram_tensor(n, shp, BF16, kind="ExternalInput").ap()
    a["cst_identf"] = nc.dram_tensor("cst_identf", [128, 128], F32,
                                     kind="ExternalInput").ap()
    a["x_out"] = nc.dram_tensor("x_out", [n_seq * T, D], F32,
                                kind="ExternalOutput").ap()
    a["l0qkv"] = nc.dram_tensor("l0qkv", [D, 3 * D], BF16).ap()
    a["qkvbf"] = nc.dram_tensor("qkvbf", [3 * D, D], BF16).ap()
    a["projbf"] = nc.dram_tensor("projbf", [D, D], BF16).ap()
    a["wfcq"] = nc.dram_tensor("wfcq", [HID, D], BF16).ap()
    a["wprq"] = nc.dram_tensor("wprq", [D, HID], BF16).ap()
    xs = [nc.dram_tensor(f"xs{i}", [T, D], F32).ap() for i in range(4)]
    acc = nc.dram_tensor("acc_s", [T, D], F32).ap()
    w1bf = nc.dram_tensor("w1bf", [NE * 8 * 128, 4096], BF16).ap()
    w2bf = nc.dram_tensor("w2bf", [NE * 8 * 128, 4096], BF16).ap()
    xbf_in = nc.dram_tensor("xbf_in", [n_seq * T, D], BF16).ap()

    with ExitStack() as st:
        p = Prog(nc, st)
        cb = alloc_common_bufs(nc, st)
        cb["ident"] = cb["_ident_sb"]
        p.dma("sync", cb["_ident_sb"][:, :], a["cst_ident"][:, :], "dcst")
        p.dma("sync", cb["_tri_sb"][:, :], a["cst_tri"][:, :], "dcst")
        p.dma("sync", cb["_identf_sb"][:, :], a["cst_identf"][:, :], "dcst")
        p.emit("vector", lambda v: v.memset(cb["eps_col"][:, :], LN_EPS),
               inc="ev")
        p.wait("tensor", "dcst")
        p.wait("scalar", "ev")
        p.wait("vector", "ev")

        with ExitStack() as st0:
            b0 = alloc_p0_bufs(nc, st0)
            emit_phase0_l0(p, a, b0, cb["_ident_sb"])
            p.flush("p0a")
        with ExitStack() as st0:
            cstg = [st0.enter_context(nc.sbuf_tensor(_u("cst0"), [128, 4096], BF16)),
                    st0.enter_context(nc.sbuf_tensor(_u("cst1"), [128, 4096], BF16))]
            emit_cast_to_scratch(p, a["L1_attn_qkv_w"], a["qkvbf"], 3 * D, D,
                                 cstg)
            emit_cast_to_scratch(p, a["L1_attn_proj_w"], a["projbf"], D, D,
                                 cstg)
            emit_cast_to_scratch(p, a["x_in"], xbf_in, n_seq * T, D, cstg)
            p.flush("p0b")
        with ExitStack() as st0:
            qb = alloc_quant_bufs(nc, st0)
            cb["ps_bc"] = st0.enter_context(
                nc.psum_tensor(_u("psbc"), [128, 512], F32))
            emit_absmean_quant(p, a["L1_ffn_W_fc"], a["wfcq"], HID, D, qb, cb,
                               "f")
            emit_absmean_quant(p, a["L1_ffn_W_proj"], a["wprq"], D, HID, qb,
                               cb, "p")
            p.flush("p0q")
            cb.pop("ps_bc")

        sbg = lambda stk, n, shp, dt: stk.enter_context(
            nc.sbuf_tensor(_u(n), shp, dt))

        gcols = st.enter_context(nc.sbuf_tensor("g_cols", [128, 128], F32))
        with ExitStack() as stR:
            cb["ps_cl"] = stR.enter_context(
                nc.psum_tensor(_u("psbc"), [128, 32], F32))
            for (lo, hi, name, sl) in (
                    (0, 8, "L0_ffn_bq", None), (8, 16, "L0_ffn_bk", None),
                    (16, 24, "L0_ffn_ln1_g", None),
                    (24, 32, "L0_ffn_ln1_b", None),
                    (32, 40, "L0_ffn_ln2_g", None),
                    (40, 48, "L0_ffn_ln2_b", None),
                    (48, 56, "L1_attn_qkv_b", (0, 1024)),
                    (56, 64, "L1_attn_qkv_b", (1024, 2048)),
                    (64, 96, "L1_ffn_ln_w", None),
                    (96, 128, "L1_ffn_ln_b", None)):
                src = a[name] if sl is None else a[name][sl[0]:sl[1]]
                emit_col_load(p, gcols[:, lo:hi], src, cb)
            p.wait("tensor", "clv")
            cb.pop("ps_cl")
        p.flush("p0c")

        def snap(*names):
            return [(n, p.n(n)) for n in names]

        def fence(waits):
            for e in ENGS:
                p.emit(e, None, waits=tuple(waits))

        FE = []
        fmm1E0 = 0
        dwD = None
        for sq in range(n_seq):
            x0 = a["x_in"][sq * T:(sq + 1) * T, :]
            xo = a["x_out"][sq * T:(sq + 1) * T, :]

            with ExitStack() as stA:
                ab = alloc_attn_bufs(nc, stA, T)
                ab["psT"] = cb["psT"]
                ab["ident"] = cb["_ident_sb"]
                xF = cb["xfb"]
                dxt0 = p.n("dxt")
                gw = None
                if sq > 0:
                    gw = lambda j, b0=fmm1E0: [("fmm1", b0 + 16 * (j + 1))]
                emit_load_xF_fast(p, xbf_in[sq * T:(sq + 1) * T, :], xF, T,
                                  group_waits=gw)
                fence(FE)
                z_tm = sbg(stA, "A_ztm", [128, (T // 128) * 1024], BF16)
                wq_sb = sbg(stA, "A_wq", [128, 8 * 1024], BF16)
                wk_sb = sbg(stA, "A_wk", [128, 8 * 1024], BF16)
                wv_sb = sbg(stA, "A_wv", [128, 8 * 1024], BF16)
                load_weight_slab(p, wq_sb, a["l0qkv"][:, 0:1024], cast=False)
                load_weight_slab(p, wk_sb, a["l0qkv"][:, 1024:2048], cast=False)
                load_weight_slab(p, wv_sb, a["l0qkv"][:, 2048:3072], cast=False)
                cfg = {"T": T, "causal": True, "xF": xF, "z_tm": z_tm,
                       "wk_sb": wk_sb, "wq_sb": wq_sb, "wv_sb": wv_sb,
                       "tri": cb["_tri_sb"], "xf_tev": None, "xf_dxt0": dxt0,
                       "qkv_ready": [("dwt", p.n("dwt")),
                                     ("dsc0", p.n("dsc0")),
                                     ("dsc1", p.n("dsc1")),
                                     ("dp0s", p.n("dp0s"))],
                       "xf_ready": [],
                       "bq_cols": None, "bk_cols": None, "bv_row": None}
                emit_attention(p, cfg, ab)
                emit_resid_ln_out(p, x0, z_tm, xs[0], T, cb, trailing=False)
                FA = snap("amm", "zmm", "qkev", "eev", "zfev", "vev", "zev",
                          "uadd")
                dwA = (p.n("dw0"), p.n("dw1"))

            with ExitStack() as stB:
                fence(FA)
                hF = cb["xfb"]
                z_tm = sbg(stB, "B_ztm", [128, (T // 128) * 1024], BF16)
                wq_sb = sbg(stB, "B_wq", [128, 8 * 1024], BF16)
                wk_sb = sbg(stB, "B_wk", [128, 8 * 1024], BF16)
                bv_row = sbg(stB, "B_bvr", [128, 1024], F32)
                bp_row = sbg(stB, "B_bpr", [128, 1024], F32)
                with ExitStack() as stR:
                    cb["ps_bc"] = stR.enter_context(
                        nc.psum_tensor(_u("psbc"), [128, 512], F32))
                    emit_row_bcast(p, bv_row, a["L0_ffn_bv"], 1024, cb)
                    emit_row_bcast(p, bp_row, a["L0_ffn_bp"], 1024, cb)
                    cb.pop("ps_bc")
                load_weight_slab(p, wq_sb, a["L0_ffn_Wq"], cast=True)
                load_weight_slab(p, wk_sb, a["L0_ffn_Wk"], cast=True)
                bcols = gcols[:, 0:16]
                ab = alloc_attn_bufs(nc, stB, T)
                ab["psT"] = cb["psT"]
                ab["ident"] = cb["_ident_sb"]
                tev0 = emit_load_ln_hF(p, xs[0], hF, T, cb,
                                       gcols[:, 16:24], gcols[:, 24:32],
                                       x_waits=dw_chunk_waits(p, *dwA))
                cfg = {"T": T, "causal": True, "xF": hF, "z_tm": z_tm,
                       "wk_sb": wk_sb, "wq_sb": wq_sb, "tri": cb["_tri_sb"],
                       "xf_tev": tev0,
                       "qkv_ready": [("dwg", p.n("dwg")),
                                     ("bc", p.n("bc"))],
                       "xf_ready": [],
                       "bq_cols": bcols[:, 0:8], "bk_cols": bcols[:, 8:16],
                       "bv_row": bv_row}
                with ExitStack() as stV:
                    wv_sb = sbg(stV, "B_wv", [128, 8 * 1024], BF16)
                    load_weight_slab(p, wv_sb, a["L0_ffn_Wv"], cast=True,
                                     sem="dwv")
                    vcfg = dict(cfg)
                    vcfg["wv_sb"] = wv_sb
                    vcfg["qkv_ready"] = [("dwv", p.n("dwv")),
                                         ("bc", p.n("bc"))]
                    vcfg["bv_ready"] = [("bc", p.n("bc"))]
                    emit_attention_v(p, vcfg, ab)
                vmm_done = p.n("amm")
                wp_sb = sbg(stB, "B_wp", [128, 8 * 1024], BF16)
                load_weight_slab(p, wp_sb, a["L0_ffn_Wp"], cast=True,
                                 sem="dwp", waits=(("amm", vmm_done),))
                emit_attention_pairs(p, cfg, ab)
                zF = hF
                emit_zf_transpose(p, z_tm, zF, T, cb,
                                  guard=[("zev", p.n("zev")),
                                         ("amm", p.n("amm"))])
                emit_proj_resid(p, zF, wp_sb, bp_row, xs[0], xs[1], T,
                                ab["ps_main"][:, 0:1024], cb, final_ln=False,
                                w_ready=[("dwp", p.n("dwp")),
                                         ("eev", p.n("eev"))],
                                x_waits=dw_chunk_waits(p, *dwA),
                                trailing=False)
                FB = snap("amm", "zmm", "qkev", "eev", "zfev", "vev", "zev",
                          "uadd", "pmm", "tmm", "tev")
                dwB = (p.n("dw0"), p.n("dw1"))

            with ExitStack() as stC:
                fence(FB)
                h2F = cb["xfb"]
                mb = alloc_moe_bufs(nc, stC)
                tev0 = emit_load_ln_hF(p, xs[1], h2F, T, cb,
                                       gcols[:, 32:40], gcols[:, 40:48],
                                       x_waits=dw_chunk_waits(p, *dwB))
                mcfg = {"T": T, "h2F": h2F, "h2f_tev": tev0,
                        "aps": {"gate_W": a["L0_ffn_gate_W"],
                                "gate_b": a["L0_ffn_gate_b"],
                                "exp_W1": a["L0_ffn_exp_W1"],
                                "exp_b1": a["L0_ffn_exp_b1"],
                                "exp_W2": a["L0_ffn_exp_W2"],
                                "exp_b2": a["L0_ffn_exp_b2"]},
                        "x1_dram": xs[0], "x2_dram": xs[1],
                        "acc_dram": acc, "out_dram": xs[2],
                        "xfb": cb["xfb"],
                        "x1_waits": dw_chunk_waits(p, *dwA),
                        "x2_waits": dw_chunk_waits(p, *dwB),
                        "w_cache": (w1bf, w2bf),
                        "cache_mode": "write" if sq == 0 else "read",
                        "h2f_ready": [], "trailing": False}
                tevC = emit_moe(p, mcfg, mb, cb)
                FC = snap("mm1", "mm2", "gel", "yev", "gv", "dacc", "wte",
                          "tmm", "tev", "ubf") + [("uadd", mcfg["_uadd_ps"])]
                dwC = (p.n("dw0"), p.n("dw1"))

            with ExitStack() as stD:
                fence(FC)
                xF = cb["xfb"]
                z_tm = sbg(stD, "D_ztm", [128, (T // 128) * 1024], BF16)
                wq_sb = sbg(stD, "D_wq", [128, 8 * 1024], BF16)
                wk_sb = sbg(stD, "D_wk", [128, 8 * 1024], BF16)
                bv_row = sbg(stD, "D_bvr", [128, 1024], F32)
                bp_row = sbg(stD, "D_bpr", [128, 1024], F32)
                with ExitStack() as stR:
                    cb["ps_bc"] = stR.enter_context(
                        nc.psum_tensor(_u("psbc"), [128, 512], F32))
                    emit_row_bcast(p, bv_row, a["L1_attn_qkv_b"][2048:3072],
                                   1024, cb)
                    emit_row_bcast(p, bp_row, a["L1_attn_proj_b"], 1024, cb)
                    cb.pop("ps_bc")
                emit_trload_slab(p, wq_sb, a["qkvbf"][0:1024, :], 1024,
                                 range(8))
                emit_trload_slab(p, wk_sb, a["qkvbf"][1024:2048, :], 1024,
                                 range(8))
                bcols = gcols[:, 48:64]
                ab = alloc_attn_bufs(nc, stD, T)
                ab["psT"] = cb["psT"]
                ab["ident"] = cb["_ident_sb"]
                cfg = {"T": T, "causal": False, "xF": xF, "z_tm": z_tm,
                       "wk_sb": wk_sb, "wq_sb": wq_sb, "tri": cb["_tri_sb"],
                       "xf_tev": tevC,
                       "qkv_ready": [("dtw", p.n("dtw")),
                                     ("bc", p.n("bc"))],
                       "xf_ready": [],
                       "bq_cols": bcols[:, 0:8], "bk_cols": bcols[:, 8:16],
                       "bv_row": bv_row}
                with ExitStack() as stV:
                    wv_sb = sbg(stV, "D_wv", [128, 8 * 1024], BF16)
                    emit_trload_slab(p, wv_sb, a["qkvbf"][2048:3072, :], 1024,
                                     range(8), sem="dtv")
                    vcfg = dict(cfg)
                    vcfg["wv_sb"] = wv_sb
                    vcfg["qkv_ready"] = [("dtv", p.n("dtv")),
                                         ("bc", p.n("bc"))]
                    vcfg["bv_ready"] = [("bc", p.n("bc"))]
                    emit_attention_v(p, vcfg, ab)
                vmm_done = p.n("amm")
                wp_sb = sbg(stD, "D_wp", [128, 8 * 1024], BF16)
                p.emit("sync", None, waits=(("amm", vmm_done),))
                emit_trload_slab(p, wp_sb, a["projbf"], 1024, range(8),
                                 sem="dtp")
                emit_attention_pairs(p, cfg, ab)
                zF = xF
                emit_zf_transpose(p, z_tm, zF, T, cb,
                                  guard=[("zev", p.n("zev")),
                                         ("amm", p.n("amm"))])
                tevD = emit_proj_resid(p, zF, wp_sb, bp_row, xs[2], xs[3], T,
                                       ab["ps_main"][:, 0:1024], cb,
                                       final_ln=True,
                                       w_ready=[("dtp", p.n("dtp")),
                                                ("eev", p.n("eev"))],
                                       xfb=cb["xfb"],
                                       x_waits=dw_chunk_waits(p, *dwC),
                                       trailing=False)
                FD = snap("amm", "zmm", "qkev", "eev", "zfev", "vev", "zev",
                          "uadd", "pmm", "tmm", "tev", "ubf")
                dwD = (p.n("dw0"), p.n("dw1"))

            with ExitStack() as stE:
                fence(FD)
                xF = cb["xfb"]
                wfc_sb = sbg(stE, "E_wfc", [128, 8 * 4096], BF16)
                wproj_sb = sbg(stE, "E_wpr", [128, 32 * 1024], BF16)

                eb = {"h_tm": [sbg(stE, "E_htm0", [128, 4096], BF16),
                               sbg(stE, "E_htm1", [128, 4096], BF16)],
                      "hF_t": sbg(stE, "E_hFt", [128, 32 * 128], BF16),
                      "ps_h": stE.enter_context(
                          nc.psum_tensor(_u("E_psh"), [128, 2048], F32)),
                      "ps_o": stE.enter_context(
                          nc.psum_tensor(_u("E_pso"), [128, 1024], F32))}
                emit_trload_slab(p, wfc_sb, a["wfcq"], 4096, range(8))
                for kc2 in range(32):
                    p.dma_tr(wproj_sb[:, kc2 * 1024:(kc2 + 1) * 1024],
                             a["wprq"][0:1024, kc2 * 128:(kc2 + 1) * 128],
                             "dtw")
                fmm1E0 = p.n("fmm1")
                ecfg = {"T": T, "xF": xF, "wfc_sb": wfc_sb, "xf_tev": tevD,
                        "wproj_sb": wproj_sb, "lnw_cols": gcols[:, 64:96],
                        "lnb_cols": gcols[:, 96:128], "x_dram": xs[3],
                        "out_dram": xo, "bufs": eb,
                        "w_ready": [("dtw", p.n("dtw"))],
                        "x_waits": dw_chunk_waits(p, *dwD),
                        "trailing": sq == n_seq - 1,
                        "xf_ready": []}
                emit_bitffn(p, ecfg, cb)
                FE = snap("fmm1", "fmm2", "fgg", "fgn", "tmm", "tev", "ln",
                          "uadd", "uout")
        p.flush("seqs")
    return nc


def kernel(**inputs):
    T, n_seq, n_cores = 2048, 4, 8
    nc = build_full(T=T, n_seq=n_seq)
    consts = host_consts()
    x = np.ascontiguousarray(inputs["L0_attn_x"])
    in_maps = []
    for c in range(n_cores):
        m = {"x_in": np.ascontiguousarray(
            x[c * n_seq:(c + 1) * n_seq].reshape(n_seq * T, D))}
        for n, _ in WNAMES:
            m[n] = inputs[n]
        m.update(consts)
        in_maps.append(m)
    res = run_bass_kernel_spmd(nc, in_maps, core_ids=list(range(n_cores)))
    out = np.empty((32, T, D), dtype=np.float32)
    for c in range(n_cores):
        out[c * n_seq:(c + 1) * n_seq] =             res.results[c]["x_out"].reshape(n_seq, T, D)
    return out


# version 15
# speedup vs baseline: 1.0564x; 1.0564x over previous
import sys
sys.path.insert(0, "/opt/trn_rl_repo")
from contextlib import ExitStack
import numpy as np
import ml_dtypes
import concourse.bass as bass
import concourse.mybir as mybir
from concourse.bass_utils import run_bass_kernel_spmd

F32 = mybir.dt.float32
BF16 = mybir.dt.bfloat16
AF = mybir.ActivationFunctionType
ALU = mybir.AluOpType
AXX = mybir.AxisListType.X

D = 1024
NH = 16
DH = 64
NPAIR = 8
HID = 4096
NE = 8
NKC = 8
LN_EPS = 1e-5

ENGS = ("tensor", "scalar", "vector", "sync", "gpsimd")

_UID = [0]


def _u(n):
    _UID[0] += 1
    return f"{n}_{_UID[0]}"


def host_consts():
    ident = np.eye(128).astype(ml_dtypes.bfloat16)
    tri = np.triu(np.ones((128, 128))).astype(ml_dtypes.bfloat16)
    esel = np.zeros((16, 8 * 128), dtype=ml_dtypes.bfloat16)
    for ci in range(8):
        for vl in range(2):
            for c in range(64):
                esel[2 * ci + vl, ci * 128 + vl * 64 + c] = 1.0
    dpat = np.zeros((128, 128), dtype=ml_dtypes.bfloat16)
    for vl in range(2):
        for il in range(2):
            for c in range(64):
                dpat[vl * 64 + c, il * 64 + c] = 1.0
    identf = np.eye(128).astype(np.float32)
    return {"cst_ident": ident, "cst_tri": tri, "cst_esel": esel,
            "cst_dpat": dpat, "cst_identf": identf}


class Prog:

    def __init__(self, nc, st: ExitStack):
        self.nc = nc
        self.st = st
        self.q = {e: [] for e in ENGS}
        self.sem = {}
        self.nblk = 0

    def get_sem(self, name):
        if name not in self.sem:
            h = self.st.enter_context(self.nc.semaphore(name))
            self.sem[name] = [h, 0]
        return self.sem[name]

    def n(self, name):
        return self.get_sem(name)[1]

    def emit(self, eng, fn, waits=(), inc=None, inc_by=1):
        winfo = [(self.get_sem(s)[0], v) for (s, v) in waits if v > 0]
        ih = None
        if inc is not None:
            rec = self.get_sem(inc)
            rec[1] += inc_by
            ih = rec[0]
        self.q[eng].append((fn, winfo, ih, inc_by))

    def dma(self, eng, dst, src, sem, waits=(), **kw):
        self.emit(eng, lambda e, d=dst, s=src: e.dma_start(d, s, **kw),
                  waits=waits, inc=sem, inc_by=16)

    def dma_tr(self, dst, src, sem, waits=()):
        self.emit("sync", lambda e, d=dst, s=src: e.dma_start_transpose(d, s),
                  waits=waits, inc=sem, inc_by=16)

    def wait(self, eng, sem_name, value=None):
        v = self.n(sem_name) if value is None else value
        if v > 0:
            self.emit(eng, None, waits=((sem_name, v),))

    def flush(self, name=None):
        self.nblk += 1
        name = name or f"b{self.nblk}"
        q = self.q
        self.q = {e: [] for e in ENGS}
        with self.nc.Block(name) as blk:
            for eng_name in ENGS:
                ops = q[eng_name]
                if not ops:
                    continue

                def runner(engine, ops=ops):
                    for fn, winfo, ih, iv in ops:
                        for (h, v) in winfo:
                            engine.wait_ge(h, v)
                        if fn is not None:
                            ins = fn(engine)
                            if ih is not None:
                                ins.then_inc(ih, iv)
                getattr(blk, eng_name)(runner)


def sb_ap(h, offset, dims):
    return bass.AP(h, offset, [[int(h.shape[1]), int(h.shape[0])]] + dims)



def emit_ln_chunk(p, x_ap, nfree, b, waits=(), dest=("rstd", "nmrs")):
    rstd_k, nmrs_k = dest
    k = (nfree + 511) // 512
    for i in range(k):
        w = tuple(waits) + ((("ln", p.n("ln")),) if i == 0 else ())
        lo, hi = i * 512, min((i + 1) * 512, nfree)
        p.emit("vector",
               lambda v, i=i, lo=lo, hi=hi: v.bn_stats(
                   b["s6"][:, i * 6:(i + 1) * 6], x_ap[:, lo:hi]),
               waits=w, inc="ln")
    p.emit("vector", lambda v, k=k: v.bn_aggr(b["s2"][:, :], b["s6"][:, 0:6 * k]),
           waits=(("ln", p.n("ln")),), inc="ln")
    p.emit("scalar", lambda s: s.activation(b[rstd_k][:, :], b["s2"][:, 1:2],
                                            AF.Ln, bias=b["eps_col"][:, :],
                                            scale=1.0),
           waits=(("ln", p.n("ln")),), inc="ln")
    p.emit("scalar", lambda s: s.activation(b[rstd_k][:, :], b[rstd_k][:, :],
                                            AF.Exp, scale=-0.5),
           waits=(("ln", p.n("ln")),), inc="ln")
    p.emit("vector", lambda v: v.scalar_tensor_tensor(
        b[nmrs_k][:, :], b["s2"][:, 0:1], -1.0, b[rstd_k][:, :],
        ALU.mult, ALU.mult),
        waits=(("ln", p.n("ln")),), inc="ln")


def emit_row_bcast(p, out, src_dram, n, b):
    row, ones, ps = b["brow"], b["ones1"], b["ps_bc"]
    p.dma("sync", row[:, 0:n], src_dram[None, :], "dbr",
          waits=(("bmm", p.n("bmm")),))
    dlv = p.n("dbr")
    p.emit("vector", lambda v: v.memset(ones[:, :], 1.0),
           waits=(("bc", p.n("bc")),), inc="bc")
    ev0 = p.n("bc")
    for j in range(0, n, 512):
        w = min(512, n - j)
        guard = p.n("bc")
        p.emit("tensor",
               lambda t, j=j, w=w: t.matmul(ps[:, 0:w], ones[:, :],
                                            row[:, j:j + w],
                                            start=True, stop=True),
               waits=(("dbr", dlv), ("bc", guard)), inc="bmm")
        p.emit("scalar",
               lambda s, j=j, w=w: s.copy(out[:, j:j + w], ps[:, 0:w]),
               waits=(("bmm", p.n("bmm")),), inc="bc")


def emit_tiles_to_fmajor(p, dst, T, cd_range, ct_range, src_col, b, waits=(),
                         gb=None):
    tiles = [(ct, cd) for ct in ct_range for cd in cd_range]
    first = True
    tev0 = p.n("tev")
    ngrp = (len(tiles) + 3) // 4
    for g in range(ngrp):
        grp = tiles[4 * g:4 * g + 4]
        half = (g % 2) * 1024
        w0 = [("tev", tev0 + 4 * (g - 1))] if g >= 2 else []
        for j, (ct, cd) in enumerate(grp):
            w = list(w0) if j == 0 else []
            if first:
                w += list(waits)
                first = False
            p.emit("tensor",
                   lambda t, ct=ct, cd=cd, j=j, half=half: t.transpose(
                       b["psT"][:, half + j * 128: half + (j + 1) * 128],
                       src_col(ct, cd), b["ident"][:, :]),
                   waits=w, inc="tmm")
            w0 = []
        mmv = p.n("tmm")
        for j, (ct, cd) in enumerate(grp):
            if gb is None:
                p.emit("vector",
                       lambda v, ct=ct, cd=cd, j=j, half=half: v.tensor_copy(
                           dst[:, cd * T + ct * 128: cd * T + (ct + 1) * 128],
                           b["psT"][:, half + j * 128: half + (j + 1) * 128]),
                       waits=((("tmm", mmv),) if j == 0 else ()), inc="tev")
            else:
                gc, bc = gb
                p.emit("vector",
                       lambda v, ct=ct, cd=cd, j=j, half=half: v.tensor_scalar(
                           dst[:, cd * T + ct * 128: cd * T + (ct + 1) * 128],
                           b["psT"][:, half + j * 128: half + (j + 1) * 128],
                           gc[:, cd:cd + 1], bc[:, cd:cd + 1],
                           ALU.mult, ALU.add),
                       waits=((("tmm", mmv),) if j == 0 else ()), inc="tev")


def emit_load_xF(p, x_dram, xF, T, b):
    nt = T // 128
    xc0, tmm0 = p.n("xc"), p.n("tmm")
    tev0 = p.n("tev")
    for ct in range(nt):
        sf, sb = b["stage_f"][ct % 2], b["stage_b"][ct % 2]
        dxs = f"dx{ct % 2}"
        p.dma("sync", sf[:, :], x_dram[ct * 128:(ct + 1) * 128, :], dxs,
              waits=(("xc", max(xc0, p.n("xc") - 1)),))
        p.emit("vector", lambda v, sf=sf, sb=sb: v.tensor_copy(sb[:, :], sf[:, :]),
               waits=((dxs, p.n(dxs)), ("tmm", tmm0 + 8 * max(0, ct - 1))),
               inc="xc")
        emit_tiles_to_fmajor(
            p, xF, T, range(NKC), [ct],
            lambda ct2, cd, sb=sb: sb[:, cd * 128:(cd + 1) * 128],
            b, waits=(("xc", p.n("xc")),))



def emit_load_xF_fast(p, xbf_dram, xF, T, sem="dxt", group_waits=None):
    for j in range(T // 512):
        gw = list(group_waits(j)) if group_waits is not None else []
        for kc in range(NKC):
            p.dma_tr(xF[:, kc * T + j * 512: kc * T + (j + 1) * 512],
                     xbf_dram[j * 512:(j + 1) * 512,
                              kc * 128:(kc + 1) * 128], sem,
                     waits=gw if kc == 0 else ())


def emit_attention(p, cfg, b):
    emit_attention_v(p, cfg, b)
    emit_attention_pairs(p, cfg, b)


def emit_attention_v(p, cfg, b):
    T, causal = cfg["T"], cfg["causal"]
    nt, nq = T // 128, T // 512
    xF = cfg["xF"]
    wv = cfg["wv_sb"]
    bvr = cfg.get("bv_row")
    wready = tuple(cfg.get("qkv_ready", ()))
    xf_ready = tuple(cfg.get("xf_ready", ()))
    v65 = b["v65"]
    pm = b["ps_main"]

    p.emit("vector", lambda v: v.memset(
        sb_ap(v65, 64, [[1040, nt], [65, 16]]), 1.0), inc="vev")
    xft = cfg.get("xf_tev")
    vev0 = p.n("vev")
    for tt in range(nt):
        ps_v = pm[:, (tt % 2) * 1024:(tt % 2) * 1024 + 1024]
        guard = max(vev0, p.n("vev") - 1)
        for kc in range(NKC):
            w = [("vev", guard)] if kc == 0 else []
            if kc == 0 and xft is not None:
                w.append(("tev", xft + 8 * (tt + 1)))
            if kc == 0 and cfg.get("xf_dxt0") is not None:
                w.append(("dxt", cfg["xf_dxt0"] + 128 * (tt // 4 + 1)))
            if tt == 0 and kc == 0:
                w += list(wready) + list(xf_ready)
            p.emit("tensor",
                   lambda t, tt=tt, kc=kc, ps_v=ps_v: t.matmul(
                       ps_v[:, 0:512],
                       xF[:, kc * T + tt * 128: kc * T + (tt + 1) * 128],
                       wv[:, kc * 1024: kc * 1024 + 512],
                       start=(kc == 0), stop=(kc == NKC - 1)),
                   waits=w)
            p.emit("tensor",
                   lambda t, tt=tt, kc=kc, ps_v=ps_v: t.matmul(
                       ps_v[:, 512:1024],
                       xF[:, kc * T + tt * 128: kc * T + (tt + 1) * 128],
                       wv[:, kc * 1024 + 512: (kc + 1) * 1024],
                       start=(kc == 0), stop=(kc == NKC - 1)),
                   inc="amm" if kc == NKC - 1 else None)
        mmv = p.n("amm")
        dst = sb_ap(v65, tt * 1040, [[65, 16], [1, 64]])
        if bvr is not None:
            bvw = tuple(cfg.get("bv_ready", ())) if tt == 0 else ()
            p.emit("vector",
                   lambda v, dst=dst, ps_v=ps_v: v.scalar_tensor_tensor(
                       dst, ps_v[:, 0:1024], 1.0, bvr[:, 0:1024],
                       ALU.mult, ALU.add),
                   waits=(("amm", mmv),) + bvw, inc="vev")
        else:
            p.emit("vector",
                   lambda v, dst=dst, ps_v=ps_v: v.tensor_copy(
                       dst, ps_v[:, 0:1024]),
                   waits=(("amm", mmv),), inc="vev")


def emit_attention_pairs(p, cfg, b):
    T, causal = cfg["T"], cfg["causal"]
    nt, nq = T // 128, T // 512
    xF, z_tm = cfg["xF"], cfg["z_tm"]
    wk, wq = cfg["wk_sb"], cfg["wq_sb"]
    bqc, bkc = cfg.get("bq_cols"), cfg.get("bk_cols")
    tri = cfg["tri"]
    wready = tuple(cfg.get("qkv_ready", ()))
    xf_ready = tuple(cfg.get("xf_ready", ()))
    v65, kq = b["v65"], b["kq"]
    est = b["est"]
    pm = b["ps_main"]
    ps_qk = pm[:, 0:min(T, 2048)]
    ps_zfA, ps_zfB = b["ps_zfA"], b["ps_zfB"]
    psT = b["psT"]
    zfsb = b["zfsb"]
    ident = b["ident"]
    kbar, rcol = b["kbar"], b["rcol"]
    v_done = p.n("vev")

    stmm_prev = 0
    zev_prev = 0
    ksum = b["ksum"]
    pend_evac = [None]
    for hp in range(NPAIR):
        cur = kq[hp % len(kq)]
        for which, wmat, bcol, off in (("k", wk, bkc, 0), ("q", wq, bqc, T)):
            guard = p.n("qkev")
            xft = cfg.get("xf_tev")
            for kc in range(NKC):
                for J in range(nq):
                    w = []
                    if kc == 0 and J == 0:
                        w = [("qkev", guard), ("amm", stmm_prev),
                             ("eev", p.n("eev")), ("vev", v_done)]
                        if hp == 0 and which == "k":
                            w += list(wready) + list(xf_ready)
                    if kc == 0 and hp == 0 and xft is not None:
                        w.append(("tev", xft + 8 * min(nt, 4 * (J + 1))))
                    if kc == 0 and hp == 0 and cfg.get("xf_dxt0") is not None:
                        w.append(("dxt", cfg["xf_dxt0"] + 128 * (J + 1)))
                    p.emit("tensor",
                           lambda t, kc=kc, J=J, hp=hp, wmat=wmat: t.matmul(
                               ps_qk[:, J * 512:(J + 1) * 512],
                               wmat[:, kc * 1024 + hp * 128: kc * 1024 + hp * 128 + 128],
                               xF[:, kc * T + J * 512: kc * T + (J + 1) * 512],
                               start=(kc == 0), stop=(kc == NKC - 1)),
                           waits=w,
                           inc="amm" if (kc == NKC - 1 and J == nq - 1) else None)
            mmv = p.n("amm")
            for J in range(nq):
                acc_k = ksum[:, J:J + 1] if which == "k" else None
                ew = (("amm", mmv),) if J == 0 else ()
                if which == "k" and J == 0:
                    ew = ew + (("ctr", p.n("ctr")),)
                if bcol is not None:
                    p.emit("scalar",
                           lambda s, J=J, off=off, hp=hp, bcol=bcol, cur=cur,
                           acc_k=acc_k: s.activation(
                               cur[:, off + J * 512: off + (J + 1) * 512],
                               ps_qk[:, J * 512:(J + 1) * 512], AF.Identity,
                               bias=bcol[:, hp:hp + 1], scale=1.0,
                               accum_out=acc_k),
                           waits=ew, inc="qkev")
                else:
                    p.emit("scalar",
                           lambda s, J=J, off=off, cur=cur, acc_k=acc_k:
                           s.activation(
                               cur[:, off + J * 512: off + (J + 1) * 512],
                               ps_qk[:, J * 512:(J + 1) * 512], AF.Identity,
                               scale=1.0, accum_out=acc_k),
                           waits=ew, inc="qkev")
            if which == "k":
                qkv_k = p.n("qkev")
                p.emit("vector", lambda v: v.reduce_sum(
                    kbar[:, 0:1], ksum[:, 0:nq], axis=AXX),
                    waits=(("qkev", qkv_k), ("ctr", p.n("ctr"))), inc="ctr")
                p.emit("vector", lambda v: v.tensor_scalar_mul(
                    kbar[:, 0:1], kbar[:, 0:1], -1.0 / T),
                    waits=(("ctr", p.n("ctr")),), inc="ctr")
                p.emit("vector", lambda v, cur=cur: v.tensor_scalar(
                    cur[:, 0:T], cur[:, 0:T], kbar[:, 0:1], None, ALU.add),
                    waits=(("ctr", p.n("ctr")),), inc="ctr")
        kready = p.n("ctr")
        qkv = p.n("qkev")

        for J in range(nq):
            plist = list(range(min(nt, (J + 1) * 4))) if causal                 else list(range(nt))
            npl = len(plist)
            exp_after = {}
            ready_after = {}
            zmm_after = {}

            def emit_st(idx):
                P = plist[idx]
                slot = idx % 2
                w = [("ctr", kready)]
                if idx == 0:
                    w.append(("qkev", qkv))
                if idx >= 2:
                    w.append(("eev", exp_after[idx - 2]))
                for (base, tp, hb) in ((0, (0, 0), 0), (64, (64, 0), 1)):
                    p.emit("tensor",
                           lambda t, P=P, J=J, slot=slot, base=base, tp=tp,
                           hb=hb, cur=cur: t.matmul(
                               pm[:, slot * 1024 + hb * 512:
                                  slot * 1024 + (hb + 1) * 512],
                               cur[base:base + 64, P * 128:(P + 1) * 128],
                               cur[base:base + 64, T + J * 512: T + (J + 1) * 512],
                               start=True, stop=True, tile_position=tp),
                           waits=w, inc="amm")
                    w = []

            def emit_exp(idx):
                P = plist[idx]
                slot, ring = idx % 2, idx % 4
                we = [("amm", p.n("amm"))]
                if idx >= 4:
                    we.append(("zmm", zmm_after[idx - 4]))
                p.emit("scalar",
                       lambda s, ring=ring, slot=slot: s.activation(
                           est[:, ring * 1024:(ring + 1) * 1024],
                           pm[:, slot * 1024:(slot + 1) * 1024],
                           AF.Exp, scale=0.125),
                       waits=we, inc="eev")
                exp_after[idx] = p.n("eev")
                rd = [("eev", exp_after[idx])]
                if causal and 4 * J <= P:
                    off = P * 128 - J * 512
                    for hb in (0, 1):
                        base = ring * 1024 + hb * 512
                        if off > 0:
                            p.emit("vector",
                                   lambda v, base=base, off=off: v.memset(
                                       est[:, base: base + off], 0.0),
                                   waits=(("eev", exp_after[idx]),),
                                   inc="bev")
                        p.emit("vector",
                               lambda v, base=base, off=off: v.tensor_tensor(
                                   est[:, base + off: base + off + 128],
                                   est[:, base + off: base + off + 128],
                                   tri[:, :], ALU.mult),
                               waits=(("eev", exp_after[idx]),), inc="bev")
                    rd.append(("bev", p.n("bev")))
                ready_after[idx] = rd

            def emit_z(idx):
                P = plist[idx]
                ring = idx % 4
                wz = list(ready_after[idx])
                if idx == 0:
                    wz += [("zfev", zev_prev), ("vev", v_done)]
                last = idx == npl - 1
                for hb, ps_zf in ((0, ps_zfA), (1, ps_zfB)):
                    p.emit("tensor",
                           lambda t, ps_zf=ps_zf, ring=ring, P=P, hp=hp,
                           hb=hb, idx=idx, last=last: t.matmul(
                               ps_zf[0:65, 0:512],
                               sb_ap(v65, P * 1040 + (2 * hp + hb) * 65,
                                     [[1, 65]]),
                               est[:, ring * 1024 + hb * 512:
                                   ring * 1024 + (hb + 1) * 512],
                               start=(idx == 0), stop=last),
                           waits=wz, inc="zmm")
                    wz = []
                zmm_after[idx] = p.n("zmm")

            for idx in range(npl):
                emit_st(idx)
                emit_exp(idx)
                if idx == 1 and pend_evac[0] is not None:
                    pend_evac[0]()
                    pend_evac[0] = None
                if idx > 1:
                    emit_z(idx - 2)
            if npl > 1:
                emit_z(npl - 2)
            emit_z(npl - 1)

            zmv = p.n("zmm")
            for hb, ps_zf in ((0, ps_zfA), (1, ps_zfB)):
                p.emit("vector",
                       lambda v, hb=hb, ps_zf=ps_zf: v.tensor_copy(
                           zfsb[0:65, hb * 512:(hb + 1) * 512],
                           ps_zf[0:65, 0:512]),
                       waits=(("zmm", zmv), ("ztr", p.n("ztr"))), inc="zfev")
            zev_prev = p.n("zfev")

            def evac(J=J, hp=hp, zfv=zev_prev):
                ztr0 = p.n("zev")
                for m in range(4):
                    for hb in (0, 1):
                        slot = 2 * m + hb
                        p.emit("tensor",
                               lambda t, m=m, hb=hb, slot=slot: t.transpose(
                                   psT[:, slot * 68: slot * 68 + 65],
                                   zfsb[0:65, hb * 512 + m * 128:
                                        hb * 512 + (m + 1) * 128],
                                   ident[0:65, 0:65]),
                               waits=(("zfev", zfv), ("zev", ztr0)), inc="ztr")
                ztv = p.n("ztr")
                for m in range(4):
                    ct = J * 4 + m
                    for hb in (0, 1):
                        head = 2 * hp + hb
                        slot = 2 * m + hb
                        p.emit("vector",
                               lambda v, slot=slot: v.reciprocal(
                                   rcol[:, slot:slot + 1],
                                   psT[:, slot * 68 + 64: slot * 68 + 65]),
                               waits=(("ztr", ztv),), inc="zev")
                        p.emit("vector",
                               lambda v, slot=slot, ct=ct, head=head:
                               v.tensor_scalar(
                                   z_tm[:, ct * 1024 + head * 64:
                                        ct * 1024 + (head + 1) * 64],
                                   psT[:, slot * 68: slot * 68 + 64],
                                   rcol[:, slot:slot + 1], None, ALU.mult),
                               waits=(("zev", p.n("zev")),), inc="zev")
            pend_evac[0] = evac
        stmm_prev = p.n("amm")
    if pend_evac[0] is not None:
        pend_evac[0]()
        pend_evac[0] = None


def alloc_attn_bufs(nc, st, T):
    nt = T // 128
    sb = lambda n, shp, dt: st.enter_context(nc.sbuf_tensor(_u(n), shp, dt))
    ps = lambda n, shp, dt=F32: st.enter_context(nc.psum_tensor(_u(n), shp, dt))
    return {
        "v65": sb("at_v65", [128, nt * 1040], BF16),
        "kq": [sb("at_kq0", [128, 2 * T], BF16)],
        "est": sb("at_est", [128, 4 * 1024], BF16),
        "kbar": sb("at_kbar", [128, 2], F32),
        "ksum": sb("at_ksum", [128, 16], F32),
        "rcol": sb("at_rcol", [128, 8], F32),
        "ps_main": ps("at_psmain", [128, 2048]),
        "ps_zfA": ps("at_pszfA", [128, 512]),
        "ps_zfB": ps("at_pszfB", [128, 512]),
        "zfsb": sb("at_zfsb", [128, 1024], BF16),
    }


def emit_resid_ln_out(p, x_dram, z_tm, out_dram, T, b, zscale=1.0,
                      trailing=True):
    nt = T // 128
    uadd0 = p.n("uadd")
    xcv = p.n("xc")
    for ct in range(nt):
        sf = b["stage_f"][ct % 2]
        drs, dws = f"dr{ct % 2}", f"dw{ct % 2}"
        dwp = f"dw{(ct - 1) % 2}"
        w = [("uadd", max(uadd0, p.n("uadd") - 1))]
        if ct < 2:
            w.append(("xc", xcv))
        p.dma("sync", sf[:, :], x_dram[ct * 128:(ct + 1) * 128, :], drs,
              waits=w)
        p.emit("vector",
               lambda v, sf=sf, ct=ct: v.tensor_tensor(
                   b["ubuf"][:, :], sf[:, :],
                   z_tm[:, ct * 1024:(ct + 1) * 1024], ALU.add),
               waits=((drs, p.n(drs)), (dwp, p.n(dwp)),
                      ("zev", p.n("zev"))),
               inc="uadd")
        emit_ln_chunk(p, b["ubuf"], 1024, b, waits=(("uadd", p.n("uadd")),))
        p.emit("scalar",
               lambda s: s.activation(b["ubuf"][:, :], b["ubuf"][:, :],
                                      AF.Identity, bias=b["nmrs"][:, :],
                                      scale=b["rstd"][:, :]),
               waits=(("ln", p.n("ln")),), inc="uout")
        p.dma("sync", out_dram[ct * 128:(ct + 1) * 128, :], b["ubuf"][:, :],
              dws, waits=(("uout", p.n("uout")),))
    if trailing:
        p.wait("sync", "dw0")
        p.wait("sync", "dw1")


def dw_chunk_waits(p, ndw0, ndw1):
    def f(ct):
        n = (ndw0, ndw1)[ct % 2]
        return [(f"dw{ct % 2}", n - 16 * (7 - ct // 2))]
    return f


def alloc_common_bufs(nc, st):
    sb = lambda n, shp, dt: st.enter_context(nc.sbuf_tensor(n, shp, dt))
    ps = lambda n, shp, dt=F32: st.enter_context(nc.psum_tensor(n, shp, dt))
    return {
        "stage_f": [sb("c_stf0", [128, 1024], F32),
                    sb("c_stf1", [128, 1024], F32)],
        "stage_b": [sb("c_stb0", [128, 1024], BF16),
                    sb("c_stb1", [128, 1024], BF16)],
        "ubuf": sb("c_ubuf", [128, 1024], F32),

        "s6": sb("c_s6", [128, 48], F32),
        "s2": sb("c_s2", [128, 2], F32),
        "rstd": sb("c_rstd", [128, 1], F32),
        "nmrs": sb("c_nmrs", [128, 1], F32),
        "rstd2": sb("c_rstd2", [128, 1], F32),
        "nmrs2": sb("c_nmrs2", [128, 1], F32),
        "eps_col": sb("c_eps", [128, 1], F32),
        "brow": sb("c_brow", [1, 1024], F32),
        "ones1": sb("c_ones1", [1, 128], F32),
        "_ident_sb": sb("c_ident", [128, 128], BF16),
        "_identf_sb": sb("c_identf", [128, 128], F32),
        "_colrow": sb("c_colrow", [32, 128], F32),
        "_tri_sb": sb("c_tri", [128, 128], BF16),
        "psT": ps("c_psT", [128, 2048], BF16),
        "xfb": sb("c_xfb", [128, 8 * 2048], BF16),
    }


def emit_load_consts(p, cb, aps):
    cb["ident"] = cb["_ident_sb"]
    p.dma("sync", cb["_ident_sb"][:, :], aps["cst_ident"][:, :], "dl")
    p.dma("sync", cb["_tri_sb"][:, :], aps["cst_tri"][:, :], "dl")
    p.emit("vector", lambda v: v.memset(cb["eps_col"][:, :], LN_EPS), inc="ev")
    p.wait("vector", "dl")
    p.wait("tensor", "dl")
    p.wait("scalar", "ev")


def load_weight_slab(p, dst, w_dram, cast, sem=None, waits=()):
    n = int(w_dram.shape[1])
    eng = "gpsimd" if cast else "sync"
    if sem is None:
        sem = "dwg" if cast else "dwt"
    for kc in range(NKC):
        p.dma(eng, dst[:, kc * n:(kc + 1) * n],
              w_dram[kc * 128:(kc + 1) * 128, :], sem,
              waits=waits if kc == 0 else ())
    return sem



def alloc_p0_bufs(nc, st):
    sb = lambda n, shp, dt: st.enter_context(nc.sbuf_tensor(_u(n), shp, dt))
    ps = lambda n, shp, dt=F32: st.enter_context(nc.psum_tensor(_u(n), shp, dt))
    return {
        "w16": sb("p0_w16", [16, 48], F32),
        "m16": sb("p0_m16", [16, 4], F32),
        "pstage": sb("p0_pstage", [16, 16], BF16),
        "probsT": sb("p0_probsT", [16, 48], BF16),
        "pbc": sb("p0_pbc", [16, 128], BF16),
        "esel": sb("p0_esel", [16, 8 * 128], BF16),
        "pco": sb("p0_pco", [128, 8 * 128], BF16),
        "dpat": sb("p0_dpat", [128, 128], BF16),
        "slab": sb("p0_slab", [128, 3072], BF16),
        "ps16": ps("p0_ps16", [16, 16], BF16),
        "psA": ps("p0_psA", [128, 128]),
    }


def emit_phase0_l0(p, aps, b, ident):
    p.dma("sync", b["esel"][:, :], aps["cst_esel"][:, :], "dp0")
    p.dma("sync", b["dpat"][:, :], aps["cst_dpat"][:, :], "dp0")
    for wi, wname in enumerate(("L0_attn_W_K", "L0_attn_W_Q", "L0_attn_W_V")):
        p.dma("sync", b["w16"][:, wi * 16:(wi + 1) * 16], aps[wname][:, :],
              "dp0")
    wp = aps["L0_attn_W_pred"].tensor
    for co in range(8):
        for vl in range(2):
            src = bass.AP(wp, (2 * co) * 4096, [[64, 64], [4096, 2], [1, 64]])
            p.emit("gpsimd",
                   lambda e, co=co, vl=vl, src=src: e.dma_start(
                       b["pco"][vl * 64:(vl + 1) * 64, co * 128:(co + 1) * 128],
                       src),
                   inc="dp0g", inc_by=16)
    dl0 = p.n("dp0")
    dg0 = p.n("dp0g")

    for wi in range(3):
        sl = b["w16"][:, wi * 16:(wi + 1) * 16]
        p.emit("vector", lambda v, sl=sl: v.reduce_max(
            b["m16"][:, 0:1], sl, axis=AXX, negate=True),
            waits=(("dp0", dl0), ("p0", p.n("p0"))), inc="p0")
        p.emit("scalar", lambda s, sl=sl: s.activation(
            sl, sl, AF.Exp, bias=b["m16"][:, 0:1], scale=1.0,
            accum_out=b["m16"][:, 1:2]),
            waits=(("p0", p.n("p0")),), inc="p0")
        p.emit("vector", lambda v: v.reciprocal(b["m16"][:, 2:3],
                                                b["m16"][:, 1:2]),
               waits=(("p0", p.n("p0")),), inc="p0")
        p.emit("vector", lambda v, sl=sl: v.tensor_scalar(
            sl, sl, b["m16"][:, 2:3], None, ALU.mult),
            waits=(("p0", p.n("p0")),), inc="p0")
        p.emit("vector", lambda v, sl=sl: v.tensor_copy(b["pstage"][:, :], sl),
               waits=(("p0", p.n("p0")), ("p0mm", p.n("p0mm"))), inc="p0")
        p.emit("tensor", lambda t: t.transpose(
            b["ps16"][:, :], b["pstage"][:, :], ident[0:16, 0:16]),
            waits=(("p0", p.n("p0")),), inc="p0mm")
        p.emit("vector", lambda v, wi=wi: v.tensor_copy(
            b["probsT"][:, wi * 16:(wi + 1) * 16], b["ps16"][:, :]),
            waits=(("p0mm", p.n("p0mm")),), inc="p0")

    colbase = {0: 1024, 1: 0, 2: 2048}
    for ci in range(8):
        slab_guard = p.n("dp0s")
        for wi in range(3):
            for co in range(8):
                pbv = p.n("p0")
                src = bass.AP(b["probsT"], wi * 16 + 2 * co,
                              [[48, 16], [1, 2], [0, 64]])
                p.emit("vector", lambda v, src=src: v.tensor_copy(
                    b["pbc"][:, :], src),
                    waits=(("p0", pbv), ("p0mm", p.n("p0mm"))), inc="p0")
                p.emit("tensor", lambda t, ci=ci: t.matmul(
                    b["psA"][:, :], b["esel"][:, ci * 128:(ci + 1) * 128],
                    b["pbc"][:, :], start=True, stop=True),
                    waits=(("p0", p.n("p0")),), inc="p0mm")
                pat = b["pco"][:, co * 128:(co + 1) * 128] if wi == 1                     else b["dpat"][:, :]
                patw = (("dp0g", dg0),) if wi == 1 else (("dp0", dl0),)
                p.emit("vector",
                       lambda v, ci=ci, co=co, wi=wi, pat=pat: v.tensor_tensor(
                           b["slab"][:, colbase[wi] + co * 128:
                                     colbase[wi] + (co + 1) * 128],
                           b["psA"][:, :], pat, ALU.mult),
                       waits=(("p0mm", p.n("p0mm")), ("dp0s", slab_guard))
                       + patw, inc="p0")
        p.dma("sync", aps["l0qkv"][ci * 128:(ci + 1) * 128, :], b["slab"][:, :],
              "dp0s", waits=(("p0", p.n("p0")),))
    p.wait("sync", "dp0s")


def emit_load_ln_hF(p, x_dram, hF, T, b, g_cols, b_cols, x_waits=None,
                    first_guard=()):
    nt = T // 128
    xc0, tmm0 = p.n("xc"), p.n("tmm")
    tev0 = p.n("tev")
    for ct in range(nt):
        sf, sb = b["stage_f"][ct % 2], b["stage_b"][ct % 2]
        dxs = f"dx{ct % 2}"
        w = [("xc", max(xc0, p.n("xc") - 1))]
        if x_waits is not None:
            w += list(x_waits(ct))
        p.dma("sync", sf[:, :], x_dram[ct * 128:(ct + 1) * 128, :], dxs,
              waits=w)
        emit_ln_chunk(p, sf, 1024, b, waits=((dxs, p.n(dxs)),))
        p.emit("scalar",
               lambda s, sf=sf, sb=sb: s.activation(
                   sb[:, :], sf[:, :], AF.Identity, bias=b["nmrs"][:, :],
                   scale=b["rstd"][:, :]),
               waits=(("ln", p.n("ln")), ("tmm", tmm0 + 8 * max(0, ct - 1))),
               inc="xc")
        fg = tuple(first_guard) if ct == 0 else ()
        emit_tiles_to_fmajor(
            p, hF, T, range(NKC), [ct],
            lambda ct2, cd, sb=sb: sb[:, cd * 128:(cd + 1) * 128],
            b, waits=(("xc", p.n("xc")),) + fg, gb=(g_cols, b_cols))
    return tev0


def emit_zf_transpose(p, z_tm, zF, T, b, guard=()):
    nt = T // 128
    emit_tiles_to_fmajor(
        p, zF, T, range(NKC), range(nt),
        lambda ct, cd: z_tm[:, ct * 1024 + cd * 128: ct * 1024 + (cd + 1) * 128],
        b, waits=tuple(guard))


def emit_proj_resid(p, zF, w_sb, bp_row, x_dram, out_dram, T, ps, b,
                    final_ln, w_ready=(), xfb=None, x_waits=None,
                    trailing=True):
    nt = T // 128
    uadd0 = p.n("uadd")
    tevv = p.n("tev")
    xcv = p.n("xc")
    tmm0 = p.n("tmm")
    tev0 = p.n("tev")
    zev_tail = p.n("zev")
    ubf_of = {}

    def hand(hct):
        sbb = b["stage_b"][hct % 2]
        tw = (("ubf", ubf_of[hct]),)
        if hct == 0:
            tw = tw + (("zev", zev_tail),)
        else:
            tw = tw + (("tev", tev0 + 8 * hct),)
        emit_tiles_to_fmajor(
            p, xfb, T, range(NKC), [hct],
            lambda ct2, cd, sbb=sbb: sbb[:, cd * 128:(cd + 1) * 128],
            b, waits=tw)
    for ct in range(nt):
        sf = b["stage_f"][ct % 2]
        drs, dws = f"dr{ct % 2}", f"dw{ct % 2}"
        dwp = f"dw{(ct - 1) % 2}"
        w = [("uadd", max(uadd0, p.n("uadd") - 1))]
        if ct < 2:
            w.append(("xc", xcv))
        if x_waits is not None:
            w += list(x_waits(ct))
        p.dma("sync", sf[:, :], x_dram[ct * 128:(ct + 1) * 128, :], drs,
              waits=w)
        guard = p.n("uadd")
        for kc in range(NKC):
            ww = []
            if kc == 0:
                ww = [("uadd", max(uadd0, guard)), ("tev", tevv)] + list(w_ready)
            p.emit("tensor",
                   lambda t, ct=ct, kc=kc: t.matmul(
                       ps[:, 0:512],
                       zF[:, kc * T + ct * 128: kc * T + (ct + 1) * 128],
                       w_sb[:, kc * 1024: kc * 1024 + 512],
                       start=(kc == 0), stop=(kc == NKC - 1)),
                   waits=ww)
            p.emit("tensor",
                   lambda t, ct=ct, kc=kc: t.matmul(
                       ps[:, 512:1024],
                       zF[:, kc * T + ct * 128: kc * T + (ct + 1) * 128],
                       w_sb[:, kc * 1024 + 512: (kc + 1) * 1024],
                       start=(kc == 0), stop=(kc == NKC - 1)),
                   inc="pmm" if kc == NKC - 1 else None)
        mmv = p.n("pmm")
        w2 = (("bc", p.n("bc")),) if ct == 0 else ()
        if xfb is not None:
            w2 = w2 + (("ubf", p.n("ubf")),)
        p.emit("vector",
               lambda v: v.scalar_tensor_tensor(
                   b["ubuf"][:, :], ps[:, 0:1024], 1.0, bp_row[:, 0:1024],
                   ALU.mult, ALU.add),
               waits=(("pmm", mmv), (dwp, p.n(dwp))) + w2, inc="uadd")
        p.emit("vector",
               lambda v, sf=sf: v.tensor_tensor(
                   b["ubuf"][:, :], b["ubuf"][:, :], sf[:, :], ALU.add),
               waits=((drs, p.n(drs)), ("uadd", p.n("uadd"))), inc="uadd")
        if final_ln:
            emit_ln_chunk(p, b["ubuf"], 1024, b, waits=(("uadd", p.n("uadd")),))
            p.emit("scalar",
                   lambda s: s.activation(b["ubuf"][:, :], b["ubuf"][:, :],
                                          AF.Identity, bias=b["nmrs"][:, :],
                                          scale=b["rstd"][:, :]),
                   waits=(("ln", p.n("ln")),), inc="uout")
            p.dma("sync", out_dram[ct * 128:(ct + 1) * 128, :],
                  b["ubuf"][:, :], dws, waits=(("uout", p.n("uout")),))
            if xfb is not None:
                sbb = b["stage_b"][ct % 2]
                p.emit("vector",
                       lambda v, sbb=sbb: v.tensor_copy(sbb[:, :],
                                                        b["ubuf"][:, :]),
                       waits=(("uout", p.n("uout")),
                              ("tmm", tmm0 + 8 * max(0, ct - 1))),
                       inc="ubf")
                ubf_of[ct] = p.n("ubf")
                if ct >= 1:
                    hand(ct - 1)
                if ct == nt - 1:
                    hand(ct)
        else:
            p.dma("sync", out_dram[ct * 128:(ct + 1) * 128, :],
                  b["ubuf"][:, :], dws, waits=(("uadd", p.n("uadd")),))
    if trailing:
        p.wait("sync", "dw0")
        p.wait("sync", "dw1")
    return tev0


def emit_col_load(p, dst_cols, src_dram, cb=None, sem="dbr"):
    n = int(src_dram.shape[0]) // 128
    if cb is None:
        src2 = src_dram.rearrange("(c p) -> p c", p=128)
        p.emit("sync", lambda e: e.dma_start(dst_cols[:, 0:n], src2,
                                             allow_slow_non_contiguous=True),
               inc=sem, inc_by=16)
        return
    cr, idf, pscl = cb["_colrow"], cb["_identf_sb"], cb["ps_cl"]
    src2 = src_dram.rearrange("(c p) -> c p", p=128)
    p.dma("sync", cr[0:n, :], src2, "dcl",
          waits=(("clm", p.n("clm")),))
    p.emit("tensor", lambda t, n=n: t.transpose(
        pscl[:, 0:n], cr[0:n, :], idf[0:n, 0:n]),
        waits=(("dcl", p.n("dcl")), ("clv", p.n("clv"))), inc="clm")
    p.emit("vector", lambda v, n=n: v.tensor_copy(dst_cols[:, 0:n],
                                                  pscl[:, 0:n]),
           waits=(("clm", p.n("clm")),), inc="clv")



def alloc_moe_bufs(nc, st):
    sb = lambda n, shp, dt: st.enter_context(nc.sbuf_tensor(_u(n), shp, dt))
    ps = lambda n, shp, dt=F32: st.enter_context(nc.psum_tensor(_u(n), shp, dt))
    return {
        "w1": sb("mo_w1", [128, 8 * 4096], BF16),
        "w2": sb("mo_w2", [128, 32 * 1024], BF16),
        "mid": sb("mo_mid", [128, 32 * 256], BF16),
        "gw": sb("mo_gw", [128, 8 * 8], BF16),
        "glog": sb("mo_glog", [128, 128], F32),
        "wcols": sb("mo_wcols", [128, 128], F32),
        "gsc": sb("mo_gsc", [128, 12], F32),
        "wb8": sb("mo_wb8", [128, 8], BF16),
        "wT": sb("mo_wT", [8, 128], BF16),
        "b2m": sb("mo_b2m", [8, 1024], BF16),
        "b1a": sb("mo_b1a", [128, 8 * 32], F32),
        "gbrow": sb("mo_gbrow", [128, 8], F32),
        "ubuf2": sb("mo_ubuf2", [128, 1024], F32),
        "ps_mid": ps("mo_psmid", [128, 1024]),
        "ps_y": ps("mo_psy", [128, 2048]),
    }


def emit_moe(p, cfg, b, cb):
    T = cfg["T"]
    nt = T // 128
    n256 = T // 256
    h2F = cfg["h2F"]
    aps = cfg["aps"]
    hready = tuple(cfg.get("h2f_ready", ()))
    ps_g = b["ps_mid"]
    cache = cfg.get("w_cache")
    cmode = cfg.get("cache_mode")

    for kc in range(NKC):
        p.dma("gpsimd", b["gw"][:, kc * 8:(kc + 1) * 8],
              aps["gate_W"][kc * 128:(kc + 1) * 128, :], "dgw")
    p.dma("gpsimd", b["b2m"][:, :], aps["exp_b2"][:, :], "dgw")
    cb["ps_bc"] = b["ps_y"][:, 0:512]
    emit_row_bcast(p, b["gbrow"], aps["gate_b"], 8, cb)
    cb.pop("ps_bc")

    def load_w1(e, waits):
        if cmode == "read":
            for kc in range(NKC):
                p.dma("sync", b["w1"][:, kc * 4096:(kc + 1) * 4096],
                      cache[0][(e * 8 + kc) * 128:(e * 8 + kc + 1) * 128, :],
                      "dl1r", waits=waits if kc == 0 else ())
            return ("dl1r", p.n("dl1r"))
        if cmode == "write":
            waits = tuple(waits) + (("dc1", p.n("dc1")),)
        for kc in range(NKC):
            p.dma("gpsimd", b["w1"][:, kc * 4096:(kc + 1) * 4096],
                  bass.AP(aps["exp_W1"].tensor, e * 1024 * 4096 + kc * 128 * 4096,
                          [[4096, 128], [1, 4096]]),
                  "dlw1", waits=waits if kc == 0 else ())
        if cmode == "write":
            lv = p.n("dlw1")
            for kc in range(NKC):
                p.dma("sync", cache[0][(e * 8 + kc) * 128:(e * 8 + kc + 1) * 128, :],
                      b["w1"][:, kc * 4096:(kc + 1) * 4096], "dc1",
                      waits=(("dlw1", lv),) if kc == 0 else ())
        return ("dlw1", p.n("dlw1"))

    def load_w2(e, waits):
        if cmode == "read":
            for g in range(8):
                p.dma("sync", b["w2"][:, g * 4096:(g + 1) * 4096],
                      cache[1][(e * 8 + g) * 128:(e * 8 + g + 1) * 128, :],
                      "dl2r", waits=waits if g == 0 else ())
            return ("dl2r", p.n("dl2r"))
        if cmode == "write":
            waits = tuple(waits) + (("dc2", p.n("dc2")),)
        for g in range(8):
            src = bass.AP(aps["exp_W2"].tensor,
                          e * 4096 * 1024 + g * 512 * 1024,
                          [[1024, 128], [128 * 1024, 4], [1, 1024]])
            p.dma("gpsimd", b["w2"][:, g * 4096:(g + 1) * 4096], src,
                  "dlw2", waits=waits if g == 0 else ())
        if cmode == "write":
            lv = p.n("dlw2")
            for g in range(8):
                p.dma("sync", cache[1][(e * 8 + g) * 128:(e * 8 + g + 1) * 128, :],
                      b["w2"][:, g * 4096:(g + 1) * 4096], "dc2",
                      waits=(("dlw2", lv),) if g == 0 else ())
        return ("dlw2", p.n("dlw2"))

    def load_b1_all():
        cr, idf = cb["_colrow"], cb["_identf_sb"]
        for e in range(NE):
            src = aps["exp_b1"].tensor
            ap = bass.AP(src, e * 4096, [[128, 32], [1, 128]])
            p.dma("sync", cr[0:32, :], ap, "db1d",
                  waits=(("b1t", p.n("b1t")),))
            p.emit("tensor", lambda t: t.transpose(
                ps_g[:, 0:32], cr[0:32, :], idf[0:32, 0:32]),
                waits=(("db1d", p.n("db1d")), ("db1", p.n("db1"))),
                inc="b1t")
            p.emit("vector", lambda v, e=e: v.tensor_copy(
                b["b1a"][:, e * 32:(e + 1) * 32], ps_g[:, 0:32]),
                waits=(("b1t", p.n("b1t")),), inc="db1")

    load_b1_all()
    b1_done = p.n("db1")
    w1sem = load_w1(0, ())
    w2sem = load_w2(0, ())

    xft = cfg.get("h2f_tev")
    for tt in range(nt):
        for kc in range(NKC):
            w = []
            if tt == 0 and kc == 0:
                w = [("dgw", p.n("dgw")), ("db1", b1_done)] + list(hready)
            if kc == 0 and xft is not None:
                w.append(("tev", xft + 8 * (tt + 1)))
            p.emit("tensor",
                   lambda t, tt=tt, kc=kc: t.matmul(
                       ps_g[:, tt * 8:(tt + 1) * 8],
                       h2F[:, kc * T + tt * 128: kc * T + (tt + 1) * 128],
                       b["gw"][:, kc * 8:(kc + 1) * 8],
                       start=(kc == 0), stop=(kc == NKC - 1)),
                   waits=w, inc="gmm" if kc == NKC - 1 else None)
    gmmv = p.n("gmm")
    p.emit("vector", lambda v: v.scalar_tensor_tensor(
        b["glog"][:, 0:nt * 8], ps_g[:, 0:nt * 8], 1.0,
        sb_ap(b["gbrow"], 0, [[0, nt], [1, 8]]), ALU.mult, ALU.add),
        waits=(("gmm", gmmv), ("bc", p.n("bc"))), inc="gv")
    glog_done = p.n("gv")
    gv_tile = {}
    gs = b["gsc"]
    for tt in range(nt):
        lg = b["glog"][:, tt * 8:(tt + 1) * 8]
        wc = b["wcols"][:, tt * 8:(tt + 1) * 8]
        p.emit("vector", lambda v, lg=lg: v.reduce_max(
            gs[:, 0:1], lg, axis=AXX, negate=True),
            waits=(("gv", p.n("gv")),), inc="gv")
        p.emit("vector", lambda v, lg=lg: v.tensor_scalar(
            gs[:, 2:3], gs[:, 0:1], -1.0, None, ALU.mult),
            waits=(("gv", p.n("gv")),), inc="gv")
        p.emit("vector", lambda v, lg=lg: v.tensor_scalar(
            b["glog"][:, 120:128] if False else gs[:, 4:12], lg, gs[:, 2:3],
            None, ALU.is_equal),
            waits=(("gv", p.n("gv")),), inc="gv")
        p.emit("vector", lambda v, lg=lg: v.scalar_tensor_tensor(
            gs[:, 4:12], gs[:, 4:12], -1e30, lg, ALU.mult, ALU.add),
            waits=(("gv", p.n("gv")),), inc="gv")
        p.emit("vector", lambda v: v.reduce_max(gs[:, 1:2], gs[:, 4:12],
                                                axis=AXX),
               waits=(("gv", p.n("gv")),), inc="gv")
        p.emit("scalar", lambda s, lg=lg, wc=wc: s.activation(
            wc, lg, AF.Exp, bias=gs[:, 0:1], scale=1.0),
            waits=(("gv", p.n("gv")),), inc="gv")
        p.emit("scalar", lambda s: s.activation(
            gs[:, 3:4], gs[:, 1:2], AF.Exp, bias=gs[:, 0:1], scale=1.0),
            waits=(("gv", p.n("gv")),), inc="gv")
        p.emit("vector", lambda v: v.tensor_scalar(
            gs[:, 3:4], gs[:, 3:4], 1.0, None, ALU.add),
            waits=(("gv", p.n("gv")),), inc="gv")
        p.emit("vector", lambda v: v.reciprocal(gs[:, 3:4], gs[:, 3:4]),
               waits=(("gv", p.n("gv")),), inc="gv")
        p.emit("vector", lambda v, lg=lg: v.tensor_scalar(
            gs[:, 4:12], lg, gs[:, 1:2], None, ALU.is_ge),
            waits=(("gv", p.n("gv")),), inc="gv")
        p.emit("vector", lambda v, wc=wc: v.tensor_tensor(
            wc, wc, gs[:, 4:12], ALU.mult),
            waits=(("gv", p.n("gv")),), inc="gv")
        p.emit("vector", lambda v, wc=wc: v.tensor_scalar(
            wc, wc, gs[:, 3:4], None, ALU.mult),
            waits=(("gv", p.n("gv")),), inc="gv")
        gv_tile[tt] = p.n("gv")
    gv_done = p.n("gv")

    def emit_acc_init():
        uadd0 = p.n("uadd")
        x1w, x2w = cfg.get("x1_waits"), cfg.get("x2_waits")
        for ct in range(nt):
            sf, sf2 = cb["stage_f"][0], cb["stage_f"][1]
            w = [("uadd", max(uadd0, p.n("uadd") - 2))]
            if ct == 0:
                w.append(("xc", p.n("xc")))
            w1x = w + (list(x1w(ct)) if x1w else [])
            w2x = w + (list(x2w(ct)) if x2w else [])
            p.dma("sync", sf[:, :], cfg["x1_dram"][ct * 128:(ct + 1) * 128, :],
                  f"dr0", waits=w1x)
            p.dma("sync", sf2[:, :], cfg["x2_dram"][ct * 128:(ct + 1) * 128, :],
                  f"dr1", waits=w2x)
            p.emit("vector", lambda v, ct=ct: v.tensor_copy(
                b["wb8"][:, :], b["wcols"][:, ct * 8:(ct + 1) * 8]),
                waits=(("gv", gv_tile[ct]), ("wtm", p.n("wtm"))), inc="wte")
            p.emit("tensor", lambda t: t.transpose(
                cb["psT"][0:8, 0:128], b["wb8"][:, :], cb["ident"][:, :]),
                waits=(("wte", p.n("wte")), ("bmm2", p.n("bmm2"))), inc="wtm")
            p.emit("vector", lambda v: v.tensor_copy(
                b["wT"][:, :], cb["psT"][0:8, 0:128]),
                waits=(("wtm", p.n("wtm")),), inc="wte")
            p.emit("tensor", lambda t, ct=ct: t.matmul(
                b["ps_y"][:, 0:512], b["wT"][:, :],
                b["b2m"][:, 0:512], start=True, stop=True),
                waits=(("wte", p.n("wte")), ("dgw", p.n("dgw")),
                       ("bc", p.n("bc")), ("uadd", p.n("uadd"))))
            p.emit("tensor", lambda t, ct=ct: t.matmul(
                b["ps_y"][:, 512:1024], b["wT"][:, :],
                b["b2m"][:, 512:1024], start=True, stop=True), inc="bmm2")
            p.emit("vector",
                   lambda v, sf=sf, sf2=sf2: v.tensor_tensor(
                       cb["ubuf"][:, :], sf[:, :], sf2[:, :], ALU.add),
                   waits=(("dr0", p.n("dr0")), ("dr1", p.n("dr1")),
                          ("dwa", p.n("dwa")), ("bmm2", p.n("bmm2"))),
                   inc="uadd")
            p.emit("vector",
                   lambda v: v.tensor_tensor(cb["ubuf"][:, :], cb["ubuf"][:, :],
                                             b["ps_y"][:, 0:1024], ALU.add),
                   waits=(("uadd", p.n("uadd")),), inc="uadd")
            p.dma("sync", cfg["acc_dram"][ct * 128:(ct + 1) * 128, :],
                  cb["ubuf"][:, :], "dwa", waits=(("uadd", p.n("uadd")),))
        return p.n("dwa")


    dacc_prev = p.n("dacc")
    acc_init = None
    for e in range(NE):
        w1v, w2v = w1sem, w2sem
        mm1_first = True
        for tt in range(n256):
            for m in range(32):
                slot = (m % 2) * 512
                guard = ("gel", max(0, p.n("gel") - 1))
                for kc in range(NKC):
                    w = [guard] if kc == 0 else []
                    if mm1_first and kc == 0:
                        w += [w1v, ("db1", b1_done), ("gv", glog_done)]
                        mm1_first = False
                    if e == 0 and m == 0 and kc == 0 and xft is not None:
                        w.append(("tev", xft + 8 * min(nt, 2 * (tt + 1))))
                    p.emit("tensor",
                           lambda t, tt=tt, m=m, kc=kc, slot=slot: t.matmul(
                               b["ps_mid"][:, slot:slot + 256],
                               b["w1"][:, kc * 4096 + m * 128:
                                       kc * 4096 + (m + 1) * 128],
                               h2F[:, kc * T + tt * 256: kc * T + tt * 256 + 256],
                               start=(kc == 0), stop=(kc == NKC - 1)),
                           waits=w, inc="mm1" if kc == NKC - 1 else None)
                p.emit("scalar",
                       lambda s, m=m, slot=slot, e=e: s.activation(
                           b["mid"][:, m * 256:(m + 1) * 256],
                           b["ps_mid"][:, slot:slot + 256], AF.Gelu,
                           bias=b["b1a"][:, e * 32 + m:e * 32 + m + 1],
                           scale=1.0),
                       waits=(("mm1", p.n("mm1")), ("mm2", max(0, p.n("mm2") - 0)),), inc="gel")
            gelv = p.n("gel")
            if e == 0 and tt == 0:
                acc_init = emit_acc_init()
                cfg["_uadd_ps"] = p.n("uadd")
                cfg["_wte"] = p.n("wte")
            for ts in range(2):
                ysl = b["ps_y"][:, ts * 1024:(ts + 1) * 1024]
                for m in range(32):
                    w = []
                    if m == 0:
                        w = [("gel", gelv), ("yev", max(0, p.n("yev") - 1)),
                             w2v, ("dwa", p.n("dwa"))]
                    p.emit("tensor",
                           lambda t, ts=ts, m=m, ysl=ysl: t.matmul(
                               ysl[:, 0:512],
                               b["mid"][:, m * 256 + ts * 128:
                                        m * 256 + ts * 128 + 128],
                               b["w2"][:, m * 1024: m * 1024 + 512],
                               start=(m == 0), stop=(m == 31)),
                           waits=w)
                    p.emit("tensor",
                           lambda t, ts=ts, m=m, ysl=ysl: t.matmul(
                               ysl[:, 512:1024],
                               b["mid"][:, m * 256 + ts * 128:
                                        m * 256 + ts * 128 + 128],
                               b["w2"][:, m * 1024 + 512: (m + 1) * 1024],
                               start=(m == 0), stop=(m == 31)),
                           inc="mm2" if m == 31 else None)
                mm2v = p.n("mm2")
                tile = tt * 2 + ts
                p.emit("vector",
                       lambda v, ysl=ysl, tile=tile, e=e: v.tensor_scalar(
                           b["ubuf2"][:, :], ysl[:, 0:1024],
                           b["wcols"][:, tile * 8 + e: tile * 8 + e + 1],
                           None, ALU.mult),
                       waits=(("mm2", mm2v), ("gv", gv_tile[tile]),
                              ("dacc", p.n("dacc"))), inc="yev")
                p.emit("gpsimd",
                       lambda g, tile=tile: g.dma_start(
                           cfg["acc_dram"][tile * 128:(tile + 1) * 128, :],
                           b["ubuf2"][:, :], accum_op=ALU.add),
                       waits=(("yev", p.n("yev")), ("dwa", acc_init),
                              ("dacc", dacc_prev)),
                       inc="dacc", inc_by=16)
        dacc_prev = p.n("dacc")
        if e + 1 < NE:
            w1sem = load_w1(e + 1, (("mm1", p.n("mm1")),))
            w2sem = load_w2(e + 1, (("mm2", p.n("mm2")),))

    uadd0 = p.n("uadd")
    dacc_fin = p.n("dacc")
    xfb = cfg.get("xfb")
    tmm0 = p.n("tmm")
    tev0 = p.n("tev")
    for ct in range(nt):
        sf = cb["stage_f"][ct % 2]
        drs, dws = f"dr{ct % 2}", f"dw{ct % 2}"
        dwp = f"dw{(ct - 1) % 2}"
        w = [("uadd", max(uadd0, p.n("uadd") - 1)),
             ("dacc", dacc_fin - 16 * (nt - 1 - ct))]
        p.dma("sync", sf[:, :], cfg["acc_dram"][ct * 128:(ct + 1) * 128, :],
              drs, waits=w)
        emit_ln_chunk(p, sf, 1024, cb, waits=((drs, p.n(drs)),))
        wbf = (("ubf", p.n("ubf")),) if xfb is not None else ()
        p.emit("scalar",
               lambda s, sf=sf: s.activation(cb["ubuf"][:, :], sf[:, :],
                                             AF.Identity,
                                             bias=cb["nmrs"][:, :],
                                             scale=cb["rstd"][:, :]),
               waits=(("ln", p.n("ln")), (dwp, p.n(dwp))) + wbf, inc="uadd")
        p.dma("sync", cfg["out_dram"][ct * 128:(ct + 1) * 128, :],
              cb["ubuf"][:, :], dws, waits=(("uadd", p.n("uadd")),))
        if xfb is not None:
            sbb = cb["stage_b"][ct % 2]
            p.emit("vector",
                   lambda v, sbb=sbb: v.tensor_copy(sbb[:, :],
                                                    cb["ubuf"][:, :]),
                   waits=(("uadd", p.n("uadd")),
                          ("tmm", tmm0 + 8 * max(0, ct - 1))),
                   inc="ubf")
            emit_tiles_to_fmajor(
                p, xfb, T, range(NKC), [ct],
                lambda ct2, cd, sbb=sbb: sbb[:, cd * 128:(cd + 1) * 128],
                cb, waits=(("ubf", p.n("ubf")),))
    if cfg.get("trailing", True):
        p.wait("sync", "dw0")
        p.wait("sync", "dw1")
    if cmode == "write":
        p.wait("sync", "dc1")
        p.wait("sync", "dc2")
    return tev0


def emit_cast_to_scratch(p, src_dram, dst_dram, rows, cols, stg, sem="dsc"):
    assert rows % 128 == 0
    for i in range(rows // 128):
        g = stg[i % 2]
        p.dma("gpsimd", g[:, 0:cols], src_dram[i * 128:(i + 1) * 128, :],
              f"dcc{i % 2}", waits=((sem + str(i % 2), p.n(sem + str(i % 2))),))
        p.dma("sync", dst_dram[i * 128:(i + 1) * 128, :], g[:, 0:cols],
              sem + str(i % 2), waits=((f"dcc{i % 2}", p.n(f"dcc{i % 2}")),))
    p.wait("sync", sem + "0")
    p.wait("sync", sem + "1")


def emit_trload_slab(p, dst_sb, src_dram_bf, n_out, kc_range, sem="dtw"):
    for kc in kc_range:
        p.dma_tr(dst_sb[:, kc * n_out:(kc + 1) * n_out],
                 src_dram_bf[0:n_out, kc * 128:(kc + 1) * 128], sem)



def emit_absmean_quant(p, src_dram, dst_dram, rows, cols, b, cb, tag):
    n128 = rows // 128
    sA, sB = b["qs"]
    acc = b["qacc"]
    dsem = "dq" + tag
    p.emit("vector", lambda v: v.memset(acc[:, :], 0.0),
           waits=(("qv", p.n("qv")),), inc="qv")
    p.emit("vector", lambda v: v.memset(b["qone"][:, :], 1.0), inc="qv")
    for i in range(n128):
        g = sA if i % 2 == 0 else sB
        ds = dsem + str(i % 2)
        p.dma("sync", g[:, 0:cols], src_dram[i * 128:(i + 1) * 128, :], ds,
              waits=(("qv", max(0, p.n("qv") - 2)),))
        p.emit("vector",
               lambda v, g=g: v.reduce_sum(b["thr"][:, :], g[:, 0:cols],
                                           axis=AXX,
                                           apply_absolute_value=True),
               waits=((ds, p.n(ds)), ("qv", p.n("qv"))), inc="qv")
        p.emit("vector",
               lambda v: v.tensor_tensor(acc[:, :], acc[:, :], b["thr"][:, :],
                                         ALU.add),
               waits=(("qv", p.n("qv")),), inc="qv")
    p.emit("tensor", lambda t: t.matmul(
        cb["ps_bc"][0:1, 0:1], acc[:, :], b["qone"][:, :],
        start=True, stop=True),
        waits=(("qv", p.n("qv")),), inc="qmm")
    p.emit("vector", lambda v: v.memset(b["q11"][:, 0:1], 0.5e-6),
           waits=(("qmm", p.n("qmm")),), inc="qv")
    p.emit("vector", lambda v, N=rows * cols: v.scalar_tensor_tensor(
        b["q11"][:, 1:2], cb["ps_bc"][0:1, 0:1], 0.5 / N, b["q11"][:, 0:1],
        ALU.mult, ALU.add),
        waits=(("qv", p.n("qv")),), inc="qv")
    p.emit("vector", lambda v: v.memset(cb["ones1"][:, :], 1.0),
           waits=(("bc", p.n("bc")),), inc="bc")
    p.emit("tensor", lambda t: t.matmul(
        cb["ps_bc"][:, 0:1], cb["ones1"][:, :], b["q11"][:, 1:2],
        start=True, stop=True),
        waits=(("qv", p.n("qv")), ("bc", p.n("bc"))), inc="qmm")
    p.emit("scalar", lambda s: s.copy(b["thr"][:, :], cb["ps_bc"][:, 0:1]),
           waits=(("qmm", p.n("qmm")),), inc="qv")
    thr_done = p.n("qv")
    for i in range(n128):
        g = sA if i % 2 == 0 else sB
        gm = b["qm"]
        ds = dsem + str(i % 2)
        p.dma("sync", g[:, 0:cols], src_dram[i * 128:(i + 1) * 128, :], ds,
              waits=(("qv", max(thr_done, p.n("qv") - 3)),))
        p.emit("scalar", lambda s, g=g: s.activation(
            b["qm"][:, 0:cols], g[:, 0:cols], AF.Abs),
            waits=((ds, p.n(ds)), ("qv", p.n("qv"))), inc="qv")
        p.emit("vector", lambda v: v.tensor_scalar(
            b["qm"][:, 0:cols], b["qm"][:, 0:cols], b["thr"][:, :], None,
            ALU.is_gt),
            waits=(("qv", p.n("qv")), ("qmm", p.n("qmm"))), inc="qv")
        p.emit("scalar", lambda s, g=g: s.sign(g[:, 0:cols], g[:, 0:cols]),
               waits=(("qv", p.n("qv") - 1),), inc="qv")
        p.emit("vector", lambda v, g=g: v.tensor_tensor(
            b["qb"][:, 0:cols], g[:, 0:cols], b["qm"][:, 0:cols], ALU.mult),
            waits=(("qv", p.n("qv")),), inc="qv")
        p.dma("sync", dst_dram[i * 128:(i + 1) * 128, :], b["qb"][:, 0:cols],
              "dqo" + tag, waits=(("qv", p.n("qv")),))
        p.wait("vector", "dqo" + tag)
    p.wait("sync", "dqo" + tag)


def alloc_quant_bufs(nc, st):
    sb = lambda n, shp, dt: st.enter_context(nc.sbuf_tensor(_u(n), shp, dt))
    return {
        "qs": [sb("q_s0", [128, 4096], F32), sb("q_s1", [128, 4096], F32)],
        "qm": sb("q_m", [128, 4096], F32),
        "qb": sb("q_b", [128, 4096], BF16),
        "qacc": sb("q_acc", [128, 1], F32),
        "qone": sb("q_one", [128, 1], F32),
        "q11": sb("q_11", [1, 2], F32),
        "thr": sb("q_thr", [128, 1], F32),
    }


def emit_bitffn(p, cfg, cb):
    T = cfg["T"]
    nt = T // 128
    xF = cfg["xF"]
    b = cfg["bufs"]
    wready = tuple(cfg.get("w_ready", ()))
    uadd0 = p.n("uadd")
    xcv = p.n("xc")
    fmm10, fgg0, fgn0 = p.n("fmm1"), p.n("fgg"), p.n("fgn")
    fmm20, tmm0, tev0 = p.n("fmm2"), p.n("tmm"), p.n("tev")
    xft = cfg.get("xf_tev")

    def emit_mm1(ct):
        sf = cb["stage_f"][ct % 2]
        w = [("uadd", max(uadd0, p.n("uadd") - 1))]
        if ct < 2:
            w.append(("xc", xcv))
        if cfg.get("x_waits") is not None:
            w += list(cfg["x_waits"](ct))
        p.dma("sync", sf[:, :], cfg["x_dram"][ct * 128:(ct + 1) * 128, :],
              f"dr{ct % 2}", waits=w)
        for qd in range(4):
            slot = (qd % 2) * 1024
            gslot = fgg0 + 4 * ct + qd - 1
            for kc in range(NKC):
                ww = []
                if kc == 0 and gslot > fgg0:
                    ww.append(("fgg", gslot))
                if ct == 0 and qd == 0 and kc == 0:
                    ww += list(wready)
                if qd == 0 and kc == 0 and xft is not None:
                    ww.append(("tev", xft + 8 * (ct + 1)))
                for h2 in range(2):
                    p.emit("tensor",
                           lambda t, ct=ct, qd=qd, kc=kc, h2=h2, slot=slot:
                           t.matmul(
                               b["ps_h"][:, slot + h2 * 512: slot + (h2 + 1) * 512],
                               xF[:, kc * T + ct * 128: kc * T + (ct + 1) * 128],
                               cfg["wfc_sb"][:, kc * 4096 + qd * 1024 + h2 * 512:
                                             kc * 4096 + qd * 1024 + (h2 + 1) * 512],
                               start=(kc == 0), stop=(kc == NKC - 1)),
                           waits=ww,
                           inc="fmm1" if (kc == NKC - 1 and h2 == 1) else None)
                    ww = []
            htm = b["h_tm"][ct % 2]
            p.emit("scalar",
                   lambda s, ct=ct, qd=qd, slot=slot, htm=htm: s.activation(
                       htm[:, qd * 1024:(qd + 1) * 1024],
                       b["ps_h"][:, slot:slot + 1024], AF.Gelu),
                   waits=(("fmm1", fmm10 + 4 * ct + qd + 1),
                          ("tmm", tmm0 + 32 * max(0, ct - 1))), inc="fgg")
        htm = b["h_tm"][ct % 2]
        emit_ln_chunk(p, htm, 4096, cb,
                      waits=(("fgg", fgg0 + 4 * (ct + 1)),),
                      dest=("rstd2", "nmrs2"))
        p.emit("scalar",
               lambda s, htm=htm: s.activation(htm[:, :], htm[:, :],
                                               AF.Identity,
                                               bias=cb["nmrs2"][:, :],
                                               scale=cb["rstd2"][:, :]),
               waits=(("ln", p.n("ln")),), inc="fgn")

    def emit_mm2(ct):
        sf = cb["stage_f"][ct % 2]
        drs, dws = f"dr{ct % 2}", f"dw{ct % 2}"
        dwp = f"dw{(ct - 1) % 2}"
        htm = b["h_tm"][ct % 2]
        fg = tuple(cfg.get("psT_ready", ())) if ct == 0 else ()
        emit_tiles_to_fmajor(
            p, b["hF_t"], 128, range(32), [0],
            lambda ct2, cd, htm=htm: htm[:, cd * 128:(cd + 1) * 128],
            cb, waits=(("fgn", fgn0 + ct + 1),
                       ("fmm2", max(fmm20, fmm20 + ct))) + fg,
            gb=(cfg["lnw_cols"], cfg["lnb_cols"]))
        for kc2 in range(32):
            ww = []
            if kc2 == 0:
                ww = [("tev", p.n("tev")), ("uadd", max(uadd0, p.n("uadd")))]
            for h2 in range(2):
                p.emit("tensor",
                       lambda t, kc2=kc2, h2=h2: t.matmul(
                           b["ps_o"][:, h2 * 512:(h2 + 1) * 512],
                           b["hF_t"][:, kc2 * 128:(kc2 + 1) * 128],
                           cfg["wproj_sb"][:, kc2 * 1024 + h2 * 512:
                                           kc2 * 1024 + (h2 + 1) * 512],
                           start=(kc2 == 0), stop=(kc2 == 31)),
                       waits=ww,
                       inc="fmm2" if (kc2 == 31 and h2 == 1) else None)
                ww = []
        mm2v = p.n("fmm2")
        p.emit("vector",
               lambda v, sf=sf: v.tensor_tensor(
                   cb["ubuf"][:, :], sf[:, :], b["ps_o"][:, 0:1024], ALU.add),
               waits=(("fmm2", mm2v), (drs, p.n(drs)), (dwp, p.n(dwp))),
               inc="uadd")
        emit_ln_chunk(p, cb["ubuf"], 1024, cb, waits=(("uadd", p.n("uadd")),))
        p.emit("scalar",
               lambda s: s.activation(cb["ubuf"][:, :], cb["ubuf"][:, :],
                                      AF.Identity, bias=cb["nmrs"][:, :],
                                      scale=cb["rstd"][:, :]),
               waits=(("ln", p.n("ln")),), inc="uout")
        p.dma("sync", cfg["out_dram"][ct * 128:(ct + 1) * 128, :],
              cb["ubuf"][:, :], dws, waits=(("uout", p.n("uout")),))

    for ct in range(nt):
        emit_mm1(ct)
        if ct >= 1:
            emit_mm2(ct - 1)
    emit_mm2(nt - 1)
    if cfg.get("trailing", True):
        p.wait("sync", "dw0")
        p.wait("sync", "dw1")

WNAMES = [
    ("L0_attn_W_K", [16, 16]), ("L0_attn_W_Q", [16, 16]),
    ("L0_attn_W_V", [16, 16]), ("L0_attn_W_pred", [16, 64, 64]),
    ("L0_ffn_ln1_g", [D]), ("L0_ffn_ln1_b", [D]),
    ("L0_ffn_ln2_g", [D]), ("L0_ffn_ln2_b", [D]),
    ("L0_ffn_Wq", [D, D]), ("L0_ffn_bq", [D]),
    ("L0_ffn_Wk", [D, D]), ("L0_ffn_bk", [D]),
    ("L0_ffn_Wv", [D, D]), ("L0_ffn_bv", [D]),
    ("L0_ffn_Wp", [D, D]), ("L0_ffn_bp", [D]),
    ("L0_ffn_gate_W", [D, NE]), ("L0_ffn_gate_b", [NE]),
    ("L0_ffn_exp_W1", [NE, D, HID]), ("L0_ffn_exp_b1", [NE, HID]),
    ("L0_ffn_exp_W2", [NE, HID, D]), ("L0_ffn_exp_b2", [NE, D]),
    ("L1_attn_qkv_w", [3 * D, D]), ("L1_attn_qkv_b", [3 * D]),
    ("L1_attn_proj_w", [D, D]), ("L1_attn_proj_b", [D]),
    ("L1_ffn_W_fc", [HID, D]), ("L1_ffn_ln_w", [HID]),
    ("L1_ffn_ln_b", [HID]), ("L1_ffn_W_proj", [D, HID]),
]


def build_full(T=2048, n_seq=4):
    nc = bass.Bass("TRN2", target_bir_lowering=False)
    a = {}
    a["x_in"] = nc.dram_tensor("x_in", [n_seq * T, D], F32,
                               kind="ExternalInput").ap()
    for n, shp in WNAMES:
        a[n] = nc.dram_tensor(n, shp, F32, kind="ExternalInput").ap()
    for n, shp in (("cst_ident", [128, 128]), ("cst_tri", [128, 128]),
                   ("cst_esel", [16, 8 * 128]), ("cst_dpat", [128, 128])):
        a[n] = nc.dram_tensor(n, shp, BF16, kind="ExternalInput").ap()
    a["cst_identf"] = nc.dram_tensor("cst_identf", [128, 128], F32,
                                     kind="ExternalInput").ap()
    a["x_out"] = nc.dram_tensor("x_out", [n_seq * T, D], F32,
                                kind="ExternalOutput").ap()
    a["l0qkv"] = nc.dram_tensor("l0qkv", [D, 3 * D], BF16).ap()
    a["qkvbf"] = nc.dram_tensor("qkvbf", [3 * D, D], BF16).ap()
    a["projbf"] = nc.dram_tensor("projbf", [D, D], BF16).ap()
    a["wfcq"] = nc.dram_tensor("wfcq", [HID, D], BF16).ap()
    a["wprq"] = nc.dram_tensor("wprq", [D, HID], BF16).ap()
    xs = [nc.dram_tensor(f"xs{i}", [T, D], F32).ap() for i in range(4)]
    acc = nc.dram_tensor("acc_s", [T, D], F32).ap()
    w1bf = nc.dram_tensor("w1bf", [NE * 8 * 128, 4096], BF16).ap()
    w2bf = nc.dram_tensor("w2bf", [NE * 8 * 128, 4096], BF16).ap()
    xbf_in = nc.dram_tensor("xbf_in", [n_seq * T, D], BF16).ap()

    with ExitStack() as st:
        p = Prog(nc, st)
        cb = alloc_common_bufs(nc, st)
        cb["ident"] = cb["_ident_sb"]
        p.dma("sync", cb["_ident_sb"][:, :], a["cst_ident"][:, :], "dcst")
        p.dma("sync", cb["_tri_sb"][:, :], a["cst_tri"][:, :], "dcst")
        p.dma("sync", cb["_identf_sb"][:, :], a["cst_identf"][:, :], "dcst")
        p.emit("vector", lambda v: v.memset(cb["eps_col"][:, :], LN_EPS),
               inc="ev")
        p.wait("tensor", "dcst")
        p.wait("scalar", "ev")
        p.wait("vector", "ev")

        with ExitStack() as st0:
            b0 = alloc_p0_bufs(nc, st0)
            emit_phase0_l0(p, a, b0, cb["_ident_sb"])
            p.flush("p0a")
        with ExitStack() as st0:
            cstg = [st0.enter_context(nc.sbuf_tensor(_u("cst0"), [128, 4096], BF16)),
                    st0.enter_context(nc.sbuf_tensor(_u("cst1"), [128, 4096], BF16))]
            emit_cast_to_scratch(p, a["L1_attn_qkv_w"], a["qkvbf"], 3 * D, D,
                                 cstg)
            emit_cast_to_scratch(p, a["L1_attn_proj_w"], a["projbf"], D, D,
                                 cstg)
            emit_cast_to_scratch(p, a["x_in"], xbf_in, n_seq * T, D, cstg)
            p.flush("p0b")
        with ExitStack() as st0:
            qb = alloc_quant_bufs(nc, st0)
            cb["ps_bc"] = st0.enter_context(
                nc.psum_tensor(_u("psbc"), [128, 512], F32))
            emit_absmean_quant(p, a["L1_ffn_W_fc"], a["wfcq"], HID, D, qb, cb,
                               "f")
            emit_absmean_quant(p, a["L1_ffn_W_proj"], a["wprq"], D, HID, qb,
                               cb, "p")
            p.flush("p0q")
            cb.pop("ps_bc")

        sbg = lambda stk, n, shp, dt: stk.enter_context(
            nc.sbuf_tensor(_u(n), shp, dt))

        gcols = st.enter_context(nc.sbuf_tensor("g_cols", [128, 128], F32))
        with ExitStack() as stR:
            cb["ps_cl"] = stR.enter_context(
                nc.psum_tensor(_u("psbc"), [128, 32], F32))
            for (lo, hi, name, sl) in (
                    (0, 8, "L0_ffn_bq", None), (8, 16, "L0_ffn_bk", None),
                    (16, 24, "L0_ffn_ln1_g", None),
                    (24, 32, "L0_ffn_ln1_b", None),
                    (32, 40, "L0_ffn_ln2_g", None),
                    (40, 48, "L0_ffn_ln2_b", None),
                    (48, 56, "L1_attn_qkv_b", (0, 1024)),
                    (56, 64, "L1_attn_qkv_b", (1024, 2048)),
                    (64, 96, "L1_ffn_ln_w", None),
                    (96, 128, "L1_ffn_ln_b", None)):
                src = a[name] if sl is None else a[name][sl[0]:sl[1]]
                emit_col_load(p, gcols[:, lo:hi], src, cb)
            p.wait("tensor", "clv")
            cb.pop("ps_cl")
        p.flush("p0c")

        def snap(*names):
            return [(n, p.n(n)) for n in names]

        def fence(waits):
            for e in ENGS:
                p.emit(e, None, waits=tuple(waits))

        FE = []
        fmm1E0 = 0
        dwD = None
        for sq in range(n_seq):
            x0 = a["x_in"][sq * T:(sq + 1) * T, :]
            xo = a["x_out"][sq * T:(sq + 1) * T, :]

            with ExitStack() as stA:
                ab = alloc_attn_bufs(nc, stA, T)
                ab["psT"] = cb["psT"]
                ab["ident"] = cb["_ident_sb"]
                xF = cb["xfb"]
                dxt0 = p.n("dxt")
                gw = None
                if sq > 0:
                    gw = lambda j, b0=fmm1E0: [("fmm1", b0 + 16 * (j + 1))]
                emit_load_xF_fast(p, xbf_in[sq * T:(sq + 1) * T, :], xF, T,
                                  group_waits=gw)
                fence(FE)
                z_tm = sbg(stA, "A_ztm", [128, (T // 128) * 1024], BF16)
                wq_sb = sbg(stA, "A_wq", [128, 8 * 1024], BF16)
                wk_sb = sbg(stA, "A_wk", [128, 8 * 1024], BF16)
                wv_sb = sbg(stA, "A_wv", [128, 8 * 1024], BF16)
                load_weight_slab(p, wq_sb, a["l0qkv"][:, 0:1024], cast=False)
                load_weight_slab(p, wk_sb, a["l0qkv"][:, 1024:2048], cast=False)
                load_weight_slab(p, wv_sb, a["l0qkv"][:, 2048:3072], cast=False)
                cfg = {"T": T, "causal": True, "xF": xF, "z_tm": z_tm,
                       "wk_sb": wk_sb, "wq_sb": wq_sb, "wv_sb": wv_sb,
                       "tri": cb["_tri_sb"], "xf_tev": None, "xf_dxt0": dxt0,
                       "qkv_ready": [("dwt", p.n("dwt")),
                                     ("dsc0", p.n("dsc0")),
                                     ("dsc1", p.n("dsc1")),
                                     ("dp0s", p.n("dp0s"))],
                       "xf_ready": [],
                       "bq_cols": None, "bk_cols": None, "bv_row": None}
                emit_attention(p, cfg, ab)
                emit_resid_ln_out(p, x0, z_tm, xs[0], T, cb, trailing=False)
                FA = snap("amm", "zmm", "qkev", "eev", "zfev", "vev", "zev",
                          "uadd")
                dwA = (p.n("dw0"), p.n("dw1"))

            with ExitStack() as stB:
                fence(FA)
                hF = cb["xfb"]
                z_tm = sbg(stB, "B_ztm", [128, (T // 128) * 1024], BF16)
                wq_sb = sbg(stB, "B_wq", [128, 8 * 1024], BF16)
                wk_sb = sbg(stB, "B_wk", [128, 8 * 1024], BF16)
                bv_row = sbg(stB, "B_bvr", [128, 1024], F32)
                bp_row = sbg(stB, "B_bpr", [128, 1024], F32)
                with ExitStack() as stR:
                    cb["ps_bc"] = stR.enter_context(
                        nc.psum_tensor(_u("psbc"), [128, 512], F32))
                    emit_row_bcast(p, bv_row, a["L0_ffn_bv"], 1024, cb)
                    emit_row_bcast(p, bp_row, a["L0_ffn_bp"], 1024, cb)
                    cb.pop("ps_bc")
                load_weight_slab(p, wq_sb, a["L0_ffn_Wq"], cast=True)
                load_weight_slab(p, wk_sb, a["L0_ffn_Wk"], cast=True)
                bcols = gcols[:, 0:16]
                ab = alloc_attn_bufs(nc, stB, T)
                ab["psT"] = cb["psT"]
                ab["ident"] = cb["_ident_sb"]
                tev0 = emit_load_ln_hF(p, xs[0], hF, T, cb,
                                       gcols[:, 16:24], gcols[:, 24:32],
                                       x_waits=dw_chunk_waits(p, *dwA))
                cfg = {"T": T, "causal": True, "xF": hF, "z_tm": z_tm,
                       "wk_sb": wk_sb, "wq_sb": wq_sb, "tri": cb["_tri_sb"],
                       "xf_tev": tev0,
                       "qkv_ready": [("dwg", p.n("dwg")),
                                     ("bc", p.n("bc"))],
                       "xf_ready": [],
                       "bq_cols": bcols[:, 0:8], "bk_cols": bcols[:, 8:16],
                       "bv_row": bv_row}
                with ExitStack() as stV:
                    wv_sb = sbg(stV, "B_wv", [128, 8 * 1024], BF16)
                    load_weight_slab(p, wv_sb, a["L0_ffn_Wv"], cast=True,
                                     sem="dwv")
                    vcfg = dict(cfg)
                    vcfg["wv_sb"] = wv_sb
                    vcfg["qkv_ready"] = [("dwv", p.n("dwv")),
                                         ("bc", p.n("bc"))]
                    vcfg["bv_ready"] = [("bc", p.n("bc"))]
                    emit_attention_v(p, vcfg, ab)
                vmm_done = p.n("amm")
                wp_sb = sbg(stB, "B_wp", [128, 8 * 1024], BF16)
                load_weight_slab(p, wp_sb, a["L0_ffn_Wp"], cast=True,
                                 sem="dwp", waits=(("amm", vmm_done),))
                emit_attention_pairs(p, cfg, ab)
                zF = hF
                emit_zf_transpose(p, z_tm, zF, T, cb,
                                  guard=[("zev", p.n("zev")),
                                         ("amm", p.n("amm"))])
                emit_proj_resid(p, zF, wp_sb, bp_row, xs[0], xs[1], T,
                                ab["ps_main"][:, 0:1024], cb, final_ln=False,
                                w_ready=[("dwp", p.n("dwp")),
                                         ("eev", p.n("eev"))],
                                x_waits=dw_chunk_waits(p, *dwA),
                                trailing=False)
                FB = snap("amm", "zmm", "qkev", "eev", "zfev", "vev", "zev",
                          "uadd", "pmm", "tmm", "tev")
                dwB = (p.n("dw0"), p.n("dw1"))

            with ExitStack() as stC:
                fence(FB)
                h2F = cb["xfb"]
                mb = alloc_moe_bufs(nc, stC)
                tev0 = emit_load_ln_hF(p, xs[1], h2F, T, cb,
                                       gcols[:, 32:40], gcols[:, 40:48],
                                       x_waits=dw_chunk_waits(p, *dwB))
                mcfg = {"T": T, "h2F": h2F, "h2f_tev": tev0,
                        "aps": {"gate_W": a["L0_ffn_gate_W"],
                                "gate_b": a["L0_ffn_gate_b"],
                                "exp_W1": a["L0_ffn_exp_W1"],
                                "exp_b1": a["L0_ffn_exp_b1"],
                                "exp_W2": a["L0_ffn_exp_W2"],
                                "exp_b2": a["L0_ffn_exp_b2"]},
                        "x1_dram": xs[0], "x2_dram": xs[1],
                        "acc_dram": acc, "out_dram": xs[2],
                        "xfb": cb["xfb"],
                        "x1_waits": dw_chunk_waits(p, *dwA),
                        "x2_waits": dw_chunk_waits(p, *dwB),
                        "w_cache": (w1bf, w2bf),
                        "cache_mode": "write" if sq == 0 else "read",
                        "h2f_ready": [], "trailing": False}
                tevC = emit_moe(p, mcfg, mb, cb)
                FC = snap("mm1", "mm2", "gel", "yev", "gv", "dacc", "wte",
                          "tmm", "tev", "ubf") + [("uadd", mcfg["_uadd_ps"])]
                dwC = (p.n("dw0"), p.n("dw1"))

            with ExitStack() as stD:
                fence(FC)
                xF = cb["xfb"]
                z_tm = sbg(stD, "D_ztm", [128, (T // 128) * 1024], BF16)
                wq_sb = sbg(stD, "D_wq", [128, 8 * 1024], BF16)
                wk_sb = sbg(stD, "D_wk", [128, 8 * 1024], BF16)
                bv_row = sbg(stD, "D_bvr", [128, 1024], F32)
                bp_row = sbg(stD, "D_bpr", [128, 1024], F32)
                with ExitStack() as stR:
                    cb["ps_bc"] = stR.enter_context(
                        nc.psum_tensor(_u("psbc"), [128, 512], F32))
                    emit_row_bcast(p, bv_row, a["L1_attn_qkv_b"][2048:3072],
                                   1024, cb)
                    emit_row_bcast(p, bp_row, a["L1_attn_proj_b"], 1024, cb)
                    cb.pop("ps_bc")
                emit_trload_slab(p, wq_sb, a["qkvbf"][0:1024, :], 1024,
                                 range(8))
                emit_trload_slab(p, wk_sb, a["qkvbf"][1024:2048, :], 1024,
                                 range(8))
                bcols = gcols[:, 48:64]
                ab = alloc_attn_bufs(nc, stD, T)
                ab["psT"] = cb["psT"]
                ab["ident"] = cb["_ident_sb"]
                cfg = {"T": T, "causal": False, "xF": xF, "z_tm": z_tm,
                       "wk_sb": wk_sb, "wq_sb": wq_sb, "tri": cb["_tri_sb"],
                       "xf_tev": tevC,
                       "qkv_ready": [("dtw", p.n("dtw")),
                                     ("bc", p.n("bc"))],
                       "xf_ready": [],
                       "bq_cols": bcols[:, 0:8], "bk_cols": bcols[:, 8:16],
                       "bv_row": bv_row}
                with ExitStack() as stV:
                    wv_sb = sbg(stV, "D_wv", [128, 8 * 1024], BF16)
                    emit_trload_slab(p, wv_sb, a["qkvbf"][2048:3072, :], 1024,
                                     range(8), sem="dtv")
                    vcfg = dict(cfg)
                    vcfg["wv_sb"] = wv_sb
                    vcfg["qkv_ready"] = [("dtv", p.n("dtv")),
                                         ("bc", p.n("bc"))]
                    vcfg["bv_ready"] = [("bc", p.n("bc"))]
                    emit_attention_v(p, vcfg, ab)
                vmm_done = p.n("amm")
                wp_sb = sbg(stD, "D_wp", [128, 8 * 1024], BF16)
                p.emit("sync", None, waits=(("amm", vmm_done),))
                emit_trload_slab(p, wp_sb, a["projbf"], 1024, range(8),
                                 sem="dtp")
                emit_attention_pairs(p, cfg, ab)
                zF = xF
                emit_zf_transpose(p, z_tm, zF, T, cb,
                                  guard=[("zev", p.n("zev")),
                                         ("amm", p.n("amm"))])
                tevD = emit_proj_resid(p, zF, wp_sb, bp_row, xs[2], xs[3], T,
                                       ab["ps_main"][:, 0:1024], cb,
                                       final_ln=True,
                                       w_ready=[("dtp", p.n("dtp")),
                                                ("eev", p.n("eev"))],
                                       xfb=cb["xfb"],
                                       x_waits=dw_chunk_waits(p, *dwC),
                                       trailing=False)
                FD = snap("amm", "zmm", "qkev", "eev", "zfev", "vev", "zev",
                          "uadd", "pmm", "tmm", "tev", "ubf")
                dwD = (p.n("dw0"), p.n("dw1"))

            with ExitStack() as stE:
                fence(FD)
                xF = cb["xfb"]
                wfc_sb = sbg(stE, "E_wfc", [128, 8 * 4096], BF16)
                wproj_sb = sbg(stE, "E_wpr", [128, 32 * 1024], BF16)

                eb = {"h_tm": [sbg(stE, "E_htm0", [128, 4096], BF16),
                               sbg(stE, "E_htm1", [128, 4096], BF16)],
                      "hF_t": sbg(stE, "E_hFt", [128, 32 * 128], BF16),
                      "ps_h": stE.enter_context(
                          nc.psum_tensor(_u("E_psh"), [128, 2048], F32)),
                      "ps_o": stE.enter_context(
                          nc.psum_tensor(_u("E_pso"), [128, 1024], F32))}
                emit_trload_slab(p, wfc_sb, a["wfcq"], 4096, range(8))
                for kc2 in range(32):
                    p.dma_tr(wproj_sb[:, kc2 * 1024:(kc2 + 1) * 1024],
                             a["wprq"][0:1024, kc2 * 128:(kc2 + 1) * 128],
                             "dtw")
                fmm1E0 = p.n("fmm1")
                ecfg = {"T": T, "xF": xF, "wfc_sb": wfc_sb, "xf_tev": tevD,
                        "wproj_sb": wproj_sb, "lnw_cols": gcols[:, 64:96],
                        "lnb_cols": gcols[:, 96:128], "x_dram": xs[3],
                        "out_dram": xo, "bufs": eb,
                        "w_ready": [("dtw", p.n("dtw"))],
                        "x_waits": dw_chunk_waits(p, *dwD),
                        "trailing": sq == n_seq - 1,
                        "xf_ready": []}
                emit_bitffn(p, ecfg, cb)
                FE = snap("fmm1", "fmm2", "fgg", "fgn", "tmm", "tev", "ln",
                          "uadd", "uout")
        p.flush("seqs")
    return nc


def kernel(**inputs):
    T, n_seq, n_cores = 2048, 4, 8
    nc = build_full(T=T, n_seq=n_seq)
    consts = host_consts()
    x = np.ascontiguousarray(inputs["L0_attn_x"])
    in_maps = []
    for c in range(n_cores):
        m = {"x_in": np.ascontiguousarray(
            x[c * n_seq:(c + 1) * n_seq].reshape(n_seq * T, D))}
        for n, _ in WNAMES:
            m[n] = inputs[n]
        m.update(consts)
        in_maps.append(m)
    res = run_bass_kernel_spmd(nc, in_maps, core_ids=list(range(n_cores)))
    out = np.empty((32, T, D), dtype=np.float32)
    for c in range(n_cores):
        out[c * n_seq:(c + 1) * n_seq] =             res.results[c]["x_out"].reshape(n_seq, T, D)
    return out


# version 16
# speedup vs baseline: 1.1161x; 1.0564x over previous
import sys
sys.path.insert(0, "/opt/trn_rl_repo")
from contextlib import ExitStack
import numpy as np
import ml_dtypes
import concourse.bass as bass
import concourse.mybir as mybir
from concourse.bass_utils import run_bass_kernel_spmd

F32 = mybir.dt.float32
BF16 = mybir.dt.bfloat16
AF = mybir.ActivationFunctionType
ALU = mybir.AluOpType
AXX = mybir.AxisListType.X

D = 1024
NH = 16
DH = 64
NPAIR = 8
HID = 4096
NE = 8
NKC = 8
LN_EPS = 1e-5

ENGS = ("tensor", "scalar", "vector", "sync", "gpsimd")

_UID = [0]


def _u(n):
    _UID[0] += 1
    return f"{n}_{_UID[0]}"


def host_consts():
    ident = np.eye(128).astype(ml_dtypes.bfloat16)
    tri = np.triu(np.ones((128, 128))).astype(ml_dtypes.bfloat16)
    esel = np.zeros((16, 8 * 128), dtype=ml_dtypes.bfloat16)
    for ci in range(8):
        for vl in range(2):
            for c in range(64):
                esel[2 * ci + vl, ci * 128 + vl * 64 + c] = 1.0
    dpat = np.zeros((128, 128), dtype=ml_dtypes.bfloat16)
    for vl in range(2):
        for il in range(2):
            for c in range(64):
                dpat[vl * 64 + c, il * 64 + c] = 1.0
    identf = np.eye(128).astype(np.float32)
    return {"cst_ident": ident, "cst_tri": tri, "cst_esel": esel,
            "cst_dpat": dpat, "cst_identf": identf}


class Prog:

    def __init__(self, nc, st: ExitStack):
        self.nc = nc
        self.st = st
        self.q = {e: [] for e in ENGS}
        self.sem = {}
        self.nblk = 0

    def get_sem(self, name):
        if name not in self.sem:
            h = self.st.enter_context(self.nc.semaphore(name))
            self.sem[name] = [h, 0]
        return self.sem[name]

    def n(self, name):
        return self.get_sem(name)[1]

    def emit(self, eng, fn, waits=(), inc=None, inc_by=1):
        winfo = [(self.get_sem(s)[0], v) for (s, v) in waits if v > 0]
        ih = None
        if inc is not None:
            rec = self.get_sem(inc)
            rec[1] += inc_by
            ih = rec[0]
        self.q[eng].append((fn, winfo, ih, inc_by))

    def dma(self, eng, dst, src, sem, waits=(), **kw):
        self.emit(eng, lambda e, d=dst, s=src: e.dma_start(d, s, **kw),
                  waits=waits, inc=sem, inc_by=16)

    def dma_tr(self, dst, src, sem, waits=()):
        self.emit("sync", lambda e, d=dst, s=src: e.dma_start_transpose(d, s),
                  waits=waits, inc=sem, inc_by=16)

    def wait(self, eng, sem_name, value=None):
        v = self.n(sem_name) if value is None else value
        if v > 0:
            self.emit(eng, None, waits=((sem_name, v),))

    def flush(self, name=None):
        self.nblk += 1
        name = name or f"b{self.nblk}"
        q = self.q
        self.q = {e: [] for e in ENGS}
        with self.nc.Block(name) as blk:
            for eng_name in ENGS:
                ops = q[eng_name]
                if not ops:
                    continue

                def runner(engine, ops=ops):
                    for fn, winfo, ih, iv in ops:
                        for (h, v) in winfo:
                            engine.wait_ge(h, v)
                        if fn is not None:
                            ins = fn(engine)
                            if ih is not None:
                                ins.then_inc(ih, iv)
                getattr(blk, eng_name)(runner)


def sb_ap(h, offset, dims):
    return bass.AP(h, offset, [[int(h.shape[1]), int(h.shape[0])]] + dims)



def emit_ln_chunk(p, x_ap, nfree, b, waits=(), dest=("rstd", "nmrs")):
    rstd_k, nmrs_k = dest
    k = (nfree + 511) // 512
    for i in range(k):
        w = tuple(waits) + ((("ln", p.n("ln")),) if i == 0 else ())
        lo, hi = i * 512, min((i + 1) * 512, nfree)
        p.emit("vector",
               lambda v, i=i, lo=lo, hi=hi: v.bn_stats(
                   b["s6"][:, i * 6:(i + 1) * 6], x_ap[:, lo:hi]),
               waits=w, inc="ln")
    p.emit("vector", lambda v, k=k: v.bn_aggr(b["s2"][:, :], b["s6"][:, 0:6 * k]),
           waits=(("ln", p.n("ln")),), inc="ln")
    p.emit("scalar", lambda s: s.activation(b[rstd_k][:, :], b["s2"][:, 1:2],
                                            AF.Ln, bias=b["eps_col"][:, :],
                                            scale=1.0),
           waits=(("ln", p.n("ln")),), inc="ln")
    p.emit("scalar", lambda s: s.activation(b[rstd_k][:, :], b[rstd_k][:, :],
                                            AF.Exp, scale=-0.5),
           waits=(("ln", p.n("ln")),), inc="ln")
    p.emit("vector", lambda v: v.scalar_tensor_tensor(
        b[nmrs_k][:, :], b["s2"][:, 0:1], -1.0, b[rstd_k][:, :],
        ALU.mult, ALU.mult),
        waits=(("ln", p.n("ln")),), inc="ln")


def emit_row_bcast(p, out, src_dram, n, b):
    row, ones, ps = b["brow"], b["ones1"], b["ps_bc"]
    p.dma("sync", row[:, 0:n], src_dram[None, :], "dbr",
          waits=(("bmm", p.n("bmm")),))
    dlv = p.n("dbr")
    p.emit("vector", lambda v: v.memset(ones[:, :], 1.0),
           waits=(("bc", p.n("bc")),), inc="bc")
    ev0 = p.n("bc")
    for j in range(0, n, 512):
        w = min(512, n - j)
        guard = p.n("bc")
        p.emit("tensor",
               lambda t, j=j, w=w: t.matmul(ps[:, 0:w], ones[:, :],
                                            row[:, j:j + w],
                                            start=True, stop=True),
               waits=(("dbr", dlv), ("bc", guard)), inc="bmm")
        p.emit("scalar",
               lambda s, j=j, w=w: s.copy(out[:, j:j + w], ps[:, 0:w]),
               waits=(("bmm", p.n("bmm")),), inc="bc")


def emit_tiles_to_fmajor(p, dst, T, cd_range, ct_range, src_col, b, waits=(),
                         gb=None):
    tiles = [(ct, cd) for ct in ct_range for cd in cd_range]
    first = True
    tev0 = p.n("tev")
    ngrp = (len(tiles) + 3) // 4
    for g in range(ngrp):
        grp = tiles[4 * g:4 * g + 4]
        half = (g % 2) * 1024
        w0 = [("tev", tev0 + 4 * (g - 1))] if g >= 2 else []
        for j, (ct, cd) in enumerate(grp):
            w = list(w0) if j == 0 else []
            if first:
                w += list(waits)
                first = False
            p.emit("tensor",
                   lambda t, ct=ct, cd=cd, j=j, half=half: t.transpose(
                       b["psT"][:, half + j * 128: half + (j + 1) * 128],
                       src_col(ct, cd), b["ident"][:, :]),
                   waits=w, inc="tmm")
            w0 = []
        mmv = p.n("tmm")
        for j, (ct, cd) in enumerate(grp):
            if gb is None:
                p.emit("vector",
                       lambda v, ct=ct, cd=cd, j=j, half=half: v.tensor_copy(
                           dst[:, cd * T + ct * 128: cd * T + (ct + 1) * 128],
                           b["psT"][:, half + j * 128: half + (j + 1) * 128]),
                       waits=((("tmm", mmv),) if j == 0 else ()), inc="tev")
            else:
                gc, bc = gb
                p.emit("vector",
                       lambda v, ct=ct, cd=cd, j=j, half=half: v.tensor_scalar(
                           dst[:, cd * T + ct * 128: cd * T + (ct + 1) * 128],
                           b["psT"][:, half + j * 128: half + (j + 1) * 128],
                           gc[:, cd:cd + 1], bc[:, cd:cd + 1],
                           ALU.mult, ALU.add),
                       waits=((("tmm", mmv),) if j == 0 else ()), inc="tev")


def emit_load_xF(p, x_dram, xF, T, b):
    nt = T // 128
    xc0, tmm0 = p.n("xc"), p.n("tmm")
    tev0 = p.n("tev")
    for ct in range(nt):
        sf, sb = b["stage_f"][ct % 2], b["stage_b"][ct % 2]
        dxs = f"dx{ct % 2}"
        p.dma("sync", sf[:, :], x_dram[ct * 128:(ct + 1) * 128, :], dxs,
              waits=(("xc", max(xc0, p.n("xc") - 1)),))
        p.emit("vector", lambda v, sf=sf, sb=sb: v.tensor_copy(sb[:, :], sf[:, :]),
               waits=((dxs, p.n(dxs)), ("tmm", tmm0 + 8 * max(0, ct - 1))),
               inc="xc")
        emit_tiles_to_fmajor(
            p, xF, T, range(NKC), [ct],
            lambda ct2, cd, sb=sb: sb[:, cd * 128:(cd + 1) * 128],
            b, waits=(("xc", p.n("xc")),))



def emit_load_xF_fast(p, xbf_dram, xF, T, sem="dxt", group_waits=None):
    for j in range(T // 512):
        gw = list(group_waits(j)) if group_waits is not None else []
        for kc in range(NKC):
            p.dma_tr(xF[:, kc * T + j * 512: kc * T + (j + 1) * 512],
                     xbf_dram[j * 512:(j + 1) * 512,
                              kc * 128:(kc + 1) * 128], sem,
                     waits=gw if kc == 0 else ())


def emit_attention(p, cfg, b):
    emit_attention_v(p, cfg, b)
    emit_attention_pairs(p, cfg, b)


def emit_attention_v(p, cfg, b):
    T, causal = cfg["T"], cfg["causal"]
    nt, nq = T // 128, T // 512
    xF = cfg["xF"]
    wv = cfg["wv_sb"]
    bvr = cfg.get("bv_row")
    wready = tuple(cfg.get("qkv_ready", ()))
    xf_ready = tuple(cfg.get("xf_ready", ()))
    v65 = b["v65"]
    pm = b["ps_main"]

    p.emit("vector", lambda v: v.memset(
        sb_ap(v65, 64, [[1040, nt], [65, 16]]), 1.0), inc="vev")
    xft = cfg.get("xf_tev")
    vev0 = p.n("vev")
    for tt in range(nt):
        ps_v = pm[:, (tt % 2) * 1024:(tt % 2) * 1024 + 1024]
        guard = max(vev0, p.n("vev") - 1)
        for kc in range(NKC):
            w = [("vev", guard)] if kc == 0 else []
            if kc == 0 and xft is not None:
                w.append(("tev", xft + 8 * (tt + 1)))
            if kc == 0 and cfg.get("xf_dxt0") is not None:
                w.append(("dxt", cfg["xf_dxt0"] + 128 * (tt // 4 + 1)))
            if tt == 0 and kc == 0:
                w += list(wready) + list(xf_ready)
            p.emit("tensor",
                   lambda t, tt=tt, kc=kc, ps_v=ps_v: t.matmul(
                       ps_v[:, 0:512],
                       xF[:, kc * T + tt * 128: kc * T + (tt + 1) * 128],
                       wv[:, kc * 1024: kc * 1024 + 512],
                       start=(kc == 0), stop=(kc == NKC - 1)),
                   waits=w)
            p.emit("tensor",
                   lambda t, tt=tt, kc=kc, ps_v=ps_v: t.matmul(
                       ps_v[:, 512:1024],
                       xF[:, kc * T + tt * 128: kc * T + (tt + 1) * 128],
                       wv[:, kc * 1024 + 512: (kc + 1) * 1024],
                       start=(kc == 0), stop=(kc == NKC - 1)),
                   inc="amm" if kc == NKC - 1 else None)
        mmv = p.n("amm")
        dst = sb_ap(v65, tt * 1040, [[65, 16], [1, 64]])
        if bvr is not None:
            bvw = tuple(cfg.get("bv_ready", ())) if tt == 0 else ()
            p.emit("vector",
                   lambda v, dst=dst, ps_v=ps_v: v.scalar_tensor_tensor(
                       dst, ps_v[:, 0:1024], 1.0, bvr[:, 0:1024],
                       ALU.mult, ALU.add),
                   waits=(("amm", mmv),) + bvw, inc="vev")
        else:
            p.emit("vector",
                   lambda v, dst=dst, ps_v=ps_v: v.tensor_copy(
                       dst, ps_v[:, 0:1024]),
                   waits=(("amm", mmv),), inc="vev")


def emit_attention_pairs(p, cfg, b):
    T, causal = cfg["T"], cfg["causal"]
    nt, nq = T // 128, T // 512
    xF, z_tm = cfg["xF"], cfg["z_tm"]
    wk, wq = cfg["wk_sb"], cfg["wq_sb"]
    bqc, bkc = cfg.get("bq_cols"), cfg.get("bk_cols")
    tri = cfg["tri"]
    wready = tuple(cfg.get("qkv_ready", ()))
    xf_ready = tuple(cfg.get("xf_ready", ()))
    v65, kq = b["v65"], b["kq"]
    est = b["est"]
    pm = b["ps_main"]
    ps_qk = pm[:, 0:min(T, 2048)]
    ps_zfA, ps_zfB = b["ps_zfA"], b["ps_zfB"]
    psT = b["psT"]
    zfsb = b["zfsb"]
    ident = b["ident"]
    kbar, rcol = b["kbar"], b["rcol"]
    v_done = p.n("vev")

    stmm_prev = 0
    zev_prev = 0
    ksum = b["ksum"]
    pend_evac = [None]
    for hp in range(NPAIR):
        cur = kq[hp % len(kq)]
        for which, wmat, bcol, off in (("k", wk, bkc, 0), ("q", wq, bqc, T)):
            guard = p.n("qkev")
            xft = cfg.get("xf_tev")
            for kc in range(NKC):
                for J in range(nq):
                    w = []
                    if kc == 0 and J == 0:
                        w = [("qkev", guard), ("amm", stmm_prev),
                             ("eev", p.n("eev")), ("vev", v_done)]
                        if hp == 0 and which == "k":
                            w += list(wready) + list(xf_ready)
                    if kc == 0 and hp == 0 and xft is not None:
                        w.append(("tev", xft + 8 * min(nt, 4 * (J + 1))))
                    if kc == 0 and hp == 0 and cfg.get("xf_dxt0") is not None:
                        w.append(("dxt", cfg["xf_dxt0"] + 128 * (J + 1)))
                    p.emit("tensor",
                           lambda t, kc=kc, J=J, hp=hp, wmat=wmat: t.matmul(
                               ps_qk[:, J * 512:(J + 1) * 512],
                               wmat[:, kc * 1024 + hp * 128: kc * 1024 + hp * 128 + 128],
                               xF[:, kc * T + J * 512: kc * T + (J + 1) * 512],
                               start=(kc == 0), stop=(kc == NKC - 1)),
                           waits=w,
                           inc="amm" if (kc == NKC - 1 and J == nq - 1) else None)
            mmv = p.n("amm")
            for J in range(nq):
                acc_k = ksum[:, J:J + 1] if which == "k" else None
                ew = (("amm", mmv),) if J == 0 else ()
                if which == "k" and J == 0:
                    ew = ew + (("ctr", p.n("ctr")),)
                if bcol is not None:
                    p.emit("scalar",
                           lambda s, J=J, off=off, hp=hp, bcol=bcol, cur=cur,
                           acc_k=acc_k: s.activation(
                               cur[:, off + J * 512: off + (J + 1) * 512],
                               ps_qk[:, J * 512:(J + 1) * 512], AF.Identity,
                               bias=bcol[:, hp:hp + 1], scale=1.0,
                               accum_out=acc_k),
                           waits=ew, inc="qkev")
                else:
                    p.emit("scalar",
                           lambda s, J=J, off=off, cur=cur, acc_k=acc_k:
                           s.activation(
                               cur[:, off + J * 512: off + (J + 1) * 512],
                               ps_qk[:, J * 512:(J + 1) * 512], AF.Identity,
                               scale=1.0, accum_out=acc_k),
                           waits=ew, inc="qkev")
            if which == "k":
                qkv_k = p.n("qkev")
                p.emit("vector", lambda v: v.reduce_sum(
                    kbar[:, 0:1], ksum[:, 0:nq], axis=AXX),
                    waits=(("qkev", qkv_k), ("ctr", p.n("ctr"))), inc="ctr")
                p.emit("vector", lambda v: v.tensor_scalar_mul(
                    kbar[:, 0:1], kbar[:, 0:1], -1.0 / T),
                    waits=(("ctr", p.n("ctr")),), inc="ctr")
                p.emit("vector", lambda v, cur=cur: v.tensor_scalar(
                    cur[:, 0:T], cur[:, 0:T], kbar[:, 0:1], None, ALU.add),
                    waits=(("ctr", p.n("ctr")),), inc="ctr")
        kready = p.n("ctr")
        qkv = p.n("qkev")

        for J in range(nq):
            plist = list(range(min(nt, (J + 1) * 4))) if causal                 else list(range(nt))
            npl = len(plist)
            exp_after = {}
            ready_after = {}
            zmm_after = {}

            def emit_st(idx):
                P = plist[idx]
                slot = idx % 2
                w = [("ctr", kready)]
                if idx == 0:
                    w.append(("qkev", qkv))
                if idx >= 2:
                    w.append(("eev", exp_after[idx - 2]))
                for (base, tp, hb) in ((0, (0, 0), 0), (64, (64, 0), 1)):
                    p.emit("tensor",
                           lambda t, P=P, J=J, slot=slot, base=base, tp=tp,
                           hb=hb, cur=cur: t.matmul(
                               pm[:, slot * 1024 + hb * 512:
                                  slot * 1024 + (hb + 1) * 512],
                               cur[base:base + 64, P * 128:(P + 1) * 128],
                               cur[base:base + 64, T + J * 512: T + (J + 1) * 512],
                               start=True, stop=True, tile_position=tp),
                           waits=w, inc="amm")
                    w = []

            def emit_exp(idx):
                P = plist[idx]
                slot, ring = idx % 2, idx % 4
                we = [("amm", p.n("amm"))]
                if idx >= 4:
                    we.append(("zmm", zmm_after[idx - 4]))
                p.emit("scalar",
                       lambda s, ring=ring, slot=slot: s.activation(
                           est[:, ring * 1024:(ring + 1) * 1024],
                           pm[:, slot * 1024:(slot + 1) * 1024],
                           AF.Exp, scale=0.125),
                       waits=we, inc="eev")
                exp_after[idx] = p.n("eev")
                rd = [("eev", exp_after[idx])]
                if causal and 4 * J <= P:
                    off = P * 128 - J * 512
                    for hb in (0, 1):
                        base = ring * 1024 + hb * 512
                        if off > 0:
                            p.emit("vector",
                                   lambda v, base=base, off=off: v.memset(
                                       est[:, base: base + off], 0.0),
                                   waits=(("eev", exp_after[idx]),),
                                   inc="bev")
                        p.emit("vector",
                               lambda v, base=base, off=off: v.tensor_tensor(
                                   est[:, base + off: base + off + 128],
                                   est[:, base + off: base + off + 128],
                                   tri[:, :], ALU.mult),
                               waits=(("eev", exp_after[idx]),), inc="bev")
                    rd.append(("bev", p.n("bev")))
                ready_after[idx] = rd

            def emit_z(idx):
                P = plist[idx]
                ring = idx % 4
                wz = list(ready_after[idx])
                if idx == 0:
                    wz += [("zfev", zev_prev), ("vev", v_done)]
                last = idx == npl - 1
                for hb, ps_zf in ((0, ps_zfA), (1, ps_zfB)):
                    p.emit("tensor",
                           lambda t, ps_zf=ps_zf, ring=ring, P=P, hp=hp,
                           hb=hb, idx=idx, last=last: t.matmul(
                               ps_zf[0:65, 0:512],
                               sb_ap(v65, P * 1040 + (2 * hp + hb) * 65,
                                     [[1, 65]]),
                               est[:, ring * 1024 + hb * 512:
                                   ring * 1024 + (hb + 1) * 512],
                               start=(idx == 0), stop=last),
                           waits=wz, inc="zmm")
                    wz = []
                zmm_after[idx] = p.n("zmm")

            for idx in range(npl):
                emit_st(idx)
                emit_exp(idx)
                if idx == 1 and pend_evac[0] is not None:
                    pend_evac[0]()
                    pend_evac[0] = None
                if idx > 1:
                    emit_z(idx - 2)
            if npl > 1:
                emit_z(npl - 2)
            emit_z(npl - 1)

            zmv = p.n("zmm")
            for hb, ps_zf in ((0, ps_zfA), (1, ps_zfB)):
                p.emit("vector",
                       lambda v, hb=hb, ps_zf=ps_zf: v.tensor_copy(
                           zfsb[0:65, hb * 512:(hb + 1) * 512],
                           ps_zf[0:65, 0:512]),
                       waits=(("zmm", zmv), ("ztr", p.n("ztr"))), inc="zfev")
            zev_prev = p.n("zfev")

            def evac(J=J, hp=hp, zfv=zev_prev):
                ztr0 = p.n("zev")
                for m in range(4):
                    for hb in (0, 1):
                        slot = 2 * m + hb
                        p.emit("tensor",
                               lambda t, m=m, hb=hb, slot=slot: t.transpose(
                                   psT[:, slot * 68: slot * 68 + 65],
                                   zfsb[0:65, hb * 512 + m * 128:
                                        hb * 512 + (m + 1) * 128],
                                   ident[0:65, 0:65]),
                               waits=(("zfev", zfv), ("zev", ztr0)), inc="ztr")
                ztv = p.n("ztr")
                for m in range(4):
                    ct = J * 4 + m
                    for hb in (0, 1):
                        head = 2 * hp + hb
                        slot = 2 * m + hb
                        p.emit("vector",
                               lambda v, slot=slot: v.reciprocal(
                                   rcol[:, slot:slot + 1],
                                   psT[:, slot * 68 + 64: slot * 68 + 65]),
                               waits=(("ztr", ztv),), inc="zev")
                        p.emit("vector",
                               lambda v, slot=slot, ct=ct, head=head:
                               v.tensor_scalar(
                                   z_tm[:, ct * 1024 + head * 64:
                                        ct * 1024 + (head + 1) * 64],
                                   psT[:, slot * 68: slot * 68 + 64],
                                   rcol[:, slot:slot + 1], None, ALU.mult),
                               waits=(("zev", p.n("zev")),), inc="zev")
            pend_evac[0] = evac
        stmm_prev = p.n("amm")
    if pend_evac[0] is not None:
        pend_evac[0]()
        pend_evac[0] = None


def alloc_attn_bufs(nc, st, T):
    nt = T // 128
    sb = lambda n, shp, dt: st.enter_context(nc.sbuf_tensor(_u(n), shp, dt))
    ps = lambda n, shp, dt=F32: st.enter_context(nc.psum_tensor(_u(n), shp, dt))
    return {
        "v65": sb("at_v65", [128, nt * 1040], BF16),
        "kq": [sb("at_kq0", [128, 2 * T], BF16)],
        "est": sb("at_est", [128, 4 * 1024], BF16),
        "kbar": sb("at_kbar", [128, 2], F32),
        "ksum": sb("at_ksum", [128, 16], F32),
        "rcol": sb("at_rcol", [128, 8], F32),
        "ps_main": ps("at_psmain", [128, 2048]),
        "ps_zfA": ps("at_pszfA", [128, 512]),
        "ps_zfB": ps("at_pszfB", [128, 512]),
        "zfsb": sb("at_zfsb", [128, 1024], BF16),
    }


def emit_resid_ln_out(p, x_dram, z_tm, out_dram, T, b, zscale=1.0,
                      trailing=True, xfb=None, gb=None):
    nt = T // 128
    uadd0 = p.n("uadd")
    xcv = p.n("xc")
    tmm0 = p.n("tmm")
    tev0 = p.n("tev")
    for ct in range(nt):
        sf = b["stage_f"][ct % 2]
        drs, dws = f"dr{ct % 2}", f"dw{ct % 2}"
        dwp = f"dw{(ct - 1) % 2}"
        w = [("uadd", max(uadd0, p.n("uadd") - 1))]
        if ct < 2:
            w.append(("xc", xcv))
        p.dma("sync", sf[:, :], x_dram[ct * 128:(ct + 1) * 128, :], drs,
              waits=w)
        w2 = (("ubf", p.n("ubf")),) if xfb is not None else ()
        p.emit("vector",
               lambda v, sf=sf, ct=ct: v.tensor_tensor(
                   b["ubuf"][:, :], sf[:, :],
                   z_tm[:, ct * 1024:(ct + 1) * 1024], ALU.add),
               waits=((drs, p.n(drs)), (dwp, p.n(dwp)),
                      ("zev", p.n("zev"))) + w2,
               inc="uadd")
        emit_ln_chunk(p, b["ubuf"], 1024, b, waits=(("uadd", p.n("uadd")),))
        p.emit("scalar",
               lambda s: s.activation(b["ubuf"][:, :], b["ubuf"][:, :],
                                      AF.Identity, bias=b["nmrs"][:, :],
                                      scale=b["rstd"][:, :]),
               waits=(("ln", p.n("ln")),), inc="uout")
        p.dma("sync", out_dram[ct * 128:(ct + 1) * 128, :], b["ubuf"][:, :],
              dws, waits=(("uout", p.n("uout")),))
        if xfb is not None:
            sbb = b["stage_b"][ct % 2]
            p.emit("vector",
                   lambda v, sbb=sbb: v.tensor_copy(sbb[:, :],
                                                    b["ubuf"][:, :]),
                   waits=(("uout", p.n("uout")),
                          ("tmm", tmm0 + 8 * max(0, ct - 1))),
                   inc="ubf")
            emit_tiles_to_fmajor(
                p, xfb, T, range(NKC), [ct],
                lambda ct2, cd, sbb=sbb: sbb[:, cd * 128:(cd + 1) * 128],
                b, waits=(("ubf", p.n("ubf")),), gb=gb)
    if trailing:
        p.wait("sync", "dw0")
        p.wait("sync", "dw1")
    return tev0


def dw_chunk_waits(p, ndw0, ndw1):
    def f(ct):
        n = (ndw0, ndw1)[ct % 2]
        return [(f"dw{ct % 2}", n - 16 * (7 - ct // 2))]
    return f


def alloc_common_bufs(nc, st):
    sb = lambda n, shp, dt: st.enter_context(nc.sbuf_tensor(n, shp, dt))
    ps = lambda n, shp, dt=F32: st.enter_context(nc.psum_tensor(n, shp, dt))
    return {
        "stage_f": [sb("c_stf0", [128, 1024], F32),
                    sb("c_stf1", [128, 1024], F32)],
        "stage_b": [sb("c_stb0", [128, 1024], BF16),
                    sb("c_stb1", [128, 1024], BF16)],
        "ubuf": sb("c_ubuf", [128, 1024], F32),

        "s6": sb("c_s6", [128, 48], F32),
        "s2": sb("c_s2", [128, 2], F32),
        "rstd": sb("c_rstd", [128, 1], F32),
        "nmrs": sb("c_nmrs", [128, 1], F32),
        "rstd2": sb("c_rstd2", [128, 1], F32),
        "nmrs2": sb("c_nmrs2", [128, 1], F32),
        "eps_col": sb("c_eps", [128, 1], F32),
        "brow": sb("c_brow", [1, 1024], F32),
        "ones1": sb("c_ones1", [1, 128], F32),
        "_ident_sb": sb("c_ident", [128, 128], BF16),
        "_identf_sb": sb("c_identf", [128, 128], F32),
        "_colrow": sb("c_colrow", [32, 128], F32),
        "_tri_sb": sb("c_tri", [128, 128], BF16),
        "psT": ps("c_psT", [128, 2048], BF16),
        "xfb": sb("c_xfb", [128, 8 * 2048], BF16),
    }


def emit_load_consts(p, cb, aps):
    cb["ident"] = cb["_ident_sb"]
    p.dma("sync", cb["_ident_sb"][:, :], aps["cst_ident"][:, :], "dl")
    p.dma("sync", cb["_tri_sb"][:, :], aps["cst_tri"][:, :], "dl")
    p.emit("vector", lambda v: v.memset(cb["eps_col"][:, :], LN_EPS), inc="ev")
    p.wait("vector", "dl")
    p.wait("tensor", "dl")
    p.wait("scalar", "ev")


def load_weight_slab(p, dst, w_dram, cast, sem=None, waits=()):
    n = int(w_dram.shape[1])
    eng = "gpsimd" if cast else "sync"
    if sem is None:
        sem = "dwg" if cast else "dwt"
    for kc in range(NKC):
        p.dma(eng, dst[:, kc * n:(kc + 1) * n],
              w_dram[kc * 128:(kc + 1) * 128, :], sem,
              waits=waits if kc == 0 else ())
    return sem



def alloc_p0_bufs(nc, st):
    sb = lambda n, shp, dt: st.enter_context(nc.sbuf_tensor(_u(n), shp, dt))
    ps = lambda n, shp, dt=F32: st.enter_context(nc.psum_tensor(_u(n), shp, dt))
    return {
        "w16": sb("p0_w16", [16, 48], F32),
        "m16": sb("p0_m16", [16, 4], F32),
        "pstage": sb("p0_pstage", [16, 16], BF16),
        "probsT": sb("p0_probsT", [16, 48], BF16),
        "pbc": sb("p0_pbc", [16, 128], BF16),
        "esel": sb("p0_esel", [16, 8 * 128], BF16),
        "pco": sb("p0_pco", [128, 8 * 128], BF16),
        "dpat": sb("p0_dpat", [128, 128], BF16),
        "slab": sb("p0_slab", [128, 3072], BF16),
        "ps16": ps("p0_ps16", [16, 16], BF16),
        "psA": ps("p0_psA", [128, 128]),
    }


def emit_phase0_l0(p, aps, b, ident):
    p.dma("sync", b["esel"][:, :], aps["cst_esel"][:, :], "dp0")
    p.dma("sync", b["dpat"][:, :], aps["cst_dpat"][:, :], "dp0")
    for wi, wname in enumerate(("L0_attn_W_K", "L0_attn_W_Q", "L0_attn_W_V")):
        p.dma("sync", b["w16"][:, wi * 16:(wi + 1) * 16], aps[wname][:, :],
              "dp0")
    wp = aps["L0_attn_W_pred"].tensor
    for co in range(8):
        for vl in range(2):
            src = bass.AP(wp, (2 * co) * 4096, [[64, 64], [4096, 2], [1, 64]])
            p.emit("gpsimd",
                   lambda e, co=co, vl=vl, src=src: e.dma_start(
                       b["pco"][vl * 64:(vl + 1) * 64, co * 128:(co + 1) * 128],
                       src),
                   inc="dp0g", inc_by=16)
    dl0 = p.n("dp0")
    dg0 = p.n("dp0g")

    for wi in range(3):
        sl = b["w16"][:, wi * 16:(wi + 1) * 16]
        p.emit("vector", lambda v, sl=sl: v.reduce_max(
            b["m16"][:, 0:1], sl, axis=AXX, negate=True),
            waits=(("dp0", dl0), ("p0", p.n("p0"))), inc="p0")
        p.emit("scalar", lambda s, sl=sl: s.activation(
            sl, sl, AF.Exp, bias=b["m16"][:, 0:1], scale=1.0,
            accum_out=b["m16"][:, 1:2]),
            waits=(("p0", p.n("p0")),), inc="p0")
        p.emit("vector", lambda v: v.reciprocal(b["m16"][:, 2:3],
                                                b["m16"][:, 1:2]),
               waits=(("p0", p.n("p0")),), inc="p0")
        p.emit("vector", lambda v, sl=sl: v.tensor_scalar(
            sl, sl, b["m16"][:, 2:3], None, ALU.mult),
            waits=(("p0", p.n("p0")),), inc="p0")
        p.emit("vector", lambda v, sl=sl: v.tensor_copy(b["pstage"][:, :], sl),
               waits=(("p0", p.n("p0")), ("p0mm", p.n("p0mm"))), inc="p0")
        p.emit("tensor", lambda t: t.transpose(
            b["ps16"][:, :], b["pstage"][:, :], ident[0:16, 0:16]),
            waits=(("p0", p.n("p0")),), inc="p0mm")
        p.emit("vector", lambda v, wi=wi: v.tensor_copy(
            b["probsT"][:, wi * 16:(wi + 1) * 16], b["ps16"][:, :]),
            waits=(("p0mm", p.n("p0mm")),), inc="p0")

    colbase = {0: 1024, 1: 0, 2: 2048}
    for ci in range(8):
        slab_guard = p.n("dp0s")
        for wi in range(3):
            for co in range(8):
                pbv = p.n("p0")
                src = bass.AP(b["probsT"], wi * 16 + 2 * co,
                              [[48, 16], [1, 2], [0, 64]])
                p.emit("vector", lambda v, src=src: v.tensor_copy(
                    b["pbc"][:, :], src),
                    waits=(("p0", pbv), ("p0mm", p.n("p0mm"))), inc="p0")
                p.emit("tensor", lambda t, ci=ci: t.matmul(
                    b["psA"][:, :], b["esel"][:, ci * 128:(ci + 1) * 128],
                    b["pbc"][:, :], start=True, stop=True),
                    waits=(("p0", p.n("p0")),), inc="p0mm")
                pat = b["pco"][:, co * 128:(co + 1) * 128] if wi == 1                     else b["dpat"][:, :]
                patw = (("dp0g", dg0),) if wi == 1 else (("dp0", dl0),)
                p.emit("vector",
                       lambda v, ci=ci, co=co, wi=wi, pat=pat: v.tensor_tensor(
                           b["slab"][:, colbase[wi] + co * 128:
                                     colbase[wi] + (co + 1) * 128],
                           b["psA"][:, :], pat, ALU.mult),
                       waits=(("p0mm", p.n("p0mm")), ("dp0s", slab_guard))
                       + patw, inc="p0")
        p.dma("sync", aps["l0qkv"][ci * 128:(ci + 1) * 128, :], b["slab"][:, :],
              "dp0s", waits=(("p0", p.n("p0")),))
    p.wait("sync", "dp0s")


def emit_load_ln_hF(p, x_dram, hF, T, b, g_cols, b_cols, x_waits=None,
                    first_guard=()):
    nt = T // 128
    xc0, tmm0 = p.n("xc"), p.n("tmm")
    tev0 = p.n("tev")
    for ct in range(nt):
        sf, sb = b["stage_f"][ct % 2], b["stage_b"][ct % 2]
        dxs = f"dx{ct % 2}"
        w = [("xc", max(xc0, p.n("xc") - 1))]
        if x_waits is not None:
            w += list(x_waits(ct))
        p.dma("sync", sf[:, :], x_dram[ct * 128:(ct + 1) * 128, :], dxs,
              waits=w)
        emit_ln_chunk(p, sf, 1024, b, waits=((dxs, p.n(dxs)),))
        p.emit("scalar",
               lambda s, sf=sf, sb=sb: s.activation(
                   sb[:, :], sf[:, :], AF.Identity, bias=b["nmrs"][:, :],
                   scale=b["rstd"][:, :]),
               waits=(("ln", p.n("ln")), ("tmm", tmm0 + 8 * max(0, ct - 1))),
               inc="xc")
        fg = tuple(first_guard) if ct == 0 else ()
        emit_tiles_to_fmajor(
            p, hF, T, range(NKC), [ct],
            lambda ct2, cd, sb=sb: sb[:, cd * 128:(cd + 1) * 128],
            b, waits=(("xc", p.n("xc")),) + fg, gb=(g_cols, b_cols))
    return tev0


def emit_zf_transpose(p, z_tm, zF, T, b, guard=()):
    nt = T // 128
    emit_tiles_to_fmajor(
        p, zF, T, range(NKC), range(nt),
        lambda ct, cd: z_tm[:, ct * 1024 + cd * 128: ct * 1024 + (cd + 1) * 128],
        b, waits=tuple(guard))


def emit_proj_resid(p, zF, w_sb, bp_row, x_dram, out_dram, T, ps, b,
                    final_ln, w_ready=(), xfb=None, x_waits=None,
                    trailing=True, gb=None):
    nt = T // 128
    uadd0 = p.n("uadd")
    tevv = p.n("tev")
    xcv = p.n("xc")
    tmm0 = p.n("tmm")
    tev0 = p.n("tev")
    zev_tail = p.n("zev")
    ubf_of = {}

    def hand(hct):
        sbb = b["stage_b"][hct % 2]
        tw = (("ubf", ubf_of[hct]),)
        if hct == 0:
            tw = tw + (("zev", zev_tail),)
        else:
            tw = tw + (("tev", tev0 + 8 * hct),)
        emit_tiles_to_fmajor(
            p, xfb, T, range(NKC), [hct],
            lambda ct2, cd, sbb=sbb: sbb[:, cd * 128:(cd + 1) * 128],
            b, waits=tw, gb=gb)
    for ct in range(nt):
        sf = b["stage_f"][ct % 2]
        drs, dws = f"dr{ct % 2}", f"dw{ct % 2}"
        dwp = f"dw{(ct - 1) % 2}"
        w = [("uadd", max(uadd0, p.n("uadd") - 1))]
        if ct < 2:
            w.append(("xc", xcv))
        if x_waits is not None:
            w += list(x_waits(ct))
        p.dma("sync", sf[:, :], x_dram[ct * 128:(ct + 1) * 128, :], drs,
              waits=w)
        guard = p.n("uadd")
        for kc in range(NKC):
            ww = []
            if kc == 0:
                ww = [("uadd", max(uadd0, guard)), ("tev", tevv)] + list(w_ready)
            p.emit("tensor",
                   lambda t, ct=ct, kc=kc: t.matmul(
                       ps[:, 0:512],
                       zF[:, kc * T + ct * 128: kc * T + (ct + 1) * 128],
                       w_sb[:, kc * 1024: kc * 1024 + 512],
                       start=(kc == 0), stop=(kc == NKC - 1)),
                   waits=ww)
            p.emit("tensor",
                   lambda t, ct=ct, kc=kc: t.matmul(
                       ps[:, 512:1024],
                       zF[:, kc * T + ct * 128: kc * T + (ct + 1) * 128],
                       w_sb[:, kc * 1024 + 512: (kc + 1) * 1024],
                       start=(kc == 0), stop=(kc == NKC - 1)),
                   inc="pmm" if kc == NKC - 1 else None)
        mmv = p.n("pmm")
        w2 = (("bc", p.n("bc")),) if ct == 0 else ()
        if xfb is not None:
            w2 = w2 + (("ubf", p.n("ubf")),)
        p.emit("vector",
               lambda v: v.scalar_tensor_tensor(
                   b["ubuf"][:, :], ps[:, 0:1024], 1.0, bp_row[:, 0:1024],
                   ALU.mult, ALU.add),
               waits=(("pmm", mmv), (dwp, p.n(dwp))) + w2, inc="uadd")
        p.emit("vector",
               lambda v, sf=sf: v.tensor_tensor(
                   b["ubuf"][:, :], b["ubuf"][:, :], sf[:, :], ALU.add),
               waits=((drs, p.n(drs)), ("uadd", p.n("uadd"))), inc="uadd")
        if final_ln:
            emit_ln_chunk(p, b["ubuf"], 1024, b, waits=(("uadd", p.n("uadd")),))
            p.emit("scalar",
                   lambda s: s.activation(b["ubuf"][:, :], b["ubuf"][:, :],
                                          AF.Identity, bias=b["nmrs"][:, :],
                                          scale=b["rstd"][:, :]),
                   waits=(("ln", p.n("ln")),), inc="uout")
            p.dma("sync", out_dram[ct * 128:(ct + 1) * 128, :],
                  b["ubuf"][:, :], dws, waits=(("uout", p.n("uout")),))
            if xfb is not None:
                sbb = b["stage_b"][ct % 2]
                p.emit("vector",
                       lambda v, sbb=sbb: v.tensor_copy(sbb[:, :],
                                                        b["ubuf"][:, :]),
                       waits=(("uout", p.n("uout")),
                              ("tmm", tmm0 + 8 * max(0, ct - 1))),
                       inc="ubf")
                ubf_of[ct] = p.n("ubf")
                if ct >= 1:
                    hand(ct - 1)
                if ct == nt - 1:
                    hand(ct)
        else:
            p.dma("sync", out_dram[ct * 128:(ct + 1) * 128, :],
                  b["ubuf"][:, :], dws, waits=(("uadd", p.n("uadd")),))
            if xfb is not None:
                emit_ln_chunk(p, b["ubuf"], 1024, b,
                              waits=(("uadd", p.n("uadd")),))
                sbb = b["stage_b"][ct % 2]
                p.emit("scalar",
                       lambda s, sbb=sbb: s.activation(
                           sbb[:, :], b["ubuf"][:, :], AF.Identity,
                           bias=b["nmrs"][:, :], scale=b["rstd"][:, :]),
                       waits=(("ln", p.n("ln")),
                              ("tmm", tmm0 + 8 * max(0, ct - 1))),
                       inc="ubf")
                ubf_of[ct] = p.n("ubf")
                if ct >= 1:
                    hand(ct - 1)
                if ct == nt - 1:
                    hand(ct)
    if trailing:
        p.wait("sync", "dw0")
        p.wait("sync", "dw1")
    return tev0


def emit_col_load(p, dst_cols, src_dram, cb=None, sem="dbr"):
    n = int(src_dram.shape[0]) // 128
    if cb is None:
        src2 = src_dram.rearrange("(c p) -> p c", p=128)
        p.emit("sync", lambda e: e.dma_start(dst_cols[:, 0:n], src2,
                                             allow_slow_non_contiguous=True),
               inc=sem, inc_by=16)
        return
    cr, idf, pscl = cb["_colrow"], cb["_identf_sb"], cb["ps_cl"]
    src2 = src_dram.rearrange("(c p) -> c p", p=128)
    p.dma("sync", cr[0:n, :], src2, "dcl",
          waits=(("clm", p.n("clm")),))
    p.emit("tensor", lambda t, n=n: t.transpose(
        pscl[:, 0:n], cr[0:n, :], idf[0:n, 0:n]),
        waits=(("dcl", p.n("dcl")), ("clv", p.n("clv"))), inc="clm")
    p.emit("vector", lambda v, n=n: v.tensor_copy(dst_cols[:, 0:n],
                                                  pscl[:, 0:n]),
           waits=(("clm", p.n("clm")),), inc="clv")



def alloc_moe_bufs(nc, st):
    sb = lambda n, shp, dt: st.enter_context(nc.sbuf_tensor(_u(n), shp, dt))
    ps = lambda n, shp, dt=F32: st.enter_context(nc.psum_tensor(_u(n), shp, dt))
    return {
        "w1": sb("mo_w1", [128, 8 * 4096], BF16),
        "w2": sb("mo_w2", [128, 32 * 1024], BF16),
        "mid": sb("mo_mid", [128, 32 * 256], BF16),
        "gw": sb("mo_gw", [128, 8 * 8], BF16),
        "glog": sb("mo_glog", [128, 128], F32),
        "wcols": sb("mo_wcols", [128, 128], F32),
        "gsc": sb("mo_gsc", [128, 12], F32),
        "wb8": sb("mo_wb8", [128, 8], BF16),
        "wT": sb("mo_wT", [8, 128], BF16),
        "b2m": sb("mo_b2m", [8, 1024], BF16),
        "b1a": sb("mo_b1a", [128, 8 * 32], F32),
        "gbrow": sb("mo_gbrow", [128, 8], F32),
        "ubuf2": sb("mo_ubuf2", [128, 1024], F32),
        "ps_mid": ps("mo_psmid", [128, 1024]),
        "ps_y": ps("mo_psy", [128, 2048]),
    }


def emit_moe(p, cfg, b, cb):
    T = cfg["T"]
    nt = T // 128
    n256 = T // 256
    h2F = cfg["h2F"]
    aps = cfg["aps"]
    hready = tuple(cfg.get("h2f_ready", ()))
    ps_g = b["ps_mid"]
    cache = cfg.get("w_cache")
    cmode = cfg.get("cache_mode")

    for kc in range(NKC):
        p.dma("gpsimd", b["gw"][:, kc * 8:(kc + 1) * 8],
              aps["gate_W"][kc * 128:(kc + 1) * 128, :], "dgw")
    p.dma("gpsimd", b["b2m"][:, :], aps["exp_b2"][:, :], "dgw")
    cb["ps_bc"] = b["ps_y"][:, 0:512]
    emit_row_bcast(p, b["gbrow"], aps["gate_b"], 8, cb)
    cb.pop("ps_bc")

    def load_w1(e, waits):
        if cmode == "read":
            for kc in range(NKC):
                p.dma("sync", b["w1"][:, kc * 4096:(kc + 1) * 4096],
                      cache[0][(e * 8 + kc) * 128:(e * 8 + kc + 1) * 128, :],
                      "dl1r", waits=waits if kc == 0 else ())
            return ("dl1r", p.n("dl1r"))
        if cmode == "write":
            waits = tuple(waits) + (("dc1", p.n("dc1")),)
        for kc in range(NKC):
            p.dma("gpsimd", b["w1"][:, kc * 4096:(kc + 1) * 4096],
                  bass.AP(aps["exp_W1"].tensor, e * 1024 * 4096 + kc * 128 * 4096,
                          [[4096, 128], [1, 4096]]),
                  "dlw1", waits=waits if kc == 0 else ())
        if cmode == "write":
            lv = p.n("dlw1")
            for kc in range(NKC):
                p.dma("sync", cache[0][(e * 8 + kc) * 128:(e * 8 + kc + 1) * 128, :],
                      b["w1"][:, kc * 4096:(kc + 1) * 4096], "dc1",
                      waits=(("dlw1", lv),) if kc == 0 else ())
        return ("dlw1", p.n("dlw1"))

    def load_w2(e, waits):
        if cmode == "read":
            for g in range(8):
                p.dma("sync", b["w2"][:, g * 4096:(g + 1) * 4096],
                      cache[1][(e * 8 + g) * 128:(e * 8 + g + 1) * 128, :],
                      "dl2r", waits=waits if g == 0 else ())
            return ("dl2r", p.n("dl2r"))
        if cmode == "write":
            waits = tuple(waits) + (("dc2", p.n("dc2")),)
        for g in range(8):
            src = bass.AP(aps["exp_W2"].tensor,
                          e * 4096 * 1024 + g * 512 * 1024,
                          [[1024, 128], [128 * 1024, 4], [1, 1024]])
            p.dma("gpsimd", b["w2"][:, g * 4096:(g + 1) * 4096], src,
                  "dlw2", waits=waits if g == 0 else ())
        if cmode == "write":
            lv = p.n("dlw2")
            for g in range(8):
                p.dma("sync", cache[1][(e * 8 + g) * 128:(e * 8 + g + 1) * 128, :],
                      b["w2"][:, g * 4096:(g + 1) * 4096], "dc2",
                      waits=(("dlw2", lv),) if g == 0 else ())
        return ("dlw2", p.n("dlw2"))

    def load_b1_all():
        cr, idf = cb["_colrow"], cb["_identf_sb"]
        for e in range(NE):
            src = aps["exp_b1"].tensor
            ap = bass.AP(src, e * 4096, [[128, 32], [1, 128]])
            p.dma("sync", cr[0:32, :], ap, "db1d",
                  waits=(("b1t", p.n("b1t")),))
            p.emit("tensor", lambda t: t.transpose(
                ps_g[:, 0:32], cr[0:32, :], idf[0:32, 0:32]),
                waits=(("db1d", p.n("db1d")), ("db1", p.n("db1"))),
                inc="b1t")
            p.emit("vector", lambda v, e=e: v.tensor_copy(
                b["b1a"][:, e * 32:(e + 1) * 32], ps_g[:, 0:32]),
                waits=(("b1t", p.n("b1t")),), inc="db1")

    load_b1_all()
    b1_done = p.n("db1")
    w1sem = load_w1(0, ())
    w2sem = load_w2(0, ())

    xft = cfg.get("h2f_tev")
    for tt in range(nt):
        for kc in range(NKC):
            w = []
            if tt == 0 and kc == 0:
                w = [("dgw", p.n("dgw")), ("db1", b1_done)] + list(hready)
            if kc == 0 and xft is not None:
                w.append(("tev", xft + 8 * (tt + 1)))
            p.emit("tensor",
                   lambda t, tt=tt, kc=kc: t.matmul(
                       ps_g[:, tt * 8:(tt + 1) * 8],
                       h2F[:, kc * T + tt * 128: kc * T + (tt + 1) * 128],
                       b["gw"][:, kc * 8:(kc + 1) * 8],
                       start=(kc == 0), stop=(kc == NKC - 1)),
                   waits=w, inc="gmm" if kc == NKC - 1 else None)
    gmmv = p.n("gmm")
    p.emit("vector", lambda v: v.scalar_tensor_tensor(
        b["glog"][:, 0:nt * 8], ps_g[:, 0:nt * 8], 1.0,
        sb_ap(b["gbrow"], 0, [[0, nt], [1, 8]]), ALU.mult, ALU.add),
        waits=(("gmm", gmmv), ("bc", p.n("bc"))), inc="gv")
    glog_done = p.n("gv")
    gv_tile = {}
    gs = b["gsc"]
    for tt in range(nt):
        lg = b["glog"][:, tt * 8:(tt + 1) * 8]
        wc = b["wcols"][:, tt * 8:(tt + 1) * 8]
        p.emit("vector", lambda v, lg=lg: v.reduce_max(
            gs[:, 0:1], lg, axis=AXX, negate=True),
            waits=(("gv", p.n("gv")),), inc="gv")
        p.emit("vector", lambda v, lg=lg: v.tensor_scalar(
            gs[:, 2:3], gs[:, 0:1], -1.0, None, ALU.mult),
            waits=(("gv", p.n("gv")),), inc="gv")
        p.emit("vector", lambda v, lg=lg: v.tensor_scalar(
            b["glog"][:, 120:128] if False else gs[:, 4:12], lg, gs[:, 2:3],
            None, ALU.is_equal),
            waits=(("gv", p.n("gv")),), inc="gv")
        p.emit("vector", lambda v, lg=lg: v.scalar_tensor_tensor(
            gs[:, 4:12], gs[:, 4:12], -1e30, lg, ALU.mult, ALU.add),
            waits=(("gv", p.n("gv")),), inc="gv")
        p.emit("vector", lambda v: v.reduce_max(gs[:, 1:2], gs[:, 4:12],
                                                axis=AXX),
               waits=(("gv", p.n("gv")),), inc="gv")
        p.emit("scalar", lambda s, lg=lg, wc=wc: s.activation(
            wc, lg, AF.Exp, bias=gs[:, 0:1], scale=1.0),
            waits=(("gv", p.n("gv")),), inc="gv")
        p.emit("scalar", lambda s: s.activation(
            gs[:, 3:4], gs[:, 1:2], AF.Exp, bias=gs[:, 0:1], scale=1.0),
            waits=(("gv", p.n("gv")),), inc="gv")
        p.emit("vector", lambda v: v.tensor_scalar(
            gs[:, 3:4], gs[:, 3:4], 1.0, None, ALU.add),
            waits=(("gv", p.n("gv")),), inc="gv")
        p.emit("vector", lambda v: v.reciprocal(gs[:, 3:4], gs[:, 3:4]),
               waits=(("gv", p.n("gv")),), inc="gv")
        p.emit("vector", lambda v, lg=lg: v.tensor_scalar(
            gs[:, 4:12], lg, gs[:, 1:2], None, ALU.is_ge),
            waits=(("gv", p.n("gv")),), inc="gv")
        p.emit("vector", lambda v, wc=wc: v.tensor_tensor(
            wc, wc, gs[:, 4:12], ALU.mult),
            waits=(("gv", p.n("gv")),), inc="gv")
        p.emit("vector", lambda v, wc=wc: v.tensor_scalar(
            wc, wc, gs[:, 3:4], None, ALU.mult),
            waits=(("gv", p.n("gv")),), inc="gv")
        gv_tile[tt] = p.n("gv")
    gv_done = p.n("gv")

    def emit_acc_init():
        uadd0 = p.n("uadd")
        x1w, x2w = cfg.get("x1_waits"), cfg.get("x2_waits")
        for ct in range(nt):
            sf, sf2 = cb["stage_f"][0], cb["stage_f"][1]
            w = [("uadd", max(uadd0, p.n("uadd") - 2))]
            if ct == 0:
                w.append(("xc", p.n("xc")))
            w1x = w + (list(x1w(ct)) if x1w else [])
            w2x = w + (list(x2w(ct)) if x2w else [])
            p.dma("sync", sf[:, :], cfg["x1_dram"][ct * 128:(ct + 1) * 128, :],
                  f"dr0", waits=w1x)
            p.dma("sync", sf2[:, :], cfg["x2_dram"][ct * 128:(ct + 1) * 128, :],
                  f"dr1", waits=w2x)
            p.emit("vector", lambda v, ct=ct: v.tensor_copy(
                b["wb8"][:, :], b["wcols"][:, ct * 8:(ct + 1) * 8]),
                waits=(("gv", gv_tile[ct]), ("wtm", p.n("wtm"))), inc="wte")
            p.emit("tensor", lambda t: t.transpose(
                cb["psT"][0:8, 0:128], b["wb8"][:, :], cb["ident"][:, :]),
                waits=(("wte", p.n("wte")), ("bmm2", p.n("bmm2"))), inc="wtm")
            p.emit("vector", lambda v: v.tensor_copy(
                b["wT"][:, :], cb["psT"][0:8, 0:128]),
                waits=(("wtm", p.n("wtm")),), inc="wte")
            p.emit("tensor", lambda t, ct=ct: t.matmul(
                b["ps_y"][:, 0:512], b["wT"][:, :],
                b["b2m"][:, 0:512], start=True, stop=True),
                waits=(("wte", p.n("wte")), ("dgw", p.n("dgw")),
                       ("bc", p.n("bc")), ("uadd", p.n("uadd"))))
            p.emit("tensor", lambda t, ct=ct: t.matmul(
                b["ps_y"][:, 512:1024], b["wT"][:, :],
                b["b2m"][:, 512:1024], start=True, stop=True), inc="bmm2")
            p.emit("vector",
                   lambda v, sf=sf, sf2=sf2: v.tensor_tensor(
                       cb["ubuf"][:, :], sf[:, :], sf2[:, :], ALU.add),
                   waits=(("dr0", p.n("dr0")), ("dr1", p.n("dr1")),
                          ("dwa", p.n("dwa")), ("bmm2", p.n("bmm2"))),
                   inc="uadd")
            p.emit("vector",
                   lambda v: v.tensor_tensor(cb["ubuf"][:, :], cb["ubuf"][:, :],
                                             b["ps_y"][:, 0:1024], ALU.add),
                   waits=(("uadd", p.n("uadd")),), inc="uadd")
            p.dma("sync", cfg["acc_dram"][ct * 128:(ct + 1) * 128, :],
                  cb["ubuf"][:, :], "dwa", waits=(("uadd", p.n("uadd")),))
        return p.n("dwa")


    dacc_prev = p.n("dacc")
    acc_init = None
    for e in range(NE):
        w1v, w2v = w1sem, w2sem
        mm1_first = True
        for tt in range(n256):
            for m in range(32):
                slot = (m % 2) * 512
                guard = ("gel", max(0, p.n("gel") - 1))
                for kc in range(NKC):
                    w = [guard] if kc == 0 else []
                    if mm1_first and kc == 0:
                        w += [w1v, ("db1", b1_done), ("gv", glog_done)]
                        mm1_first = False
                    if e == 0 and m == 0 and kc == 0 and xft is not None:
                        w.append(("tev", xft + 8 * min(nt, 2 * (tt + 1))))
                    p.emit("tensor",
                           lambda t, tt=tt, m=m, kc=kc, slot=slot: t.matmul(
                               b["ps_mid"][:, slot:slot + 256],
                               b["w1"][:, kc * 4096 + m * 128:
                                       kc * 4096 + (m + 1) * 128],
                               h2F[:, kc * T + tt * 256: kc * T + tt * 256 + 256],
                               start=(kc == 0), stop=(kc == NKC - 1)),
                           waits=w, inc="mm1" if kc == NKC - 1 else None)
                p.emit("scalar",
                       lambda s, m=m, slot=slot, e=e: s.activation(
                           b["mid"][:, m * 256:(m + 1) * 256],
                           b["ps_mid"][:, slot:slot + 256], AF.Gelu,
                           bias=b["b1a"][:, e * 32 + m:e * 32 + m + 1],
                           scale=1.0),
                       waits=(("mm1", p.n("mm1")), ("mm2", max(0, p.n("mm2") - 0)),), inc="gel")
            gelv = p.n("gel")
            if e == 0 and tt == 0:
                acc_init = emit_acc_init()
                cfg["_uadd_ps"] = p.n("uadd")
                cfg["_wte"] = p.n("wte")
            for ts in range(2):
                ysl = b["ps_y"][:, ts * 1024:(ts + 1) * 1024]
                for m in range(32):
                    w = []
                    if m == 0:
                        w = [("gel", gelv), ("yev", max(0, p.n("yev") - 1)),
                             w2v, ("dwa", p.n("dwa"))]
                    p.emit("tensor",
                           lambda t, ts=ts, m=m, ysl=ysl: t.matmul(
                               ysl[:, 0:512],
                               b["mid"][:, m * 256 + ts * 128:
                                        m * 256 + ts * 128 + 128],
                               b["w2"][:, m * 1024: m * 1024 + 512],
                               start=(m == 0), stop=(m == 31)),
                           waits=w)
                    p.emit("tensor",
                           lambda t, ts=ts, m=m, ysl=ysl: t.matmul(
                               ysl[:, 512:1024],
                               b["mid"][:, m * 256 + ts * 128:
                                        m * 256 + ts * 128 + 128],
                               b["w2"][:, m * 1024 + 512: (m + 1) * 1024],
                               start=(m == 0), stop=(m == 31)),
                           inc="mm2" if m == 31 else None)
                mm2v = p.n("mm2")
                tile = tt * 2 + ts
                p.emit("vector",
                       lambda v, ysl=ysl, tile=tile, e=e: v.tensor_scalar(
                           b["ubuf2"][:, :], ysl[:, 0:1024],
                           b["wcols"][:, tile * 8 + e: tile * 8 + e + 1],
                           None, ALU.mult),
                       waits=(("mm2", mm2v), ("gv", gv_tile[tile]),
                              ("dacc", p.n("dacc"))), inc="yev")
                p.emit("gpsimd",
                       lambda g, tile=tile: g.dma_start(
                           cfg["acc_dram"][tile * 128:(tile + 1) * 128, :],
                           b["ubuf2"][:, :], accum_op=ALU.add),
                       waits=(("yev", p.n("yev")), ("dwa", acc_init),
                              ("dacc", dacc_prev)),
                       inc="dacc", inc_by=16)
        dacc_prev = p.n("dacc")
        if e + 1 < NE:
            w1sem = load_w1(e + 1, (("mm1", p.n("mm1")),))
            w2sem = load_w2(e + 1, (("mm2", p.n("mm2")),))

    uadd0 = p.n("uadd")
    dacc_fin = p.n("dacc")
    xfb = cfg.get("xfb")
    tmm0 = p.n("tmm")
    tev0 = p.n("tev")
    for ct in range(nt):
        sf = cb["stage_f"][ct % 2]
        drs, dws = f"dr{ct % 2}", f"dw{ct % 2}"
        dwp = f"dw{(ct - 1) % 2}"
        w = [("uadd", max(uadd0, p.n("uadd") - 1)),
             ("dacc", dacc_fin - 16 * (nt - 1 - ct))]
        p.dma("sync", sf[:, :], cfg["acc_dram"][ct * 128:(ct + 1) * 128, :],
              drs, waits=w)
        emit_ln_chunk(p, sf, 1024, cb, waits=((drs, p.n(drs)),))
        wbf = (("ubf", p.n("ubf")),) if xfb is not None else ()
        p.emit("scalar",
               lambda s, sf=sf: s.activation(cb["ubuf"][:, :], sf[:, :],
                                             AF.Identity,
                                             bias=cb["nmrs"][:, :],
                                             scale=cb["rstd"][:, :]),
               waits=(("ln", p.n("ln")), (dwp, p.n(dwp))) + wbf, inc="uadd")
        p.dma("sync", cfg["out_dram"][ct * 128:(ct + 1) * 128, :],
              cb["ubuf"][:, :], dws, waits=(("uadd", p.n("uadd")),))
        if xfb is not None:
            sbb = cb["stage_b"][ct % 2]
            p.emit("vector",
                   lambda v, sbb=sbb: v.tensor_copy(sbb[:, :],
                                                    cb["ubuf"][:, :]),
                   waits=(("uadd", p.n("uadd")),
                          ("tmm", tmm0 + 8 * max(0, ct - 1))),
                   inc="ubf")
            emit_tiles_to_fmajor(
                p, xfb, T, range(NKC), [ct],
                lambda ct2, cd, sbb=sbb: sbb[:, cd * 128:(cd + 1) * 128],
                cb, waits=(("ubf", p.n("ubf")),))
    if cfg.get("trailing", True):
        p.wait("sync", "dw0")
        p.wait("sync", "dw1")
    if cmode == "write":
        p.wait("sync", "dc1")
        p.wait("sync", "dc2")
    return tev0


def emit_cast_to_scratch(p, src_dram, dst_dram, rows, cols, stg, sem="dsc"):
    assert rows % 128 == 0
    for i in range(rows // 128):
        g = stg[i % 2]
        p.dma("gpsimd", g[:, 0:cols], src_dram[i * 128:(i + 1) * 128, :],
              f"dcc{i % 2}", waits=((sem + str(i % 2), p.n(sem + str(i % 2))),))
        p.dma("sync", dst_dram[i * 128:(i + 1) * 128, :], g[:, 0:cols],
              sem + str(i % 2), waits=((f"dcc{i % 2}", p.n(f"dcc{i % 2}")),))
    p.wait("sync", sem + "0")
    p.wait("sync", sem + "1")


def emit_trload_slab(p, dst_sb, src_dram_bf, n_out, kc_range, sem="dtw"):
    for kc in kc_range:
        p.dma_tr(dst_sb[:, kc * n_out:(kc + 1) * n_out],
                 src_dram_bf[0:n_out, kc * 128:(kc + 1) * 128], sem)



def emit_absmean_quant(p, src_dram, dst_dram, rows, cols, b, cb, tag):
    n128 = rows // 128
    sA, sB = b["qs"]
    acc = b["qacc"]
    dsem = "dq" + tag
    p.emit("vector", lambda v: v.memset(acc[:, :], 0.0),
           waits=(("qv", p.n("qv")),), inc="qv")
    p.emit("vector", lambda v: v.memset(b["qone"][:, :], 1.0), inc="qv")
    for i in range(n128):
        g = sA if i % 2 == 0 else sB
        ds = dsem + str(i % 2)
        p.dma("sync", g[:, 0:cols], src_dram[i * 128:(i + 1) * 128, :], ds,
              waits=(("qv", max(0, p.n("qv") - 2)),))
        p.emit("vector",
               lambda v, g=g: v.reduce_sum(b["thr"][:, :], g[:, 0:cols],
                                           axis=AXX,
                                           apply_absolute_value=True),
               waits=((ds, p.n(ds)), ("qv", p.n("qv"))), inc="qv")
        p.emit("vector",
               lambda v: v.tensor_tensor(acc[:, :], acc[:, :], b["thr"][:, :],
                                         ALU.add),
               waits=(("qv", p.n("qv")),), inc="qv")
    p.emit("tensor", lambda t: t.matmul(
        cb["ps_bc"][0:1, 0:1], acc[:, :], b["qone"][:, :],
        start=True, stop=True),
        waits=(("qv", p.n("qv")),), inc="qmm")
    p.emit("vector", lambda v: v.memset(b["q11"][:, 0:1], 0.5e-6),
           waits=(("qmm", p.n("qmm")),), inc="qv")
    p.emit("vector", lambda v, N=rows * cols: v.scalar_tensor_tensor(
        b["q11"][:, 1:2], cb["ps_bc"][0:1, 0:1], 0.5 / N, b["q11"][:, 0:1],
        ALU.mult, ALU.add),
        waits=(("qv", p.n("qv")),), inc="qv")
    p.emit("vector", lambda v: v.memset(cb["ones1"][:, :], 1.0),
           waits=(("bc", p.n("bc")),), inc="bc")
    p.emit("tensor", lambda t: t.matmul(
        cb["ps_bc"][:, 0:1], cb["ones1"][:, :], b["q11"][:, 1:2],
        start=True, stop=True),
        waits=(("qv", p.n("qv")), ("bc", p.n("bc"))), inc="qmm")
    p.emit("scalar", lambda s: s.copy(b["thr"][:, :], cb["ps_bc"][:, 0:1]),
           waits=(("qmm", p.n("qmm")),), inc="qv")
    thr_done = p.n("qv")
    for i in range(n128):
        g = sA if i % 2 == 0 else sB
        gm = b["qm"]
        ds = dsem + str(i % 2)
        p.dma("sync", g[:, 0:cols], src_dram[i * 128:(i + 1) * 128, :], ds,
              waits=(("qv", max(thr_done, p.n("qv") - 3)),))
        p.emit("scalar", lambda s, g=g: s.activation(
            b["qm"][:, 0:cols], g[:, 0:cols], AF.Abs),
            waits=((ds, p.n(ds)), ("qv", p.n("qv"))), inc="qv")
        p.emit("vector", lambda v: v.tensor_scalar(
            b["qm"][:, 0:cols], b["qm"][:, 0:cols], b["thr"][:, :], None,
            ALU.is_gt),
            waits=(("qv", p.n("qv")), ("qmm", p.n("qmm"))), inc="qv")
        p.emit("scalar", lambda s, g=g: s.sign(g[:, 0:cols], g[:, 0:cols]),
               waits=(("qv", p.n("qv") - 1),), inc="qv")
        p.emit("vector", lambda v, g=g: v.tensor_tensor(
            b["qb"][:, 0:cols], g[:, 0:cols], b["qm"][:, 0:cols], ALU.mult),
            waits=(("qv", p.n("qv")),), inc="qv")
        p.dma("sync", dst_dram[i * 128:(i + 1) * 128, :], b["qb"][:, 0:cols],
              "dqo" + tag, waits=(("qv", p.n("qv")),))
        p.wait("vector", "dqo" + tag)
    p.wait("sync", "dqo" + tag)


def alloc_quant_bufs(nc, st):
    sb = lambda n, shp, dt: st.enter_context(nc.sbuf_tensor(_u(n), shp, dt))
    return {
        "qs": [sb("q_s0", [128, 4096], F32), sb("q_s1", [128, 4096], F32)],
        "qm": sb("q_m", [128, 4096], F32),
        "qb": sb("q_b", [128, 4096], BF16),
        "qacc": sb("q_acc", [128, 1], F32),
        "qone": sb("q_one", [128, 1], F32),
        "q11": sb("q_11", [1, 2], F32),
        "thr": sb("q_thr", [128, 1], F32),
    }


def emit_bitffn(p, cfg, cb):
    T = cfg["T"]
    nt = T // 128
    xF = cfg["xF"]
    b = cfg["bufs"]
    wready = tuple(cfg.get("w_ready", ()))
    uadd0 = p.n("uadd")
    xcv = p.n("xc")
    fmm10, fgg0, fgn0 = p.n("fmm1"), p.n("fgg"), p.n("fgn")
    fmm20, tmm0, tev0 = p.n("fmm2"), p.n("tmm"), p.n("tev")
    xft = cfg.get("xf_tev")

    def emit_mm1(ct):
        sf = cb["stage_f"][ct % 2]
        w = [("uadd", max(uadd0, p.n("uadd") - 1))]
        if ct < 2:
            w.append(("xc", xcv))
        if cfg.get("x_waits") is not None:
            w += list(cfg["x_waits"](ct))
        p.dma("sync", sf[:, :], cfg["x_dram"][ct * 128:(ct + 1) * 128, :],
              f"dr{ct % 2}", waits=w)
        for qd in range(4):
            slot = (qd % 2) * 1024
            gslot = fgg0 + 4 * ct + qd - 1
            for kc in range(NKC):
                ww = []
                if kc == 0 and gslot > fgg0:
                    ww.append(("fgg", gslot))
                if ct == 0 and qd == 0 and kc == 0:
                    ww += list(wready)
                if qd == 0 and kc == 0 and xft is not None:
                    ww.append(("tev", xft + 8 * (ct + 1)))
                for h2 in range(2):
                    p.emit("tensor",
                           lambda t, ct=ct, qd=qd, kc=kc, h2=h2, slot=slot:
                           t.matmul(
                               b["ps_h"][:, slot + h2 * 512: slot + (h2 + 1) * 512],
                               xF[:, kc * T + ct * 128: kc * T + (ct + 1) * 128],
                               cfg["wfc_sb"][:, kc * 4096 + qd * 1024 + h2 * 512:
                                             kc * 4096 + qd * 1024 + (h2 + 1) * 512],
                               start=(kc == 0), stop=(kc == NKC - 1)),
                           waits=ww,
                           inc="fmm1" if (kc == NKC - 1 and h2 == 1) else None)
                    ww = []
            htm = b["h_tm"][ct % 2]
            p.emit("scalar",
                   lambda s, ct=ct, qd=qd, slot=slot, htm=htm: s.activation(
                       htm[:, qd * 1024:(qd + 1) * 1024],
                       b["ps_h"][:, slot:slot + 1024], AF.Gelu),
                   waits=(("fmm1", fmm10 + 4 * ct + qd + 1),
                          ("tmm", tmm0 + 32 * max(0, ct - 1))), inc="fgg")
        htm = b["h_tm"][ct % 2]
        emit_ln_chunk(p, htm, 4096, cb,
                      waits=(("fgg", fgg0 + 4 * (ct + 1)),),
                      dest=("rstd2", "nmrs2"))
        p.emit("scalar",
               lambda s, htm=htm: s.activation(htm[:, :], htm[:, :],
                                               AF.Identity,
                                               bias=cb["nmrs2"][:, :],
                                               scale=cb["rstd2"][:, :]),
               waits=(("ln", p.n("ln")),), inc="fgn")

    def emit_mm2(ct):
        sf = cb["stage_f"][ct % 2]
        drs, dws = f"dr{ct % 2}", f"dw{ct % 2}"
        dwp = f"dw{(ct - 1) % 2}"
        htm = b["h_tm"][ct % 2]
        fg = tuple(cfg.get("psT_ready", ())) if ct == 0 else ()
        emit_tiles_to_fmajor(
            p, b["hF_t"], 128, range(32), [0],
            lambda ct2, cd, htm=htm: htm[:, cd * 128:(cd + 1) * 128],
            cb, waits=(("fgn", fgn0 + ct + 1),
                       ("fmm2", max(fmm20, fmm20 + ct))) + fg,
            gb=(cfg["lnw_cols"], cfg["lnb_cols"]))
        for kc2 in range(32):
            ww = []
            if kc2 == 0:
                ww = [("tev", p.n("tev")), ("uadd", max(uadd0, p.n("uadd")))]
            for h2 in range(2):
                p.emit("tensor",
                       lambda t, kc2=kc2, h2=h2: t.matmul(
                           b["ps_o"][:, h2 * 512:(h2 + 1) * 512],
                           b["hF_t"][:, kc2 * 128:(kc2 + 1) * 128],
                           cfg["wproj_sb"][:, kc2 * 1024 + h2 * 512:
                                           kc2 * 1024 + (h2 + 1) * 512],
                           start=(kc2 == 0), stop=(kc2 == 31)),
                       waits=ww,
                       inc="fmm2" if (kc2 == 31 and h2 == 1) else None)
                ww = []
        mm2v = p.n("fmm2")
        p.emit("vector",
               lambda v, sf=sf: v.tensor_tensor(
                   cb["ubuf"][:, :], sf[:, :], b["ps_o"][:, 0:1024], ALU.add),
               waits=(("fmm2", mm2v), (drs, p.n(drs)), (dwp, p.n(dwp))),
               inc="uadd")
        emit_ln_chunk(p, cb["ubuf"], 1024, cb, waits=(("uadd", p.n("uadd")),))
        p.emit("scalar",
               lambda s: s.activation(cb["ubuf"][:, :], cb["ubuf"][:, :],
                                      AF.Identity, bias=cb["nmrs"][:, :],
                                      scale=cb["rstd"][:, :]),
               waits=(("ln", p.n("ln")),), inc="uout")
        p.dma("sync", cfg["out_dram"][ct * 128:(ct + 1) * 128, :],
              cb["ubuf"][:, :], dws, waits=(("uout", p.n("uout")),))

    for ct in range(nt):
        emit_mm1(ct)
        if ct >= 1:
            emit_mm2(ct - 1)
    emit_mm2(nt - 1)
    if cfg.get("trailing", True):
        p.wait("sync", "dw0")
        p.wait("sync", "dw1")

WNAMES = [
    ("L0_attn_W_K", [16, 16]), ("L0_attn_W_Q", [16, 16]),
    ("L0_attn_W_V", [16, 16]), ("L0_attn_W_pred", [16, 64, 64]),
    ("L0_ffn_ln1_g", [D]), ("L0_ffn_ln1_b", [D]),
    ("L0_ffn_ln2_g", [D]), ("L0_ffn_ln2_b", [D]),
    ("L0_ffn_Wq", [D, D]), ("L0_ffn_bq", [D]),
    ("L0_ffn_Wk", [D, D]), ("L0_ffn_bk", [D]),
    ("L0_ffn_Wv", [D, D]), ("L0_ffn_bv", [D]),
    ("L0_ffn_Wp", [D, D]), ("L0_ffn_bp", [D]),
    ("L0_ffn_gate_W", [D, NE]), ("L0_ffn_gate_b", [NE]),
    ("L0_ffn_exp_W1", [NE, D, HID]), ("L0_ffn_exp_b1", [NE, HID]),
    ("L0_ffn_exp_W2", [NE, HID, D]), ("L0_ffn_exp_b2", [NE, D]),
    ("L1_attn_qkv_w", [3 * D, D]), ("L1_attn_qkv_b", [3 * D]),
    ("L1_attn_proj_w", [D, D]), ("L1_attn_proj_b", [D]),
    ("L1_ffn_W_fc", [HID, D]), ("L1_ffn_ln_w", [HID]),
    ("L1_ffn_ln_b", [HID]), ("L1_ffn_W_proj", [D, HID]),
]


def build_full(T=2048, n_seq=4):
    nc = bass.Bass("TRN2", target_bir_lowering=False)
    a = {}
    a["x_in"] = nc.dram_tensor("x_in", [n_seq * T, D], F32,
                               kind="ExternalInput").ap()
    for n, shp in WNAMES:
        a[n] = nc.dram_tensor(n, shp, F32, kind="ExternalInput").ap()
    for n, shp in (("cst_ident", [128, 128]), ("cst_tri", [128, 128]),
                   ("cst_esel", [16, 8 * 128]), ("cst_dpat", [128, 128])):
        a[n] = nc.dram_tensor(n, shp, BF16, kind="ExternalInput").ap()
    a["cst_identf"] = nc.dram_tensor("cst_identf", [128, 128], F32,
                                     kind="ExternalInput").ap()
    a["x_out"] = nc.dram_tensor("x_out", [n_seq * T, D], F32,
                                kind="ExternalOutput").ap()
    a["l0qkv"] = nc.dram_tensor("l0qkv", [D, 3 * D], BF16).ap()
    a["qkvbf"] = nc.dram_tensor("qkvbf", [3 * D, D], BF16).ap()
    a["projbf"] = nc.dram_tensor("projbf", [D, D], BF16).ap()
    a["wfcq"] = nc.dram_tensor("wfcq", [HID, D], BF16).ap()
    a["wprq"] = nc.dram_tensor("wprq", [D, HID], BF16).ap()
    xs = [nc.dram_tensor(f"xs{i}", [T, D], F32).ap() for i in range(4)]
    acc = nc.dram_tensor("acc_s", [T, D], F32).ap()
    w1bf = nc.dram_tensor("w1bf", [NE * 8 * 128, 4096], BF16).ap()
    w2bf = nc.dram_tensor("w2bf", [NE * 8 * 128, 4096], BF16).ap()
    xbf_in = nc.dram_tensor("xbf_in", [n_seq * T, D], BF16).ap()

    with ExitStack() as st:
        p = Prog(nc, st)
        cb = alloc_common_bufs(nc, st)
        cb["ident"] = cb["_ident_sb"]
        p.dma("sync", cb["_ident_sb"][:, :], a["cst_ident"][:, :], "dcst")
        p.dma("sync", cb["_tri_sb"][:, :], a["cst_tri"][:, :], "dcst")
        p.dma("sync", cb["_identf_sb"][:, :], a["cst_identf"][:, :], "dcst")
        p.emit("vector", lambda v: v.memset(cb["eps_col"][:, :], LN_EPS),
               inc="ev")
        p.wait("tensor", "dcst")
        p.wait("scalar", "ev")
        p.wait("vector", "ev")

        with ExitStack() as st0:
            b0 = alloc_p0_bufs(nc, st0)
            emit_phase0_l0(p, a, b0, cb["_ident_sb"])
            p.flush("p0a")
        with ExitStack() as st0:
            cstg = [st0.enter_context(nc.sbuf_tensor(_u("cst0"), [128, 4096], BF16)),
                    st0.enter_context(nc.sbuf_tensor(_u("cst1"), [128, 4096], BF16))]
            emit_cast_to_scratch(p, a["L1_attn_qkv_w"], a["qkvbf"], 3 * D, D,
                                 cstg)
            emit_cast_to_scratch(p, a["L1_attn_proj_w"], a["projbf"], D, D,
                                 cstg)
            emit_cast_to_scratch(p, a["x_in"], xbf_in, n_seq * T, D, cstg)
            p.flush("p0b")
        with ExitStack() as st0:
            qb = alloc_quant_bufs(nc, st0)
            cb["ps_bc"] = st0.enter_context(
                nc.psum_tensor(_u("psbc"), [128, 512], F32))
            emit_absmean_quant(p, a["L1_ffn_W_fc"], a["wfcq"], HID, D, qb, cb,
                               "f")
            emit_absmean_quant(p, a["L1_ffn_W_proj"], a["wprq"], D, HID, qb,
                               cb, "p")
            p.flush("p0q")
            cb.pop("ps_bc")

        sbg = lambda stk, n, shp, dt: stk.enter_context(
            nc.sbuf_tensor(_u(n), shp, dt))

        gcols = st.enter_context(nc.sbuf_tensor("g_cols", [128, 128], F32))
        with ExitStack() as stR:
            cb["ps_cl"] = stR.enter_context(
                nc.psum_tensor(_u("psbc"), [128, 32], F32))
            for (lo, hi, name, sl) in (
                    (0, 8, "L0_ffn_bq", None), (8, 16, "L0_ffn_bk", None),
                    (16, 24, "L0_ffn_ln1_g", None),
                    (24, 32, "L0_ffn_ln1_b", None),
                    (32, 40, "L0_ffn_ln2_g", None),
                    (40, 48, "L0_ffn_ln2_b", None),
                    (48, 56, "L1_attn_qkv_b", (0, 1024)),
                    (56, 64, "L1_attn_qkv_b", (1024, 2048)),
                    (64, 96, "L1_ffn_ln_w", None),
                    (96, 128, "L1_ffn_ln_b", None)):
                src = a[name] if sl is None else a[name][sl[0]:sl[1]]
                emit_col_load(p, gcols[:, lo:hi], src, cb)
            p.wait("tensor", "clv")
            cb.pop("ps_cl")
        p.flush("p0c")

        def snap(*names):
            return [(n, p.n(n)) for n in names]

        def fence(waits):
            for e in ENGS:
                p.emit(e, None, waits=tuple(waits))

        FE = []
        fmm1E0 = 0
        dwD = None
        for sq in range(n_seq):
            x0 = a["x_in"][sq * T:(sq + 1) * T, :]
            xo = a["x_out"][sq * T:(sq + 1) * T, :]

            with ExitStack() as stA:
                ab = alloc_attn_bufs(nc, stA, T)
                ab["psT"] = cb["psT"]
                ab["ident"] = cb["_ident_sb"]
                xF = cb["xfb"]
                dxt0 = p.n("dxt")
                gw = None
                if sq > 0:
                    gw = lambda j, b0=fmm1E0: [("fmm1", b0 + 16 * (j + 1))]
                emit_load_xF_fast(p, xbf_in[sq * T:(sq + 1) * T, :], xF, T,
                                  group_waits=gw)
                fence(FE)
                z_tm = sbg(stA, "A_ztm", [128, (T // 128) * 1024], BF16)
                wq_sb = sbg(stA, "A_wq", [128, 8 * 1024], BF16)
                wk_sb = sbg(stA, "A_wk", [128, 8 * 1024], BF16)
                wv_sb = sbg(stA, "A_wv", [128, 8 * 1024], BF16)
                load_weight_slab(p, wq_sb, a["l0qkv"][:, 0:1024], cast=False)
                load_weight_slab(p, wk_sb, a["l0qkv"][:, 1024:2048], cast=False)
                load_weight_slab(p, wv_sb, a["l0qkv"][:, 2048:3072], cast=False)
                cfg = {"T": T, "causal": True, "xF": xF, "z_tm": z_tm,
                       "wk_sb": wk_sb, "wq_sb": wq_sb, "wv_sb": wv_sb,
                       "tri": cb["_tri_sb"], "xf_tev": None, "xf_dxt0": dxt0,
                       "qkv_ready": [("dwt", p.n("dwt")),
                                     ("dsc0", p.n("dsc0")),
                                     ("dsc1", p.n("dsc1")),
                                     ("dp0s", p.n("dp0s"))],
                       "xf_ready": [],
                       "bq_cols": None, "bk_cols": None, "bv_row": None}
                emit_attention(p, cfg, ab)
                tevA = emit_resid_ln_out(p, x0, z_tm, xs[0], T, cb,
                                         trailing=False, xfb=cb["xfb"],
                                         gb=(gcols[:, 16:24],
                                             gcols[:, 24:32]))
                FA = snap("amm", "zmm", "qkev", "eev", "zfev", "vev", "zev",
                          "uadd")
                dwA = (p.n("dw0"), p.n("dw1"))

            with ExitStack() as stB:
                fence(FA)
                hF = cb["xfb"]
                z_tm = sbg(stB, "B_ztm", [128, (T // 128) * 1024], BF16)
                wq_sb = sbg(stB, "B_wq", [128, 8 * 1024], BF16)
                wk_sb = sbg(stB, "B_wk", [128, 8 * 1024], BF16)
                bv_row = sbg(stB, "B_bvr", [128, 1024], F32)
                bp_row = sbg(stB, "B_bpr", [128, 1024], F32)
                with ExitStack() as stR:
                    cb["ps_bc"] = stR.enter_context(
                        nc.psum_tensor(_u("psbc"), [128, 512], F32))
                    emit_row_bcast(p, bv_row, a["L0_ffn_bv"], 1024, cb)
                    emit_row_bcast(p, bp_row, a["L0_ffn_bp"], 1024, cb)
                    cb.pop("ps_bc")
                load_weight_slab(p, wq_sb, a["L0_ffn_Wq"], cast=True)
                load_weight_slab(p, wk_sb, a["L0_ffn_Wk"], cast=True)
                bcols = gcols[:, 0:16]
                ab = alloc_attn_bufs(nc, stB, T)
                ab["psT"] = cb["psT"]
                ab["ident"] = cb["_ident_sb"]
                cfg = {"T": T, "causal": True, "xF": hF, "z_tm": z_tm,
                       "wk_sb": wk_sb, "wq_sb": wq_sb, "tri": cb["_tri_sb"],
                       "xf_tev": tevA,
                       "qkv_ready": [("dwg", p.n("dwg")),
                                     ("bc", p.n("bc"))],
                       "xf_ready": [],
                       "bq_cols": bcols[:, 0:8], "bk_cols": bcols[:, 8:16],
                       "bv_row": bv_row}
                with ExitStack() as stV:
                    wv_sb = sbg(stV, "B_wv", [128, 8 * 1024], BF16)
                    load_weight_slab(p, wv_sb, a["L0_ffn_Wv"], cast=True,
                                     sem="dwv")
                    vcfg = dict(cfg)
                    vcfg["wv_sb"] = wv_sb
                    vcfg["qkv_ready"] = [("dwv", p.n("dwv")),
                                         ("bc", p.n("bc"))]
                    vcfg["bv_ready"] = [("bc", p.n("bc"))]
                    emit_attention_v(p, vcfg, ab)
                vmm_done = p.n("amm")
                wp_sb = sbg(stB, "B_wp", [128, 8 * 1024], BF16)
                load_weight_slab(p, wp_sb, a["L0_ffn_Wp"], cast=True,
                                 sem="dwp", waits=(("amm", vmm_done),))
                emit_attention_pairs(p, cfg, ab)
                zF = hF
                emit_zf_transpose(p, z_tm, zF, T, cb,
                                  guard=[("zev", p.n("zev")),
                                         ("amm", p.n("amm"))])
                tevB = emit_proj_resid(p, zF, wp_sb, bp_row, xs[0], xs[1], T,
                                       ab["ps_main"][:, 0:1024], cb,
                                       final_ln=False,
                                       w_ready=[("dwp", p.n("dwp")),
                                                ("eev", p.n("eev"))],
                                       x_waits=dw_chunk_waits(p, *dwA),
                                       trailing=False, xfb=cb["xfb"],
                                       gb=(gcols[:, 32:40], gcols[:, 40:48]))
                FB = snap("amm", "zmm", "qkev", "eev", "zfev", "vev", "zev",
                          "uadd", "pmm")
                dwB = (p.n("dw0"), p.n("dw1"))

            with ExitStack() as stC:
                fence(FB)
                h2F = cb["xfb"]
                mb = alloc_moe_bufs(nc, stC)
                mcfg = {"T": T, "h2F": h2F, "h2f_tev": tevB,
                        "aps": {"gate_W": a["L0_ffn_gate_W"],
                                "gate_b": a["L0_ffn_gate_b"],
                                "exp_W1": a["L0_ffn_exp_W1"],
                                "exp_b1": a["L0_ffn_exp_b1"],
                                "exp_W2": a["L0_ffn_exp_W2"],
                                "exp_b2": a["L0_ffn_exp_b2"]},
                        "x1_dram": xs[0], "x2_dram": xs[1],
                        "acc_dram": acc, "out_dram": xs[2],
                        "xfb": cb["xfb"],
                        "x1_waits": dw_chunk_waits(p, *dwA),
                        "x2_waits": dw_chunk_waits(p, *dwB),
                        "w_cache": (w1bf, w2bf),
                        "cache_mode": "write" if sq == 0 else "read",
                        "h2f_ready": [], "trailing": False}
                tevC = emit_moe(p, mcfg, mb, cb)
                FC = snap("mm1", "mm2", "gel", "yev", "gv", "dacc",
                          "wte") + [("uadd", mcfg["_uadd_ps"])]
                dwC = (p.n("dw0"), p.n("dw1"))

            with ExitStack() as stD:
                fence(FC)
                xF = cb["xfb"]
                z_tm = sbg(stD, "D_ztm", [128, (T // 128) * 1024], BF16)
                wq_sb = sbg(stD, "D_wq", [128, 8 * 1024], BF16)
                wk_sb = sbg(stD, "D_wk", [128, 8 * 1024], BF16)
                bv_row = sbg(stD, "D_bvr", [128, 1024], F32)
                bp_row = sbg(stD, "D_bpr", [128, 1024], F32)
                with ExitStack() as stR:
                    cb["ps_bc"] = stR.enter_context(
                        nc.psum_tensor(_u("psbc"), [128, 512], F32))
                    emit_row_bcast(p, bv_row, a["L1_attn_qkv_b"][2048:3072],
                                   1024, cb)
                    emit_row_bcast(p, bp_row, a["L1_attn_proj_b"], 1024, cb)
                    cb.pop("ps_bc")
                emit_trload_slab(p, wq_sb, a["qkvbf"][0:1024, :], 1024,
                                 range(8))
                emit_trload_slab(p, wk_sb, a["qkvbf"][1024:2048, :], 1024,
                                 range(8))
                bcols = gcols[:, 48:64]
                ab = alloc_attn_bufs(nc, stD, T)
                ab["psT"] = cb["psT"]
                ab["ident"] = cb["_ident_sb"]
                cfg = {"T": T, "causal": False, "xF": xF, "z_tm": z_tm,
                       "wk_sb": wk_sb, "wq_sb": wq_sb, "tri": cb["_tri_sb"],
                       "xf_tev": tevC,
                       "qkv_ready": [("dtw", p.n("dtw")),
                                     ("bc", p.n("bc"))],
                       "xf_ready": [],
                       "bq_cols": bcols[:, 0:8], "bk_cols": bcols[:, 8:16],
                       "bv_row": bv_row}
                with ExitStack() as stV:
                    wv_sb = sbg(stV, "D_wv", [128, 8 * 1024], BF16)
                    emit_trload_slab(p, wv_sb, a["qkvbf"][2048:3072, :], 1024,
                                     range(8), sem="dtv")
                    vcfg = dict(cfg)
                    vcfg["wv_sb"] = wv_sb
                    vcfg["qkv_ready"] = [("dtv", p.n("dtv")),
                                         ("bc", p.n("bc"))]
                    vcfg["bv_ready"] = [("bc", p.n("bc"))]
                    emit_attention_v(p, vcfg, ab)
                vmm_done = p.n("amm")
                wp_sb = sbg(stD, "D_wp", [128, 8 * 1024], BF16)
                p.emit("sync", None, waits=(("amm", vmm_done),))
                emit_trload_slab(p, wp_sb, a["projbf"], 1024, range(8),
                                 sem="dtp")
                emit_attention_pairs(p, cfg, ab)
                zF = xF
                emit_zf_transpose(p, z_tm, zF, T, cb,
                                  guard=[("zev", p.n("zev")),
                                         ("amm", p.n("amm"))])
                tevD = emit_proj_resid(p, zF, wp_sb, bp_row, xs[2], xs[3], T,
                                       ab["ps_main"][:, 0:1024], cb,
                                       final_ln=True,
                                       w_ready=[("dtp", p.n("dtp")),
                                                ("eev", p.n("eev"))],
                                       xfb=cb["xfb"],
                                       x_waits=dw_chunk_waits(p, *dwC),
                                       trailing=False)
                FD = snap("amm", "zmm", "qkev", "eev", "zfev", "vev", "zev",
                          "uadd", "pmm")
                tevD_full = p.n("tev")
                dwD = (p.n("dw0"), p.n("dw1"))

            with ExitStack() as stE:
                fence(FD)
                xF = cb["xfb"]
                wfc_sb = sbg(stE, "E_wfc", [128, 8 * 4096], BF16)
                wproj_sb = sbg(stE, "E_wpr", [128, 32 * 1024], BF16)

                eb = {"h_tm": [sbg(stE, "E_htm0", [128, 4096], BF16),
                               sbg(stE, "E_htm1", [128, 4096], BF16)],
                      "hF_t": sbg(stE, "E_hFt", [128, 32 * 128], BF16),
                      "ps_h": stE.enter_context(
                          nc.psum_tensor(_u("E_psh"), [128, 2048], F32)),
                      "ps_o": stE.enter_context(
                          nc.psum_tensor(_u("E_pso"), [128, 1024], F32))}
                emit_trload_slab(p, wfc_sb, a["wfcq"], 4096, range(8))
                for kc2 in range(32):
                    p.dma_tr(wproj_sb[:, kc2 * 1024:(kc2 + 1) * 1024],
                             a["wprq"][0:1024, kc2 * 128:(kc2 + 1) * 128],
                             "dtw")
                fmm1E0 = p.n("fmm1")
                ecfg = {"T": T, "xF": xF, "wfc_sb": wfc_sb, "xf_tev": tevD,
                        "wproj_sb": wproj_sb, "lnw_cols": gcols[:, 64:96],
                        "lnb_cols": gcols[:, 96:128], "x_dram": xs[3],
                        "out_dram": xo, "bufs": eb,
                        "w_ready": [("dtw", p.n("dtw"))],
                        "x_waits": dw_chunk_waits(p, *dwD),
                        "psT_ready": [("tev", tevD_full)],
                        "trailing": sq == n_seq - 1,
                        "xf_ready": []}
                emit_bitffn(p, ecfg, cb)
                FE = snap("fmm1", "fmm2", "fgg", "fgn", "tmm", "tev", "ln",
                          "uadd", "uout")
        p.flush("seqs")
    return nc


def kernel(**inputs):
    T, n_seq, n_cores = 2048, 4, 8
    nc = build_full(T=T, n_seq=n_seq)
    consts = host_consts()
    x = np.ascontiguousarray(inputs["L0_attn_x"])
    in_maps = []
    for c in range(n_cores):
        m = {"x_in": np.ascontiguousarray(
            x[c * n_seq:(c + 1) * n_seq].reshape(n_seq * T, D))}
        for n, _ in WNAMES:
            m[n] = inputs[n]
        m.update(consts)
        in_maps.append(m)
    res = run_bass_kernel_spmd(nc, in_maps, core_ids=list(range(n_cores)))
    out = np.empty((32, T, D), dtype=np.float32)
    for c in range(n_cores):
        out[c * n_seq:(c + 1) * n_seq] =             res.results[c]["x_out"].reshape(n_seq, T, D)
    return out
